# version 21
# baseline (speedup 1.0000x reference)
import numpy as np
import ml_dtypes
from contextlib import ExitStack

import concourse.bass as bass
import concourse.mybir as mybir
from concourse.bass_utils import run_bass_kernel_spmd

F32 = mybir.dt.float32
BF16 = mybir.dt.bfloat16
AF = mybir.ActivationFunctionType
ALU = mybir.AluOpType
AX = mybir.AxisListType
BF = ml_dtypes.bfloat16

NCORES = 8


class Cfg:
    def __init__(self, C, I, TOKS, HALF, LNW, SW, L, DW, GH, BIT, COMP=128.9):
        self.C = C
        self.I = I
        self.CT = C // 128
        self.IT = I // 128
        self.ITC = self.IT // NCORES
        self.RVK = 3 * C
        self.OT = self.RVK // 128
        self.TOKS = TOKS
        self.HALF = HALF
        self.NH = TOKS // HALF
        self.LNW = LNW
        self.NLW = HALF // LNW
        self.SW = SW
        self.NSW = HALF // SW
        self.L = L
        self.NI = SW // L
        self.OPW = 2 * SW
        self.DW = DW
        self.NDW = HALF // DW
        self.GH = GH
        self.NG = self.CT // GH
        self.TB = min(1024, HALF)
        self.BIT = BIT
        self.EB = min(1024, HALF)
        self.EBS = min(256, HALF)
        self.NEB = HALF // self.EB
        self.COMP = COMP
        self.RCHW = min(128, BIT * HALF // 16)
        assert SW % L == 0 and HALF % SW == 0 and HALF % LNW == 0
        assert HALF % DW == 0 and self.CT % GH == 0 and HALF % 128 == 0
        assert self.IT % (NCORES * 1) == 0 and self.IT % BIT == 0


CFG_REAL = Cfg(C=2048, I=8192, TOKS=2048, HALF=1024, LNW=256, SW=64, L=32,
               DW=512, GH=4, BIT=2)
CFG_MINI = Cfg(C=256, I=1024, TOKS=256, HALF=128, LNW=32, SW=16, L=8,
               DW=64, GH=1, BIT=2)


ENGS = ("pe", "dve", "act", "pool", "sp")
DMAQ = {"sp": "qsp", "act": "qact", "pool": "qpool"}


class Op:
    __slots__ = ("eng", "emit", "reads", "writes", "kind", "idx", "qidx",
                 "deps", "inc", "waits", "drain", "tag", "chan")

    def __init__(self, eng, emit, reads, writes, kind, tag="", chan=None):
        self.eng, self.emit, self.reads, self.writes = eng, emit, list(reads), list(writes)
        self.kind, self.tag, self.chan = kind, tag, chan
        self.idx = self.qidx = -1
        self.deps, self.waits = [], []
        self.inc = self.drain = False


class Sched:

    def __init__(self):
        self.ops = []
        self.last_w = {}
        self.readers = {}
        self.eng_ops = {e: [] for e in ENGS}
        self.chan_count = {}
        self.chan_last = {}
        self.cc_count = 0

    def add(self, eng, emit, reads=(), writes=(), kind="c", tag="", chan=None):
        op = Op(eng, emit, reads, writes, kind, tag, chan)
        if kind == "c":
            dom = eng
            op.idx = len(self.eng_ops[eng])
            me = (dom, op.idx)
        elif kind == "d":
            assert chan is not None
            dom = "q:" + chan
            op.qidx = self.chan_count.get(dom, 0)
            self.chan_count[dom] = op.qidx + 1
            op.idx = len(self.eng_ops[eng])
            me = (dom, op.qidx)
        else:
            dom = "cc%d" % self.cc_count
            op.qidx = self.cc_count
            self.cc_count += 1
            op.idx = len(self.eng_ops[eng])
            me = (dom, 0)
        self.eng_ops[eng].append(op)
        deps = set()
        for t in op.reads:
            w = self.last_w.get(t)
            if w is not None:
                deps.add(w)
        for t in op.writes:
            w = self.last_w.get(t)
            if w is not None:
                deps.add(w)
            for r in self.readers.get(t, ()):
                deps.add(r)
        if kind == "d":
            prev = self.chan_last.get(dom)
            if prev is not None:
                deps.add(prev)
            self.chan_last[dom] = me
        deps.discard(me)
        op.deps = sorted(deps)
        for t in op.reads:
            self.readers.setdefault(t, []).append(me)
        for t in op.writes:
            self.last_w[t] = me
            self.readers[t] = []
        self.ops.append((op, me))
        return op

    def plan(self):
        eff = {}
        issue = {e: {} for e in ENGS}
        run = {e: {} for e in ENGS}
        need_inc = set()
        for op, me in self.ops:
            e = op.eng
            clk = issue[e]
            if e == "pe":
                for k, v in run[e].items():
                    if clk.get(k, -1) < v:
                        clk[k] = v
                if clk.get(e, -1) < op.idx - 1:
                    clk[e] = op.idx - 1
            mywaits = []
            for dep in op.deps:
                ddom, didx = dep
                if clk.get(ddom, -1) >= didx:
                    continue
                if ddom == e:
                    op.drain = True
                    for k, v in run[e].items():
                        if clk.get(k, -1) < v:
                            clk[k] = v
                    if clk.get(e, -1) < op.idx - 1:
                        clk[e] = op.idx - 1
                    if clk.get(ddom, -1) < didx:
                        raise RuntimeError("drain cannot cover dep")
                else:
                    mywaits.append(dep)
                    need_inc.add(dep)
                    for k, v in eff.get(dep, {}).items():
                        if clk.get(k, -1) < v:
                            clk[k] = v
                    if clk.get(ddom, -1) < didx:
                        clk[ddom] = didx
            op.waits = mywaits
            myeff = dict(clk)
            myeff[me[0]] = max(myeff.get(me[0], -1), me[1])
            eff[me] = myeff
            run_eff = myeff if op.kind == "c" else clk
            re_ = run[e]
            for k, v in run_eff.items():
                if re_.get(k, -1) < v:
                    re_[k] = v
        for op, me in self.ops:
            if me in need_inc:
                op.inc = True
            if op.kind in ("d", "cc"):
                op.inc = True
        self.inc_prefix = {}
        for e in ENGS:
            cnt, pref = 0, []
            for op in self.eng_ops[e]:
                if op.kind == "c" and op.inc:
                    cnt += 1
                pref.append(cnt)
            self.inc_prefix[e] = pref

    def wait_value(self, dep):
        dom, idx = dep
        if dom in ENGS:
            return (dom, self.inc_prefix[dom][idx])
        if dom.startswith("q:"):
            return (dom, 16 * (idx + 1))
        return (dom, 1)

    def emit_engine(self, ename, eng, sems):
        waited = {}
        for op in self.eng_ops[ename]:
            if op.drain:
                eng.drain()
            for dep in op.waits:
                dom, val = self.wait_value(dep)
                if waited.get(dom, -1) >= val:
                    continue
                eng.wait_ge(sems[dom], val)
                waited[dom] = val
            ins = op.emit(eng)
            if op.kind == "c":
                if op.inc:
                    ins.then_inc(sems[ename], 1)
            elif op.kind == "d":
                ins.then_inc(sems["q:" + op.chan], 16)
            else:
                ins.then_inc(sems["cc%d" % op.qidx], 1)

    def stats(self):
        n_drain = sum(1 for o, _ in self.ops if o.drain)
        n_wait = sum(len(o.waits) for o, _ in self.ops)
        per_eng = {e: len(self.eng_ops[e]) for e in ENGS}
        return dict(total=len(self.ops), drains=n_drain, waits=n_wait, **per_eng)


def spline_grid(n_ctrl, n):
    t = np.linspace(0.0, n_ctrl - 1.0, n, dtype=np.float64)
    i = np.clip(np.floor(t).astype(np.int64), 0, n_ctrl - 2)
    f = (t - i).astype(np.float32)
    idx = np.stack([np.clip(i + k, 0, n_ctrl - 1) for k in (-1, 0, 1, 2)], 0)
    f2, f3 = f * f, f * f * f
    coeffs = np.stack([
        0.5 * (-f + 2 * f2 - f3),
        0.5 * (2 - 5 * f2 + 3 * f3),
        0.5 * (f + 4 * f2 - 3 * f3),
        0.5 * (-f2 + f3),
    ]).astype(np.float32)
    return idx, coeffs


def host_prep(cfg, inputs):
    g = cfg
    x = inputs["L0_attn_x"]
    W_rvk = inputs["L0_attn_W_rvk"].astype(np.float32)
    W_out = inputs["L0_attn_W_out"].astype(np.float32)
    td = inputs["L0_attn_time_decay"].astype(np.float32)
    tf = inputs["L0_attn_time_first"].astype(np.float32)

    n = g.I * g.C
    idx, coeffs = spline_grid(inputs["L0_ffn_gate_cp"].shape[0], n)

    def tile_gu(a):
        return a.reshape(g.IT, 128, g.CT, 128).transpose(0, 3, 2, 1)

    def tile_dn(a):
        return a.reshape(g.CT, 128, g.IT, 128).transpose(2, 3, 0, 1)

    gate_cp = inputs["L0_ffn_gate_cp"].astype(np.float32)
    up_cp = inputs["L0_ffn_up_cp"].astype(np.float32)
    down_cp = inputs["L0_ffn_down_cp"].astype(np.float32)
    n2 = g.C * g.I
    idx2, coeffs2 = idx, coeffs
    g_arr = [tile_gu(gate_cp[idx[k]]) for k in range(4)] + \
            [tile_gu(coeffs[k]) for k in range(4)]
    u_arr = [tile_gu(up_cp[idx[k]]) for k in range(4)]
    d_arr = [tile_dn(down_cp[idx2[k]]) for k in range(4)] + \
            [tile_dn(coeffs2[k]) for k in range(4)]

    def core_sl(arrs, r):
        per = np.stack(
            [a[g.ITC * r:g.ITC * (r + 1)].transpose(1, 0, 2, 3).reshape(128, -1)
             for a in arrs], axis=1)
        n_arr = per.shape[1]
        samp = per.shape[2]
        nch = samp // g.RCHW
        per = per.reshape(128, n_arr, nch, g.RCHW).transpose(0, 2, 1, 3)
        return np.ascontiguousarray(per.reshape(128, -1))

    wrvk_t = np.ascontiguousarray(
        W_rvk.reshape(g.OT, 128, g.CT, 128).transpose(3, 0, 2, 1).reshape(128, -1))
    wout_t = np.ascontiguousarray(
        W_out.reshape(g.CT, 128, g.CT, 128).transpose(3, 0, 2, 1).reshape(128, -1))
    td_r = np.ascontiguousarray(td.reshape(g.CT, 128).T)
    tf_r = np.ascontiguousarray(tf.reshape(g.CT, 128).T)
    ident = np.eye(128, dtype=np.float32)

    in_maps = []
    for core in range(NCORES):
        b, s = core // 2, core % 2
        xc = np.ascontiguousarray(
            x[b, s * g.TOKS:(s + 1) * g.TOKS].astype(np.float32))
        msk = np.zeros((128, NCORES * 3 * g.CT), np.float32)
        init_add = np.zeros((128, 3 * g.CT), np.float32)
        if s == 1:
            msk[:, (core - 1) * 3 * g.CT:core * 3 * g.CT] = 1.0
        else:
            init_add[:, 2 * g.CT:3 * g.CT] = -1e38
        in_maps.append({
            "x_in": xc,
            "grec": core_sl(g_arr, core),
            "urec": core_sl(u_arr, core),
            "drec": core_sl(d_arr, core),
            "wrvk_f": wrvk_t, "wout_f": wout_t,
            "td_r": td_r, "tf_r": tf_r, "ident": ident,
            "msk": msk, "init_add": init_add,
        })
    return in_maps


def build_nc(g):
    CT, IT, OT, L, NI, SW, LNW, HALF, DW, GH, NG, TB, BIT = (
        g.CT, g.IT, g.OT, g.L, g.NI, g.SW, g.LNW, g.HALF, g.DW, g.GH, g.NG,
        g.TB, g.BIT)
    CL = CT * L
    SAMP = g.ITC * CT * 128

    nc = bass.Bass("TRN2", target_bir_lowering=False)

    x_in = nc.dram_tensor("x_in", [g.TOKS, g.C], F32, kind="ExternalInput")
    grec = nc.dram_tensor("grec", [128, 8 * SAMP], F32, kind="ExternalInput")
    urec = nc.dram_tensor("urec", [128, 4 * SAMP], F32, kind="ExternalInput")
    drec = nc.dram_tensor("drec", [128, 8 * SAMP], F32, kind="ExternalInput")
    wrvk_f = nc.dram_tensor("wrvk_f", [128, OT * CT * 128], F32, kind="ExternalInput")
    wout_f = nc.dram_tensor("wout_f", [128, CT * CT * 128], F32, kind="ExternalInput")
    td_rD = nc.dram_tensor("td_r", [128, CT], F32, kind="ExternalInput")
    tf_rD = nc.dram_tensor("tf_r", [128, CT], F32, kind="ExternalInput")
    identD = nc.dram_tensor("ident", [128, 128], F32, kind="ExternalInput")
    mskD = nc.dram_tensor("msk", [128, NCORES * 3 * CT], F32, kind="ExternalInput")
    initD = nc.dram_tensor("init_add", [128, 3 * CT], F32, kind="ExternalInput")
    x_out = nc.dram_tensor("x_out", [g.TOKS, g.C], F32, kind="ExternalOutput")

    g_sh = nc.dram_tensor("g_sh", [g.ITC * 128, CT * 128], BF16)
    u_sh = nc.dram_tensor("u_sh", [g.ITC * 128, CT * 128], BF16)
    d_sh = nc.dram_tensor("d_sh", [g.ITC * 128, CT * 128], BF16)
    HITC = max(1, g.ITC // 2)
    NSPLIT = 2 if g.ITC >= 2 else 1
    g_full_a = nc.dram_tensor("g_full_a", [NCORES * HITC * 128, CT * 128], BF16, addr_space="Shared")
    g_full_b = nc.dram_tensor("g_full_b", [NCORES * HITC * 128, CT * 128], BF16, addr_space="Shared")
    u_full_a = nc.dram_tensor("u_full_a", [NCORES * HITC * 128, CT * 128], BF16, addr_space="Shared")
    u_full_b = nc.dram_tensor("u_full_b", [NCORES * HITC * 128, CT * 128], BF16, addr_space="Shared")
    d_full = nc.dram_tensor("d_full", [IT * 128, CT * 128], BF16, addr_space="Shared")
    wrvk_b = nc.dram_tensor("wrvk_b", [128, OT * CT * 128], BF16)
    wout_b = nc.dram_tensor("wout_b", [128, CT * CT * 128], BF16)
    inter_s = nc.dram_tensor("inter_s", [IT, 128, HALF], BF16)
    kpk = nc.dram_tensor("kpk", [g.NH, 128, g.NSW * NI * CL], F32)
    vpk = nc.dram_tensor("vpk", [g.NH, 128, g.NSW * NI * CL], F32)
    tpk = nc.dram_tensor("tpk", [g.NH, 128, g.NSW * NI * CL], BF16)
    xr_sp = nc.dram_tensor("xr_sp", [g.NH, 128, CT * HALF], F32)
    ysk = nc.dram_tensor("ysk", [128, g.NSW * CT * SW], BF16)
    st_in = nc.dram_tensor("st_in", [128, 3 * CT], F32)
    st_out = nc.dram_tensor("st_out", [NCORES * 128, 3 * CT], F32)

    sch = Sched()

    with ExitStack() as _ctx:
        EC = _ctx.enter_context
        XR = EC(nc.sbuf_tensor("XR", [128, CT * HALF], F32))
        XN = EC(nc.sbuf_tensor("XN", [128, CT * HALF], BF16))
        XSQ = EC(nc.sbuf_tensor("XSQ", [128, 2 * LNW], BF16))
        KP = EC(nc.sbuf_tensor("KP", [128, NI * CL], F32))
        VP = EC(nc.sbuf_tensor("VP", [128, NI * CL], F32))
        TP = EC(nc.sbuf_tensor("TP", [128, NI * CL], BF16))
        YS = EC(nc.sbuf_tensor("YS", [128, 2 * CT * g.OPW], BF16))
        IBST = EC(nc.sbuf_tensor("IBST", [128, 2 * BIT * HALF], BF16))
        IBO = EC(nc.sbuf_tensor("IBO", [128, 2 * HALF], BF16))
        SG = EC(nc.sbuf_tensor("SG", [128, TB], F32))
        WG = EC(nc.sbuf_tensor("WG", [128, 2 * CT * 128], BF16))
        WU = EC(nc.sbuf_tensor("WU", [128, 2 * CT * 128], BF16))
        WD = EC(nc.sbuf_tensor("WD", [128, 2 * BIT * GH * 128], BF16))
        WR = EC(nc.sbuf_tensor("WR", [128, 2 * CT * 128], BF16))
        KVST = EC(nc.sbuf_tensor("KVST", [128, 2 * g.EBS], F32))
        RST = EC(nc.sbuf_tensor("RST", [128, 2 * g.EBS], BF16))
        EZK = EC(nc.sbuf_tensor("EZK", [128, NI * 2 * CL], F32))
        PQH = EC(nc.sbuf_tensor("PQH", [128, 2 * 2 * CL], F32))
        E3W = EC(nc.sbuf_tensor("E3W", [128, NI * CL], F32))
        TMPS = EC(nc.sbuf_tensor("TMPS", [128, CL], F32))
        SSM = EC(nc.sbuf_tensor("SSM", [128, 28 * CT], F32))
        LNT = EC(nc.sbuf_tensor("LNT", [128, LNW], F32))
        CEWZ = EC(nc.sbuf_tensor("CEWZ", [128, 2 * CL], F32))
        CEU = EC(nc.sbuf_tensor("CEU", [128, CL], F32))
        ONESL = EC(nc.sbuf_tensor("ONESL", [128, L], F32))
        ONESBF = EC(nc.sbuf_tensor("ONESBF", [128, 1], BF16))
        ONER = EC(nc.sbuf_tensor("ONER", [1, 128], F32))
        IDENT = EC(nc.sbuf_tensor("IDENT", [128, 128], F32))
        ROWS = EC(nc.sbuf_tensor("ROWS", [1, 2 * LNW], F32))
        EPSC = EC(nc.sbuf_tensor("EPSC", [1, 1], F32))
        LHC = EC(nc.sbuf_tensor("LHC", [128, 1], F32))
        GTH = EC(nc.sbuf_tensor("GTH", [128, NCORES * 3 * CT], F32))
        MSKS = EC(nc.sbuf_tensor("MSKS", [128, NCORES * 3 * CT], F32))
        INITA = EC(nc.sbuf_tensor("INITA", [128, 3 * CT], F32))
        XST = EC(nc.sbuf_tensor("XST", [128, g.C], F32))
        pA = EC(nc.psum_tensor("pA", [128, 1024], F32))
        pB = EC(nc.psum_tensor("pB", [128, 1024], F32))
        pC = EC(nc.psum_tensor("pC", [128, 1024], F32))
        pD = EC(nc.psum_tensor("pD", [128, 1024], F32))
        s_pe = EC(nc.semaphore("s_pe"))
        s_dve = EC(nc.semaphore("s_dve"))
        s_act = EC(nc.semaphore("s_act"))
        s_pool = EC(nc.semaphore("s_pool"))
        s_sp = EC(nc.semaphore("s_sp"))
        s_cc0 = EC(nc.semaphore("s_cc0"))
        s_cc1 = EC(nc.semaphore("s_cc1"))
        s_cc2 = EC(nc.semaphore("s_cc2"))
        s_cc3 = EC(nc.semaphore("s_cc3"))
        s_cc4 = EC(nc.semaphore("s_cc4"))
        s_cc5 = EC(nc.semaphore("s_cc5"))
        s_cc6 = EC(nc.semaphore("s_cc6"))
        s_cc7 = EC(nc.semaphore("s_cc7"))
        block = EC(nc.Block())
        def ptoks(base, width):
            n = max(1, (width + 511) // 512)
            return [PB(base + i) for i in range(n)]

        sems = {"pe": s_pe, "dve": s_dve, "act": s_act, "pool": s_pool,
                "sp": s_sp,
                "cc0": s_cc0, "cc1": s_cc1, "cc2": s_cc2, "cc3": s_cc3,
                "cc4": s_cc4, "cc5": s_cc5, "cc6": s_cc6, "cc7": s_cc7}

        def A(eng, fn, reads=(), writes=(), tag=""):
            sch.add(eng, fn, reads, writes, "c", tag)

        def D(eng, chan, fn, reads=(), writes=(), tag=""):
            sch.add(eng, fn, reads, writes, "d", tag, chan=chan)

        def CC(fn, reads=(), writes=(), tag=""):
            sch.add("pool", fn, reads, writes, "cc", tag)

        PB = lambda i: "pb%d" % i

        def sl(i):
            return SSM[:, i * CT:(i + 1) * CT]


        D("sp", "ini", lambda e: e.dma_start(out=IDENT[:, :], in_=identD[:, :]),
          writes=["IDENT"])
        D("sp", "ini", lambda e: e.dma_start(out=MSKS[:, :], in_=mskD[:, :]),
          writes=["MSKS"])
        D("sp", "ini", lambda e: e.dma_start(out=INITA[:, :], in_=initD[:, :]),
          writes=["INITA"])
        D("sp", "ini", lambda e: e.dma_start(out=sl(20), in_=td_rD[:, :]), writes=["ssm20"])
        D("sp", "ini", lambda e: e.dma_start(out=sl(21), in_=tf_rD[:, :]), writes=["ssm21"])
        A("pool", lambda e: e.memset(sl(4), 0.0), writes=["st0"])
        A("pool", lambda e: e.memset(sl(5), 0.0), writes=["st0"])
        A("pool", lambda e: e.memset(sl(8), -1e38), writes=["st0"])
        A("pool", lambda e: e.memset(sl(6), 0.0), writes=["st1"])
        A("pool", lambda e: e.memset(sl(7), 0.0), writes=["st1"])
        A("pool", lambda e: e.memset(sl(9), -1e38), writes=["st1"])
        A("pool", lambda e: e.memset(ONESL[:, :], 1.0), writes=["ONESL"])
        A("pool", lambda e: e.memset(EPSC[:, :], 1e-5), writes=["EPSC"])
        A("pool", lambda e: e.memset(LHC[:, :], float(np.log(0.5))), writes=["LHC"])
        A("pool", lambda e: e.memset(ONESBF[:, :], 1.0), writes=["ONESBF"])
        A("pool", lambda e: e.memset(ONER[:, :], 1.0), writes=["ONER"])
        A("act", lambda e: e.activation(sl(18), sl(20), AF.Exp),
          reads=["ssm20"], writes=["ssm18"])
        A("dve", lambda e: e.tensor_scalar_mul(sl(0), sl(18), -1.0),
          reads=["ssm18"], writes=["ssm0"])
        A("act", lambda e: e.activation(sl(2), sl(0), AF.Exp),
          reads=["ssm0"], writes=["ssm2"])
        A("dve", lambda e: e.tensor_scalar_mul(sl(1), sl(0), float(L)),
          reads=["ssm0"], writes=["ssm1"])
        A("act", lambda e: e.activation(sl(3), sl(21), AF.Exp),
          reads=["ssm21"], writes=["ssm3"])
        for ct in range(CT):
            A("dve", lambda e, ct=ct: e.tensor_scalar_mul(
                CEWZ[:, ct * L:(ct + 1) * L], ONESL[:, :], sl(2)[:, ct:ct + 1]),
              reads=["ONESL", "ssm2"], writes=["CEWZ"])
            A("pool", lambda e, ct=ct: e.tensor_scalar_mul(
                CEU[:, ct * L:(ct + 1) * L], ONESL[:, :], sl(3)[:, ct:ct + 1]),
              reads=["ONESL", "ssm3"], writes=["CEU"])
        A("dve", lambda e: e.tensor_copy(CEWZ[:, CL:2 * CL], CEWZ[:, 0:CL]),
          reads=["CEWZ"], writes=["CEWZ"])
        A("dve", lambda e: e.memset(
            CEWZ[:, :].rearrange("p (c l) -> p c l", l=L)[:, :, 0:1], 0.0),
          writes=["CEWZ"])

        RCHW = g.RCHW

        def recon(rec_dram, n_own, coef_src, sh_dram, wtag):
            nch = SAMP // RCHW
            ibf = IBST[:, :].bitcast(F32)
            wgf = WG[:, :].bitcast(F32)
            bufs = [ibf[:, 0:8 * RCHW], ibf[:, 8 * RCHW:16 * RCHW],
                    wgf[:, 0:8 * RCHW], wgf[:, 8 * RCHW:16 * RCHW]]
            queues = ["act", "sp", "act", "sp"]
            for s in range(nch):
                bi = s % 4
                buf = bufs[bi]
                btok = "rs%d" % bi
                q = queues[bi]

                def ld(e, s=s, buf=buf, rec=rec_dram, n_own=n_own):
                    return e.dma_start(
                        out=buf[:, 0:n_own * RCHW],
                        in_=rec[:, s * n_own * RCHW:(s + 1) * n_own * RCHW])
                D(q, btok, ld, writes=[btok], tag=wtag + "ld")
                if coef_src is not None:
                    def ldc(e, s=s, buf=buf, cd=coef_src):
                        return e.dma_start(
                            out=buf[:, 4 * RCHW:8 * RCHW],
                            in_=cd[:, s * 8 * RCHW + 4 * RCHW:
                                   (s + 1) * 8 * RCHW])
                    D(q, btok, ldc, writes=[btok], tag=wtag + "ldc")
                perit = CT * 128
                it_loc = (s * RCHW) // perit
                coff = (s * RCHW) % perit
                wo = WD[:, coff:coff + RCHW]
                par = s % 3
                ceng = "pool" if par == 2 else "dve"
                wtok = "rw%d" % par
                t0 = PQH[:, par * RCHW:(par + 1) * RCHW]
                t1 = EZK[:, par * RCHW:(par + 1) * RCHW]
                rt0, rt1 = "rt0_%d" % par, "rt1_%d" % par
                a_ = lambda i, buf=buf: buf[:, i * RCHW:(i + 1) * RCHW]
                A(ceng, lambda e, a=a_, t0=t0: e.tensor_mul(t0, a(0), a(4)),
                  reads=[btok], writes=[rt0], tag=wtag)
                A(ceng, lambda e, a=a_, t1=t1: e.tensor_mul(t1, a(1), a(5)),
                  reads=[btok], writes=[rt1], tag=wtag)
                A(ceng, lambda e, t0=t0, t1=t1: e.tensor_add(t0, t0, t1),
                  reads=[rt0, rt1], writes=[rt0], tag=wtag)
                A(ceng, lambda e, a=a_, t1=t1: e.tensor_mul(t1, a(2), a(6)),
                  reads=[btok, rt0], writes=[rt1], tag=wtag)
                A(ceng, lambda e, t0=t0, t1=t1: e.tensor_add(t0, t0, t1),
                  reads=[rt0, rt1], writes=[rt0], tag=wtag)
                A(ceng, lambda e, a=a_, t1=t1: e.tensor_mul(t1, a(3), a(7)),
                  reads=[btok, rt0], writes=[rt1], tag=wtag)
                A(ceng, lambda e, wo=wo, t0=t0, t1=t1: e.tensor_add(wo, t0, t1),
                  reads=[rt0, rt1], writes=[wtok], tag=wtag)
                if coff + RCHW == perit:
                    def stw(e, it_loc=it_loc, sh=sh_dram):
                        return e.dma_start(
                            out=sh[it_loc * 128:(it_loc + 1) * 128, :],
                            in_=WD[:, 0:perit])
                    D("sp", "sst0", stw, reads=["rw0", "rw1", "rw2"],
                      writes=["%s_sh%d" % (wtag, it_loc)], tag=wtag + "st")
            return ["%s_sh%d" % (wtag, i) for i in range(g.ITC)]


        def xr_sl(ct, t0, n):
            return XR[:, ct * HALF + t0:ct * HALF + t0 + n]

        def xr_tok(ct, t0):
            return "XR_%d_%d" % (ct, t0 // DW)

        def xr_toks_range(ct, t0, n):
            return ["XR_%d_%d" % (ct, d) for d in
                    range(t0 // DW, (t0 + n - 1) // DW + 1)]

        def xn_sl(ct, t0, n):
            return XN[:, ct * HALF + t0:ct * HALF + t0 + n]

        def xn_tok(ct, w):
            return "XN_%d_%d" % (ct, w)

        def xn_toks_range(ct, t0, n):
            return [xn_tok(ct, w) for w in range(t0 // LNW, (t0 + n - 1) // LNW + 1)]

        def ln_pass(h, w, pid):
            t0 = w * LNW
            px = "ln%s_h%d_w%d" % (pid, h, w)
            rd = lambda i: ROWS[0:1, i * LNW:(i + 1) * LNW]
            for ct in range(CT):
                A("dve", lambda e, ct=ct: e.tensor_copy(
                    xn_sl(ct, t0, LNW), xr_sl(ct, t0, LNW)),
                  reads=[xr_tok(ct, t0)], writes=[xn_tok(ct, w)], tag=px)
            for ct in range(CT):
                xq = XSQ[:, (ct % 2) * LNW:(ct % 2 + 1) * LNW]
                xqt = "XSQ%d" % (ct % 2)
                A("pool", lambda e, ct=ct, xq=xq: e.tensor_mul(
                    xq, xn_sl(ct, t0, LNW), xn_sl(ct, t0, LNW)),
                  reads=[xn_tok(ct, w)], writes=[xqt], tag=px)
                A("pe", lambda e, ct=ct: e.matmul(
                    pB[0:1, 0:LNW], ONESBF[:, :], xn_sl(ct, t0, LNW),
                    start=(ct == 0), stop=(ct == CT - 1)),
                  reads=[xn_tok(ct, w), "ONESBF"], writes=[PB(2)], tag=px)
                A("pe", lambda e, ct=ct, xq=xq: e.matmul(
                    pB[0:1, 512:512 + LNW], ONESBF[:, :], xq,
                    start=(ct == 0), stop=(ct == CT - 1)),
                  reads=[xqt, "ONESBF"], writes=[PB(3)], tag=px)
            A("dve", lambda e: e.tensor_scalar_mul(rd(0), pB[0:1, 0:LNW], 1.0 / g.C),
              reads=[PB(2)], writes=["row0"], tag=px)
            A("dve", lambda e: e.tensor_scalar_mul(rd(1), pB[0:1, 512:512 + LNW], 1.0 / g.C),
              reads=[PB(3)], writes=["row1"], tag=px)
            A("dve", lambda e: e.tensor_mul(LNT[0:1, :], rd(0), rd(0)),
              reads=["row0"], writes=["LNT"], tag=px)
            A("dve", lambda e: e.tensor_sub(rd(1), rd(1), LNT[0:1, :]),
              reads=["row1", "LNT"], writes=["row1"], tag=px)
            A("act", lambda e: e.activation(rd(1), rd(1), AF.Ln, bias=EPSC[:, 0:1]),
              reads=["row1", "EPSC"], writes=["row1"], tag=px)
            A("act", lambda e: e.activation(rd(1), rd(1), AF.Exp, scale=-0.5),
              reads=["row1"], writes=["row1"], tag=px)
            A("dve", lambda e: e.tensor_mul(rd(0), rd(0), rd(1)),
              reads=["row0", "row1"], writes=["row0"], tag=px)
            A("pe", lambda e: e.matmul(pB[:, LNW:2 * LNW], ONER[:, :], rd(1),
                                       start=True, stop=True),
              reads=["row1", "ONER"], writes=[PB(2)], tag=px)
            A("pe", lambda e: e.matmul(pB[:, 512 + LNW:512 + 2 * LNW], ONER[:, :], rd(0),
                                       start=True, stop=True),
              reads=["row0", "ONER"], writes=[PB(3)], tag=px)
            for ct in range(CT):
                A("dve", lambda e, ct=ct: e.tensor_mul(
                    LNT[:, :], xr_sl(ct, t0, LNW), pB[:, LNW:2 * LNW]),
                  reads=[xr_tok(ct, t0), PB(2)], writes=["LNT"], tag=px)
                A("dve", lambda e, ct=ct: e.tensor_sub(
                    xn_sl(ct, t0, LNW), LNT[:, :], pB[:, 512 + LNW:512 + 2 * LNW]),
                  reads=["LNT", PB(3)], writes=[xn_tok(ct, w)], tag=px)

        GU_ORDER = [r * g.ITC + ph * HITC + l
                    for ph in range(2) for r in range(NCORES) for l in range(HITC)]

        def gu_src(it):
            r, l = it // g.ITC, it % g.ITC
            ph = l // HITC
            row = (r * HITC + (l % HITC)) * 128
            return ph, row

        def gu_sweep_gen(h, call):
            px = "gu%d_h%d" % (call, h)
            NTB = HALF // TB
            for ii, it in enumerate(GU_ORDER):
                yield
                gb = WG[:, (ii % 2) * CT * 128:(ii % 2 + 1) * CT * 128]
                ub = WU[:, (ii % 2) * CT * 128:(ii % 2 + 1) * CT * 128]
                gbt, ubt = "WG%d" % (ii % 2), "WU%d" % (ii % 2)
                ph, row = gu_src(it)
                gsrc = g_full_a if ph == 0 else g_full_b
                usrc = u_full_a if ph == 0 else u_full_b
                gtok = "g_fullA" if ph == 0 else "g_fullB"
                utok = "u_fullA" if ph == 0 else "u_fullB"
                D("sp", gbt, lambda e, row=row, gb=gb, gsrc=gsrc: e.dma_start(
                    out=gb, in_=gsrc[row:row + 128, :]),
                  reads=[gtok], writes=[gbt], tag=px + "ld")
                D("sp", ubt, lambda e, row=row, ub=ub, usrc=usrc: e.dma_start(
                    out=ub, in_=usrc[row:row + 128, :]),
                  reads=[utok], writes=[ubt], tag=px + "ld")
                gp = pA if ii % 2 == 0 else pC
                up = pB if ii % 2 == 0 else pD
                gpt = ptoks(0 if ii % 2 == 0 else 4, TB)
                upt = ptoks(2 if ii % 2 == 0 else 6, TB)
                NB = (TB + 511) // 512
                BW = min(512, TB)
                for tb in range(NTB):
                    for ht in range(CT):
                        xr_ = xn_toks_range(ht, tb * TB, TB)
                        for nb in range(NB):
                            A("pe", lambda e, ht=ht, tb=tb, nb=nb, gp=gp, gb=gb:
                              e.matmul(
                                gp[:, nb * BW:(nb + 1) * BW],
                                gb[:, ht * 128:(ht + 1) * 128],
                                XN[:, ht * HALF + tb * TB + nb * BW:
                                   ht * HALF + tb * TB + (nb + 1) * BW],
                                start=(ht == 0), stop=(ht == CT - 1)),
                              reads=[gbt] + xr_, writes=gpt, tag=px)
                    for ht in range(CT):
                        xr_ = xn_toks_range(ht, tb * TB, TB)
                        for nb in range(NB):
                            A("pe", lambda e, ht=ht, tb=tb, nb=nb, up=up, ub=ub:
                              e.matmul(
                                up[:, nb * BW:(nb + 1) * BW],
                                ub[:, ht * 128:(ht + 1) * 128],
                                XN[:, ht * HALF + tb * TB + nb * BW:
                                   ht * HALF + tb * TB + (nb + 1) * BW],
                                start=(ht == 0), stop=(ht == CT - 1)),
                              reads=[ubt] + xr_, writes=upt, tag=px)
                    ib = IBO[:, (ii % 2) * HALF + tb * TB:(ii % 2) * HALF + (tb + 1) * TB]
                    A("act", lambda e, gp=gp: e.activation(SG[:, 0:TB], gp[:, 0:TB], AF.Silu),
                      reads=gpt, writes=["SG"], tag=px)
                    A("dve", lambda e, up=up, ib=ib: e.tensor_mul(ib, SG[:, 0:TB], up[:, 0:TB]),
                      reads=["SG"] + upt, writes=["IBO%d" % (it % 2)], tag=px)
                D("pool", "int", lambda e, it=it: e.dma_start(
                    out=inter_s[it, :, :],
                    in_=IBO[:, (it % 2) * HALF:(it % 2 + 1) * HALF]),
                  reads=["IBO%d" % (it % 2)], writes=["int_%d" % it], tag=px + "st")

        def down_pass(h, call):
            px = "dn%d_h%d" % (call, h)
            for gr in range(NG):
                if GH == 4:
                    pmap4 = [pA, pB, pC, pD]
                    pbase = 0
                else:
                    pmap4 = ([pA, pB] if gr % 2 == 0 else [pC, pD]) * 2
                    pbase = 0 if gr % 2 == 0 else 4
                for itb in range(IT // BIT):
                    wdb = WD[:, (itb % 2) * BIT * GH * 128:
                             (itb % 2 + 1) * BIT * GH * 128]
                    wdt = "WD%d" % (itb % 2)
                    D("sp", wdt, lambda e, itb=itb, gr=gr, wdb=wdb: e.dma_start(
                        out=wdb.rearrange("p (i x) -> p i x", i=BIT),
                        in_=d_full[itb * BIT * 128:(itb + 1) * BIT * 128,
                                   gr * GH * 128:(gr + 1) * GH * 128]
                        .rearrange("(i p) x -> p i x", p=128)),
                      reads=["d_full"], writes=[wdt], tag=px + "ld")
                    ib4 = IBST[:, (itb % 2) * BIT * HALF:(itb % 2 + 1) * BIT * HALF]
                    ibt = "IBST%d" % (itb % 2)
                    D("sp", ibt, lambda e, itb=itb, ib4=ib4: e.dma_start(
                        out=ib4.rearrange("p (i x) -> p i x", i=BIT),
                        in_=inter_s[itb * BIT:(itb + 1) * BIT, :, :]
                        .rearrange("i p x -> p i x")),
                      reads=["int_%d" % (itb * BIT + ii) for ii in range(BIT)],
                      writes=[ibt], tag=px + "ldi")
                    for ii in range(BIT):
                        it = itb * BIT + ii
                        for hh in range(GH):
                            for wd in range(g.NDW):
                                dtile = pmap4[hh % 4 if GH == 4 else hh % 2][
                                    :, (wd % 2) * 512:(wd % 2) * 512 + DW]
                                A("pe", lambda e, ii=ii, hh=hh, it=it, wd=wd,
                                    wdb=wdb, ib4=ib4, dtile=dtile: e.matmul(
                                    dtile,
                                    wdb[:, (ii * GH + hh) * 128:
                                        (ii * GH + hh + 1) * 128],
                                    ib4[:, ii * HALF + wd * DW:
                                        ii * HALF + (wd + 1) * DW],
                                    start=(it == 0), stop=(it == IT - 1)),
                                  reads=[wdt, ibt],
                                  writes=[PB((2 * hh + (wd % 2)) if GH == 4 else
                                             (pbase + 2 * (hh % 2) + (wd % 2)))],
                                  tag=px)
                for hh in range(GH):
                    for wd in range(g.NDW):
                        ht = gr * GH + hh
                        dtile = pmap4[hh % 4 if GH == 4 else hh % 2][
                            :, (wd % 2) * 512:(wd % 2) * 512 + DW]
                        A("dve", lambda e, ht=ht, wd=wd, dtile=dtile:
                          e.scalar_tensor_tensor(
                            xr_sl(ht, wd * DW, DW), dtile, 0.5,
                            xr_sl(ht, wd * DW, DW), ALU.mult, ALU.add),
                          reads=[PB((2 * hh + (wd % 2)) if GH == 4 else
                                    (pbase + 2 * (hh % 2) + (wd % 2))),
                                 xr_tok(ht, wd * DW)],
                          writes=[xr_tok(ht, wd * DW)], tag=px + "ev")

        def rvk_gen(h):
            px = "rvk_h%d" % h
            EB, EBS = g.EB, g.EBS
            NSUB = EB // EBS
            ot_order = ([2 * CT + i for i in range(CT)] +
                        [CT + i for i in range(CT)] + list(range(CT)))
            for oi, ot in enumerate(ot_order):
                yield
                wb = WR[:, (oi % 2) * CT * 128:(oi % 2 + 1) * CT * 128]
                wbt = "WR%d" % (oi % 2)
                D("sp", wbt, lambda e, ot=ot, wb=wb: e.dma_start(
                    out=wb, in_=wrvk_b[:, ot * CT * 128:(ot + 1) * CT * 128]),
                  reads=["wrvkb%d" % ot], writes=[wbt], tag=px + "ld")
                pp = [pA, pB, pC, pD][oi % 4]
                ppt = ptoks([0, 2, 4, 6][oi % 4], EB)
                kind = ot // CT
                ct = ot % CT
                NBE = (EB + 511) // 512
                BWE = min(512, EB)
                for ht in range(CT):
                    xr_ = xn_toks_range(ht, 0, HALF)
                    for nb in range(NBE):
                        A("pe", lambda e, ht=ht, nb=nb, pp=pp, wb=wb: e.matmul(
                            pp[:, nb * BWE:(nb + 1) * BWE],
                            wb[:, ht * 128:(ht + 1) * 128],
                            XN[:, ht * HALF + nb * BWE:ht * HALF + (nb + 1) * BWE],
                            start=(ht == 0), stop=(ht == CT - 1)),
                          reads=[wbt] + xr_, writes=ppt, tag=px)
                for sub in range(NSUB):
                    bsl = sub % 2
                    if kind == 0:
                        stg = RST[:, bsl * EBS:(bsl + 1) * EBS]
                        stok = "RST%d" % bsl
                        A("act", lambda e, stg=stg, pp=pp, sub=sub: e.activation(
                            stg, pp[:, sub * EBS:(sub + 1) * EBS], AF.Tanh,
                            scale=0.5),
                          reads=ppt, writes=[stok], tag=px)
                        dstp = tpk
                    else:
                        stg = KVST[:, bsl * EBS:(bsl + 1) * EBS]
                        stok = "KVST%d" % bsl
                        A("act", lambda e, stg=stg, pp=pp, sub=sub: e.copy(
                            stg, pp[:, sub * EBS:(sub + 1) * EBS]),
                          reads=ppt, writes=[stok], tag=px)
                        dstp = vpk if kind == 1 else kpk
                    wpersb = EBS // SW

                    def stp(e, stg=stg, ot=ot, ct=ct, h=h, sub=sub, dstp=dstp,
                            wpersb=wpersb):
                        dst = dstp[h, :, :].rearrange(
                            "p (wj c l) -> p wj c l", c=CT, l=L)[
                            :, sub * wpersb * NI:(sub + 1) * wpersb * NI, ct, :]
                        return e.dma_start(
                            out=dst,
                            in_=stg.rearrange("p (wj l) -> p wj l", l=L))
                    D("pool", "pk", stp, reads=[stok],
                      writes=["%s_h%d_%d_%d" % (("tpk", "vpk", "kpk")[kind], h, ct, sub)],
                      tag=px + "st")

        def rvk_sweep(h, side=(), start=0):
            gen = rvk_gen(h)
            i = 0
            side = list(side)
            si = 0
            n_after = OT - start
            every = max(1, n_after // max(1, len(side))) if side else 1
            for _ in gen:
                i += 1
                if side and i > start and (i - start) % every == 0 and si < len(side):
                    side[si]()
                    si += 1
            while si < len(side):
                side[si]()
                si += 1

        def pack_toks(pfx, h):
            return ["%s_h%d_%d_%d" % (pfx, h, ct, sub)
                    for ct in range(CT) for sub in range(g.EB // g.EBS)]

        sidx = [0]

        def kv_load(h, w, full):
            NICL = NI * CL
            D("pool", "KP", lambda e, h=h, w=w: e.dma_start(
                out=KP[:, :], in_=kpk[h, :, w * NICL:(w + 1) * NICL]),
              reads=pack_toks("kpk", h), writes=["KP"], tag="kvld")
            D("pool", "VP", lambda e, h=h, w=w: e.dma_start(
                out=VP[:, :], in_=vpk[h, :, w * NICL:(w + 1) * NICL]),
              reads=pack_toks("vpk", h), writes=["VP"], tag="kvld")
            if full:
                D("pool", "TP", lambda e, h=h, w=w: e.dma_start(
                    out=TP[:, :], in_=tpk[h, :, w * NICL:(w + 1) * NICL]),
                  reads=pack_toks("tpk", h), writes=["TP"], tag="kvld")

        def scan_window(h, w, full, to_scratch=False):
            kv_load(h, w, full)
            px = "sc%s_h%d_w%d" % ("F" if full else "S", h, w)
            pars = [(sidx[0] + j) % 2 for j in range(NI)]
            ezv_ = lambda j: EZK[:, j * 2 * CL:j * 2 * CL + CL]
            ekk_ = lambda j: EZK[:, j * 2 * CL + CL:(j + 1) * 2 * CL]
            ph_ = lambda j: PQH[:, pars[j] * 2 * CL:pars[j] * 2 * CL + CL]
            qh_ = lambda j: PQH[:, pars[j] * 2 * CL + CL:(pars[j] + 1) * 2 * CL]
            e3_ = lambda j: E3W[:, j * CL:(j + 1) * CL]
            lam_sl = [13, 18]
            r3 = lambda ap: ap.rearrange("p (c l) -> p c l", l=L)
            bco = lambda ap: ap.rearrange("p (c o) -> p c o", o=1)
            for j in range(NI):
                kj = KP[:, j * CL:(j + 1) * CL]
                A("dve", lambda e, j=j, kj=kj: e.reduce_max(
                    sl(10 + j), r3(kj), axis=AX.X),
                  reads=["KP"], writes=["cmk%d" % j], tag=px)
            for j in range(NI):
                sbp, nbp = pars[j], 1 - pars[j]
                Mp, Mn = sl(8 + sbp), sl(8 + nbp)
                A("pool", lambda e, Mp=Mp: e.tensor_add(sl(12), Mp, sl(1)),
                  reads=["st%d" % sbp, "ssm1"], writes=["ssm12"], tag=px)
                A("dve", lambda e, Mn=Mn, j=j: e.tensor_max(Mn, sl(12), sl(10 + j)),
                  reads=["ssm12", "cmk%d" % j], writes=["st%d" % nbp], tag=px)
                A("dve", lambda e, Mp=Mp, Mn=Mn, j=j: e.tensor_sub(
                    sl(lam_sl[j]), Mp, Mn),
                  reads=["st%d" % sbp, "st%d" % nbp], writes=["lam%d" % j], tag=px)
                A("act", lambda e, j=j: e.activation(
                    sl(lam_sl[j]), sl(lam_sl[j]), AF.Exp),
                  reads=["lam%d" % j], writes=["lam%d" % j], tag=px)
            for j in range(NI):
                kj = KP[:, j * CL:(j + 1) * CL]
                Mn = sl(8 + (1 - pars[j]))
                A("dve", lambda e, j=j, kj=kj, Mn=Mn: e.tensor_tensor(
                    r3(ekk_(j)), r3(kj),
                    bco(Mn).broadcast_to([128, CT, L]), ALU.subtract),
                  reads=["KP", "st%d" % (1 - pars[j])], writes=["ekk%d" % j],
                  tag=px)
            eks = EZK[:, :].rearrange("p (j x) -> p j x", j=NI)[:, :, CL:2 * CL]
            A("act", lambda e: e.activation(eks, eks, AF.Exp),
              reads=["ekk%d" % j2 for j2 in range(NI)],
              writes=["ekk%d" % j2 for j2 in range(NI)], tag=px)
            ezs = EZK[:, :].rearrange("p (j x) -> p j x", j=NI)[:, :, 0:CL]
            vps = VP[:, :].rearrange("p (j x) -> p j x", j=NI)
            A("dve", lambda e: e.tensor_tensor(ezs, eks, vps, ALU.mult),
              reads=["ekk%d" % j2 for j2 in range(NI)] + ["VP"],
              writes=["ezv%d" % j2 for j2 in range(NI)], tag=px)
            if full:
                e3s = E3W[:, :].rearrange("p (j x) -> p j x", j=NI)
                ceub = CEU[:, :].rearrange("p (o x) -> p o x", o=1)                     .broadcast_to([128, NI, CL])
                A("pool", lambda e: e.tensor_tensor(e3s, eks, ceub, ALU.mult),
                  reads=["ekk%d" % j2 for j2 in range(NI)] + ["CEU"],
                  writes=["e3w"], tag=px)
            for j in range(NI):
                sbp, nbp = pars[j], 1 - pars[j]
                Pp, Qp = sl(4 + sbp * 2), sl(5 + sbp * 2)
                Pn, Qn = sl(4 + nbp * 2), sl(5 + nbp * 2)
                stok, ntok = "st%d" % sbp, "st%d" % nbp
                A("dve", lambda e, j=j, Pp=Pp: e.tensor_mul(sl(14), Pp, sl(lam_sl[j])),
                  reads=[stok, "lam%d" % j], writes=["ssm14"], tag=px)
                A("dve", lambda e, j=j, Qp=Qp: e.tensor_mul(sl(15), Qp, sl(lam_sl[j])),
                  reads=[stok, "lam%d" % j], writes=["ssm15"], tag=px)
                A("dve", lambda e: e.tensor_mul(sl(16), sl(14), sl(2)),
                  reads=["ssm14", "ssm2"], writes=["ssm16"], tag=px)
                A("dve", lambda e: e.tensor_mul(sl(17), sl(15), sl(2)),
                  reads=["ssm15", "ssm2"], writes=["ssm17"], tag=px)
                ez3, ek3 = r3(ezv_(j)), r3(ekk_(j))
                A("dve", lambda e, ez3=ez3: e.tensor_add(
                    ez3[:, :, 0:1], ez3[:, :, 0:1], bco(sl(16))),
                  reads=["ezv%d" % j, "ssm16"], writes=["ezv%d" % j], tag=px)
                A("dve", lambda e, ek3=ek3: e.tensor_add(
                    ek3[:, :, 0:1], ek3[:, :, 0:1], bco(sl(17))),
                  reads=["ekk%d" % j, "ssm17"], writes=["ekk%d" % j], tag=px)
                A("dve", lambda e, j=j: e.tensor_tensor_scan(
                    PQH[:, pars[j] * 2 * CL:(pars[j] + 1) * 2 * CL],
                    CEWZ[:, :], EZK[:, j * 2 * CL:(j + 1) * 2 * CL], 0.0,
                    ALU.mult, ALU.add),
                  reads=["CEWZ", "ezv%d" % j, "ekk%d" % j],
                  writes=["pqh%d" % pars[j]], tag=px)
                A("dve", lambda e, j=j, Pn=Pn: e.tensor_copy(
                    bco(Pn), r3(ph_(j))[:, :, L - 1:L]),
                  reads=["pqh%d" % pars[j]], writes=[ntok], tag=px)
                A("pool", lambda e, j=j, Qn=Qn: e.tensor_copy(
                    bco(Qn), r3(qh_(j))[:, :, L - 1:L]),
                  reads=["pqh%d" % pars[j]], writes=[ntok], tag=px)
                if full:
                    A("pool", lambda e, j=j: e.tensor_copy(sl(22 + j), sl(14)),
                      reads=["ssm14"], writes=["ip%d" % j], tag=px)
                    A("pool", lambda e, j=j: e.tensor_copy(sl(24 + j), sl(15)),
                      reads=["ssm15"], writes=["iq%d" % j], tag=px)
                sidx[0] += 1
            if not full:
                return
            b = (w // 2) % 2
            for j in range(NI):
                PSH, QSH = ezv_(j), ekk_(j)
                ps3, qs3 = r3(PSH), r3(QSH)
                ph3, qh3 = r3(ph_(j)), r3(qh_(j))
                A("dve", lambda e, ps3=ps3, ph3=ph3: e.tensor_copy(
                    ps3[:, :, 1:L], ph3[:, :, 0:L - 1]),
                  reads=["pqh%d" % pars[j], "ezv%d" % j], writes=["ezv%d" % j],
                  tag=px)
                A("dve", lambda e, ps3=ps3, j=j: e.tensor_copy(
                    ps3[:, :, 0:1], bco(sl(22 + j))),
                  reads=["ip%d" % j], writes=["ezv%d" % j], tag=px)
                A("pool", lambda e, qs3=qs3, qh3=qh3: e.tensor_copy(
                    qs3[:, :, 1:L], qh3[:, :, 0:L - 1]),
                  reads=["pqh%d" % pars[j], "ekk%d" % j], writes=["ekk%d" % j],
                  tag=px)
                A("pool", lambda e, qs3=qs3, j=j: e.tensor_copy(
                    qs3[:, :, 0:1], bco(sl(24 + j))),
                  reads=["iq%d" % j], writes=["ekk%d" % j], tag=px)
                vj = VP[:, j * CL:(j + 1) * CL]
                A("dve", lambda e, j=j, vj=vj: e.tensor_mul(TMPS[:, :], e3_(j), vj),
                  reads=["e3w", "VP"], writes=["tms"], tag=px)
                A("dve", lambda e, PSH=PSH: e.tensor_add(PSH, PSH, TMPS[:, :]),
                  reads=["ezv%d" % j, "tms"], writes=["ezv%d" % j], tag=px)
                A("pool", lambda e, QSH=QSH, j=j: e.tensor_add(QSH, QSH, e3_(j)),
                  reads=["ekk%d" % j, "e3w"], writes=["ekk%d" % j], tag=px)
                A("act", lambda e, QSH=QSH: e.activation(QSH, QSH, AF.Ln),
                  reads=["ekk%d" % j], writes=["ekk%d" % j], tag=px)
                A("act", lambda e, QSH=QSH: e.activation(
                    QSH, QSH, AF.Exp, scale=-1.0, bias=LHC[:, 0:1]),
                  reads=["ekk%d" % j, "LHC"], writes=["ekk%d" % j], tag=px)
                A("dve", lambda e, PSH=PSH, QSH=QSH: e.tensor_mul(PSH, PSH, QSH),
                  reads=["ezv%d" % j, "ekk%d" % j], writes=["ezv%d" % j], tag=px)
                tj = TP[:, j * CL:(j + 1) * CL]
                A("pool", lambda e, PSH=PSH, tj=tj: e.tensor_mul(TMPS[:, :], PSH, tj),
                  reads=["ezv%d" % j, "TP"], writes=["tms"], tag=px)
                ysb = YS[:, b * CT * g.OPW:(b + 1) * CT * g.OPW]
                ys3 = ysb.rearrange("p (c s) -> p c s", s=g.OPW)[
                    :, :, (w % 2) * SW + j * L:(w % 2) * SW + (j + 1) * L]
                A("dve", lambda e, ys3=ys3, PSH=PSH: e.tensor_add(
                    ys3, r3(PSH), r3(TMPS[:, :])),
                  reads=["ezv%d" % j, "tms"],
                  writes=["YS%d_%d_%d" % (b, w % 2, j)], tag=px)
            if to_scratch:
                D("pool", "ysst", lambda e, w=w, b=b: e.dma_start(
                    out=ysk[:, :].rearrange(
                        "p (wp c s) -> p wp c s", c=CT, s=g.OPW)[
                        :, w // 2, :, (w % 2) * SW:(w % 2 + 1) * SW],
                    in_=YS[:, b * CT * g.OPW:(b + 1) * CT * g.OPW].rearrange(
                        "p (c o) -> p c o", o=g.OPW)[:, :, (w % 2) * SW:
                                                     (w % 2 + 1) * SW]),
                  reads=["YS%d_%d_%d" % (b, w % 2, j) for j in range(NI)],
                  writes=["ysk_%d" % w], tag="ysst")

        def weave(main_gen, side_steps, every):
            i = 0
            side = list(side_steps)
            si = 0
            for _ in main_gen:
                i += 1
                if si < len(side) and i % every == 0:
                    side[si]()
                    si += 1
            while si < len(side):
                side[si]()
                si += 1

        def gu_sweep(h, call, side=()):
            weave(gu_sweep_gen(h, call), side, max(1, IT // max(1, len(side) + 1)))

        def outproj(h, wp, from_scratch=False):
            px = "op_h%d_p%d" % (h, wp)
            t0 = wp * g.OPW
            b = wp % 2
            ysb = YS[:, b * CT * g.OPW:(b + 1) * CT * g.OPW]
            ystoks = ["YS%d_%d_%d" % (b, k, jj)
                      for k in range(2) for jj in range(NI)]
            if from_scratch:
                D("pool", "ysld", lambda e, wp=wp, ysb=ysb: e.dma_start(
                    out=ysb, in_=ysk[:, wp * CT * g.OPW:(wp + 1) * CT * g.OPW]),
                  reads=["ysk_%d" % (2 * wp), "ysk_%d" % (2 * wp + 1)],
                  writes=ystoks, tag=px + "ysld")
            for ot in range(CT):
                wb = WR[:, (ot % 2) * CT * 128:(ot % 2 + 1) * CT * 128]
                wbt = "WR%d" % (ot % 2)
                D("sp", wbt, lambda e, ot=ot, wb=wb: e.dma_start(
                    out=wb, in_=wout_b[:, ot * CT * 128:(ot + 1) * CT * 128]),
                  reads=["woutb%d" % ot], writes=[wbt], tag=px + "ld")
                pp = pC if ot % 2 == 0 else pD
                ppt = PB(4 if ot % 2 == 0 else 6)
                for ct in range(CT):
                    A("pe", lambda e, ct=ct, pp=pp, wb=wb, ysb=ysb: e.matmul(
                        pp[:, 0:g.OPW], wb[:, ct * 128:(ct + 1) * 128],
                        ysb[:, ct * g.OPW:(ct + 1) * g.OPW],
                        start=(ct == 0), stop=(ct == CT - 1)),
                      reads=[wbt] + ystoks, writes=[ppt], tag=px)
                A("dve", lambda e, ot=ot, pp=pp, t0=t0: e.tensor_add(
                    xr_sl(ot, t0, g.OPW), xr_sl(ot, t0, g.OPW), pp[:, 0:g.OPW]),
                  reads=[ppt, xr_tok(ot, t0)], writes=[xr_tok(ot, t0)],
                  tag=px + "ev")

        def x_load(h):
            px = "xin_h%d" % h
            for tt in range(HALF // 128):
                D("sp", "XST", lambda e, tt=tt, h=h: e.dma_start(
                    out=XST[:, :],
                    in_=x_in[h * HALF + tt * 128:h * HALF + (tt + 1) * 128, :]),
                  writes=["XST"], tag=px)
                for ct in range(CT):
                    A("pe", lambda e, ct=ct: e.transpose(
                        pA[:, 0:128], XST[:, ct * 128:(ct + 1) * 128], IDENT[:, :]),
                      reads=["XST", "IDENT"], writes=[PB(0)], tag=px)
                    A("act", lambda e, ct=ct, tt=tt: e.copy(
                        xr_sl(ct, tt * 128, 128), pA[:, 0:128]),
                      reads=[PB(0)], writes=xr_toks_range(ct, tt * 128, 128),
                      tag=px)

        def x_store(h):
            px = "xout_h%d" % h
            outs = []
            for tt in range(HALF // 128):
                for ct in range(CT):
                    A("pe", lambda e, ct=ct, tt=tt: e.transpose(
                        pA[:, 128:256], xr_sl(ct, tt * 128, 128), IDENT[:, :]),
                      reads=xr_toks_range(ct, tt * 128, 128) + ["IDENT"],
                      writes=[PB(0)], tag=px)
                    A("act", lambda e, ct=ct: e.copy(
                        XST[:, ct * 128:(ct + 1) * 128], pA[:, 128:256]),
                      reads=[PB(0)], writes=["XST"], tag=px)
                tok = "xout_%d_%d" % (h, tt)
                D("pool", "xo", lambda e, tt=tt, h=h: e.dma_start(
                    out=x_out[h * HALF + tt * 128:h * HALF + (tt + 1) * 128, :],
                    in_=XST[:, :]),
                  reads=["XST"], writes=[tok], tag=px)
                outs.append(tok)
            return outs

        x_load(0)
        for w in range(g.NLW):
            ln_pass(0, w, "1")
        gsh_toks = recon(grec, 8, None, g_sh, "g")
        ush_toks = recon(urec, 4, grec, u_sh, "u")
        dsh_toks = recon(drec, 8, None, d_sh, "d")

        HB = HITC * 128
        CC(lambda e: e.collective_compute(
            "AllGather", ALU.bypass, replica_groups=[list(range(NCORES))],
            ins=[g_sh[0:HB, :]], outs=[g_full_a.ap().opt()]),
           reads=gsh_toks[:HITC], writes=["g_fullA"], tag="AGg1")
        CC(lambda e: e.collective_compute(
            "AllGather", ALU.bypass, replica_groups=[list(range(NCORES))],
            ins=[u_sh[0:HB, :]], outs=[u_full_a.ap().opt()]),
           reads=ush_toks[:HITC], writes=["u_fullA"], tag="AGu1")
        if NSPLIT == 2:
            CC(lambda e: e.collective_compute(
                "AllGather", ALU.bypass, replica_groups=[list(range(NCORES))],
                ins=[g_sh[HB:2 * HB, :]], outs=[g_full_b.ap().opt()]),
               reads=gsh_toks[HITC:], writes=["g_fullB"], tag="AGg2")
            CC(lambda e: e.collective_compute(
                "AllGather", ALU.bypass, replica_groups=[list(range(NCORES))],
                ins=[u_sh[HB:2 * HB, :]], outs=[u_full_b.ap().opt()]),
               reads=ush_toks[HITC:], writes=["u_fullB"], tag="AGu2")
        CC(lambda e: e.collective_compute(
            "AllGather", ALU.bypass, replica_groups=[list(range(NCORES))],
            ins=[d_sh.ap().opt()], outs=[d_full.ap().opt()]),
           reads=dsh_toks, writes=["d_full"], tag="AGd")

        def cast_w(src, dst, nblk, tag):
            for b_ in range(nblk):
                buf = WR[:, (b_ % 2) * CT * 128:(b_ % 2 + 1) * CT * 128]
                btok = "WR%d" % (b_ % 2)
                D("pool", btok + "c", lambda e, b_=b_, buf=buf, src=src: e.dma_start(
                    out=buf, in_=src[:, b_ * CT * 128:(b_ + 1) * CT * 128]),
                  writes=[btok], tag=tag)
                D("act", "cst", lambda e, b_=b_, buf=buf, dst=dst: e.dma_start(
                    out=dst[:, b_ * CT * 128:(b_ + 1) * CT * 128], in_=buf),
                  reads=[btok], writes=["%s%d" % (tag, b_)], tag=tag)
        cast_w(wrvk_f, wrvk_b, OT, "wrvkb")
        cast_w(wout_f, wout_b, CT, "woutb")
        gu_sweep(0, 1)
        down_pass(0, 1)
        for w in range(g.NLW):
            ln_pass(0, w, "2")
        rvk_sweep(0)
        D("pool", "spl", lambda e: e.dma_start(out=xr_sp[0, :, :], in_=XR[:, :]),
          reads=["XR_%d_%d" % (ct, d) for ct in range(CT) for d in range(g.NDW)],
          writes=["xr_sp0"], tag="spill0")
        x_load(1)
        for w in range(g.NLW):
            ln_pass(1, w, "1")
        gu_sweep(1, 1, side=[
            (lambda w=w: scan_window(0, w, False)) for w in range(g.NSW)])
        down_pass(1, 1)
        for w in range(g.NLW):
            ln_pass(1, w, "2")
        rvk_sweep(1, side=[
            (lambda w=w: scan_window(1, w, False)) for w in range(g.NSW)],
            start=2 * CT + 2)
        D("pool", "spl", lambda e: e.dma_start(out=xr_sp[1, :, :], in_=XR[:, :]),
          reads=["XR_%d_%d" % (ct, d) for ct in range(CT) for d in range(g.NDW)],
          writes=["xr_sp1"], tag="spill1")

        sb = sidx[0] % 2
        A("dve", lambda e, sb=sb: e.tensor_copy(GTH[:, 0:CT], sl(4 + sb * 2)),
          reads=["st%d" % sb], writes=["stpk"], tag="B")
        A("dve", lambda e, sb=sb: e.tensor_copy(GTH[:, CT:2 * CT], sl(5 + sb * 2)),
          reads=["st%d" % sb], writes=["stpk"], tag="B")
        A("dve", lambda e, sb=sb: e.tensor_copy(GTH[:, 2 * CT:3 * CT], sl(8 + sb)),
          reads=["st%d" % sb], writes=["stpk"], tag="B")
        D("pool", "stio", lambda e: e.dma_start(out=st_in[:, :], in_=GTH[:, 0:3 * CT]),
          reads=["stpk"], writes=["st_in"], tag="B")
        CC(lambda e: e.collective_compute(
            "AllGather", ALU.bypass, replica_groups=[list(range(NCORES))],
            ins=[st_in.ap().opt()], outs=[st_out.ap().opt()]),
           reads=["st_in"], writes=["st_out"], tag="AGst")
        D("sp", "gth", lambda e: e.dma_start(
            out=GTH[:, :].rearrange("p (r f) -> p r f", r=NCORES),
            in_=st_out[:, :].rearrange("(r p) f -> p r f", p=128)),
          reads=["st_out"], writes=["stpk"], tag="B")
        A("dve", lambda e: e.tensor_mul(GTH[:, :], GTH[:, :], MSKS[:, :]),
          reads=["stpk", "MSKS"], writes=["stpk"], tag="B")
        A("dve", lambda e: e.tensor_reduce(
            LNT[:, 0:3 * CT],
            GTH[:, :].rearrange("p (r f) -> p f r", f=3 * CT), AX.X, ALU.add),
          reads=["stpk"], writes=["LNT"], tag="B")
        A("dve", lambda e: e.tensor_add(LNT[:, 0:3 * CT], LNT[:, 0:3 * CT],
                                        INITA[:, :]),
          reads=["LNT", "INITA"], writes=["LNT"], tag="B")
        sidx[0] = 0
        A("dve", lambda e: e.tensor_copy(sl(4), LNT[:, 0:CT]),
          reads=["LNT"], writes=["st0"], tag="B")
        A("dve", lambda e: e.tensor_copy(sl(5), LNT[:, CT:2 * CT]),
          reads=["LNT"], writes=["st0"], tag="B")
        A("dve", lambda e: e.tensor_copy(sl(8), LNT[:, 2 * CT:3 * CT]),
          reads=["LNT"], writes=["st0"], tag="B")

        out_toks = []
        D("sp", "rld", lambda e: e.dma_start(out=XR[:, :], in_=xr_sp[0, :, :]),
          reads=["xr_sp0"],
          writes=["XR_%d_%d" % (ct, d) for ct in range(CT) for d in range(g.NDW)],
          tag="rld0")
        for w in range(g.NSW):
            scan_window(0, w, True)
            if w % 2 == 1:
                outproj(0, w // 2)
        for w in range(g.NLW):
            ln_pass(0, w, "3")
        gu_sweep(0, 2, side=[
            (lambda w=w: scan_window(1, w, True, to_scratch=True))
            for w in range(g.NSW)])
        down_pass(0, 2)
        out_toks += x_store(0)
        D("sp", "rld", lambda e: e.dma_start(out=XR[:, :], in_=xr_sp[1, :, :]),
          reads=["xr_sp1"],
          writes=["XR_%d_%d" % (ct, d) for ct in range(CT) for d in range(g.NDW)],
          tag="rld1")
        for wp in range(g.NSW // 2):
            outproj(1, wp, from_scratch=True)
        for w in range(g.NLW):
            ln_pass(1, w, "3")
        gu_sweep(1, 2)
        down_pass(1, 2)
        out_toks += x_store(1)

        A("pool", lambda e: e.engine_nop(), reads=out_toks, tag="fence")

        for ch in sorted(sch.chan_count):
            sems[ch] = EC(nc.semaphore("sem_" + ch.replace(":", "_")))
        sch.plan()

        @block.tensor
        def _(eng):
            sch.emit_engine("pe", eng, sems)

        @block.vector
        def _(eng):
            sch.emit_engine("dve", eng, sems)

        @block.scalar
        def _(eng):
            sch.emit_engine("act", eng, sems)

        @block.gpsimd
        def _(eng):
            sch.emit_engine("pool", eng, sems)

        @block.sync
        def _(eng):
            sch.emit_engine("sp", eng, sems)

    return nc, sch


def kernel(**inputs):
    g = CFG_REAL
    in_maps = host_prep(g, inputs)
    nc, _ = build_nc(g)
    res = run_bass_kernel_spmd(nc, in_maps, core_ids=list(range(NCORES)))
    out = np.empty((4, 2 * g.TOKS, g.C), np.float32)
    for core in range(NCORES):
        b, s = core // 2, core % 2
        out[b, s * g.TOKS:(s + 1) * g.TOKS] = res.results[core]["x_out"]
    return out




# version 23
# speedup vs baseline: 1.0087x; 1.0087x over previous
import numpy as np
import ml_dtypes
from contextlib import ExitStack

import concourse.bass as bass
import concourse.mybir as mybir
from concourse.bass_utils import run_bass_kernel_spmd

F32 = mybir.dt.float32
BF16 = mybir.dt.bfloat16
AF = mybir.ActivationFunctionType
ALU = mybir.AluOpType
AX = mybir.AxisListType
BF = ml_dtypes.bfloat16

NCORES = 8


class Cfg:
    def __init__(self, C, I, TOKS, HALF, LNW, SW, L, DW, GH, BIT, COMP=128.9):
        self.C = C
        self.I = I
        self.CT = C // 128
        self.IT = I // 128
        self.ITC = self.IT // NCORES
        self.RVK = 3 * C
        self.OT = self.RVK // 128
        self.TOKS = TOKS
        self.HALF = HALF
        self.NH = TOKS // HALF
        self.LNW = LNW
        self.NLW = HALF // LNW
        self.SW = SW
        self.NSW = HALF // SW
        self.L = L
        self.NI = SW // L
        self.OPW = 2 * SW
        self.DW = DW
        self.NDW = HALF // DW
        self.GH = GH
        self.NG = self.CT // GH
        self.TB = min(1024, HALF)
        self.BIT = BIT
        self.EB = min(1024, HALF)
        self.EBS = min(256, HALF)
        self.NEB = HALF // self.EB
        self.COMP = COMP
        self.RCHW = min(128, BIT * HALF // 16)
        assert SW % L == 0 and HALF % SW == 0 and HALF % LNW == 0
        assert HALF % DW == 0 and self.CT % GH == 0 and HALF % 128 == 0
        assert self.IT % (NCORES * 1) == 0 and self.IT % BIT == 0


CFG_REAL = Cfg(C=2048, I=8192, TOKS=2048, HALF=1024, LNW=256, SW=64, L=32,
               DW=512, GH=4, BIT=2)
CFG_MINI = Cfg(C=256, I=1024, TOKS=256, HALF=128, LNW=32, SW=16, L=8,
               DW=64, GH=1, BIT=2)


ENGS = ("pe", "dve", "act", "pool", "sp")
DMAQ = {"sp": "qsp", "act": "qact", "pool": "qpool"}


class Op:
    __slots__ = ("eng", "emit", "reads", "writes", "kind", "idx", "qidx",
                 "deps", "inc", "waits", "drain", "tag", "chan")

    def __init__(self, eng, emit, reads, writes, kind, tag="", chan=None):
        self.eng, self.emit, self.reads, self.writes = eng, emit, list(reads), list(writes)
        self.kind, self.tag, self.chan = kind, tag, chan
        self.idx = self.qidx = -1
        self.deps, self.waits = [], []
        self.inc = self.drain = False


class Sched:

    def __init__(self):
        self.ops = []
        self.last_w = {}
        self.readers = {}
        self.eng_ops = {e: [] for e in ENGS}
        self.chan_count = {}
        self.chan_last = {}
        self.cc_count = 0

    def add(self, eng, emit, reads=(), writes=(), kind="c", tag="", chan=None):
        op = Op(eng, emit, reads, writes, kind, tag, chan)
        if kind == "c":
            dom = eng
            op.idx = len(self.eng_ops[eng])
            me = (dom, op.idx)
        elif kind == "d":
            assert chan is not None
            dom = "q:" + chan
            op.qidx = self.chan_count.get(dom, 0)
            self.chan_count[dom] = op.qidx + 1
            op.idx = len(self.eng_ops[eng])
            me = (dom, op.qidx)
        else:
            dom = "cc%d" % self.cc_count
            op.qidx = self.cc_count
            self.cc_count += 1
            op.idx = len(self.eng_ops[eng])
            me = (dom, 0)
        self.eng_ops[eng].append(op)
        deps = set()
        for t in op.reads:
            w = self.last_w.get(t)
            if w is not None:
                deps.add(w)
        for t in op.writes:
            w = self.last_w.get(t)
            if w is not None:
                deps.add(w)
            for r in self.readers.get(t, ()):
                deps.add(r)
        if kind == "d":
            prev = self.chan_last.get(dom)
            if prev is not None:
                deps.add(prev)
            self.chan_last[dom] = me
        deps.discard(me)
        op.deps = sorted(deps)
        for t in op.reads:
            self.readers.setdefault(t, []).append(me)
        for t in op.writes:
            self.last_w[t] = me
            self.readers[t] = []
        self.ops.append((op, me))
        return op

    def plan(self):
        eff = {}
        issue = {e: {} for e in ENGS}
        run = {e: {} for e in ENGS}
        need_inc = set()
        for op, me in self.ops:
            e = op.eng
            clk = issue[e]
            if e == "pe":
                for k, v in run[e].items():
                    if clk.get(k, -1) < v:
                        clk[k] = v
                if clk.get(e, -1) < op.idx - 1:
                    clk[e] = op.idx - 1
            mywaits = []
            for dep in op.deps:
                ddom, didx = dep
                if clk.get(ddom, -1) >= didx:
                    continue
                if ddom == e:
                    op.drain = True
                    for k, v in run[e].items():
                        if clk.get(k, -1) < v:
                            clk[k] = v
                    if clk.get(e, -1) < op.idx - 1:
                        clk[e] = op.idx - 1
                    if clk.get(ddom, -1) < didx:
                        raise RuntimeError("drain cannot cover dep")
                else:
                    mywaits.append(dep)
                    need_inc.add(dep)
                    for k, v in eff.get(dep, {}).items():
                        if clk.get(k, -1) < v:
                            clk[k] = v
                    if clk.get(ddom, -1) < didx:
                        clk[ddom] = didx
            op.waits = mywaits
            myeff = dict(clk)
            myeff[me[0]] = max(myeff.get(me[0], -1), me[1])
            eff[me] = myeff
            run_eff = myeff if op.kind == "c" else clk
            re_ = run[e]
            for k, v in run_eff.items():
                if re_.get(k, -1) < v:
                    re_[k] = v
        for op, me in self.ops:
            if me in need_inc:
                op.inc = True
            if op.kind in ("d", "cc"):
                op.inc = True
        self.inc_prefix = {}
        for e in ENGS:
            cnt, pref = 0, []
            for op in self.eng_ops[e]:
                if op.kind == "c" and op.inc:
                    cnt += 1
                pref.append(cnt)
            self.inc_prefix[e] = pref

    def wait_value(self, dep):
        dom, idx = dep
        if dom in ENGS:
            return (dom, self.inc_prefix[dom][idx])
        if dom.startswith("q:"):
            return (dom, 16 * (idx + 1))
        return (dom, 1)

    def emit_engine(self, ename, eng, sems):
        waited = {}
        for op in self.eng_ops[ename]:
            if op.drain:
                eng.drain()
            for dep in op.waits:
                dom, val = self.wait_value(dep)
                if waited.get(dom, -1) >= val:
                    continue
                eng.wait_ge(sems[dom], val)
                waited[dom] = val
            ins = op.emit(eng)
            if op.kind == "c":
                if op.inc:
                    ins.then_inc(sems[ename], 1)
            elif op.kind == "d":
                ins.then_inc(sems["q:" + op.chan], 16)
            else:
                ins.then_inc(sems["cc%d" % op.qidx], 1)

    def stats(self):
        n_drain = sum(1 for o, _ in self.ops if o.drain)
        n_wait = sum(len(o.waits) for o, _ in self.ops)
        per_eng = {e: len(self.eng_ops[e]) for e in ENGS}
        return dict(total=len(self.ops), drains=n_drain, waits=n_wait, **per_eng)


def spline_grid(n_ctrl, n):
    t = np.linspace(0.0, n_ctrl - 1.0, n, dtype=np.float64)
    i = np.clip(np.floor(t).astype(np.int64), 0, n_ctrl - 2)
    f = (t - i).astype(np.float32)
    idx = np.stack([np.clip(i + k, 0, n_ctrl - 1) for k in (-1, 0, 1, 2)], 0)
    f2, f3 = f * f, f * f * f
    coeffs = np.stack([
        0.5 * (-f + 2 * f2 - f3),
        0.5 * (2 - 5 * f2 + 3 * f3),
        0.5 * (f + 4 * f2 - 3 * f3),
        0.5 * (-f2 + f3),
    ]).astype(np.float32)
    return idx, coeffs


def host_prep(cfg, inputs):
    g = cfg
    x = inputs["L0_attn_x"]
    W_rvk = inputs["L0_attn_W_rvk"].astype(np.float32)
    W_out = inputs["L0_attn_W_out"].astype(np.float32)
    td = inputs["L0_attn_time_decay"].astype(np.float32)
    tf = inputs["L0_attn_time_first"].astype(np.float32)

    n = g.I * g.C
    idx, coeffs = spline_grid(inputs["L0_ffn_gate_cp"].shape[0], n)

    def tile_gu(a):
        return a.reshape(g.IT, 128, g.CT, 128).transpose(0, 3, 2, 1)

    def tile_dn(a):
        return a.reshape(g.CT, 128, g.IT, 128).transpose(2, 3, 0, 1)

    gate_cp = inputs["L0_ffn_gate_cp"].astype(np.float32)
    up_cp = inputs["L0_ffn_up_cp"].astype(np.float32)
    down_cp = inputs["L0_ffn_down_cp"].astype(np.float32)
    n2 = g.C * g.I
    idx2, coeffs2 = idx, coeffs
    g_arr = [tile_gu(gate_cp[idx[k]]) for k in range(4)] + \
            [tile_gu(coeffs[k]) for k in range(4)]
    u_arr = [tile_gu(up_cp[idx[k]]) for k in range(4)]
    d_arr = [tile_dn(down_cp[idx2[k]]) for k in range(4)] + \
            [tile_dn(coeffs2[k]) for k in range(4)]

    def core_sl(arrs, r):
        per = np.stack(
            [a[g.ITC * r:g.ITC * (r + 1)].transpose(1, 0, 2, 3).reshape(128, -1)
             for a in arrs], axis=1)
        n_arr = per.shape[1]
        samp = per.shape[2]
        nch = samp // g.RCHW
        per = per.reshape(128, n_arr, nch, g.RCHW).transpose(0, 2, 1, 3)
        return np.ascontiguousarray(per.reshape(128, -1))

    wrvk_t = np.ascontiguousarray(
        W_rvk.reshape(g.OT, 128, g.CT, 128).transpose(3, 0, 2, 1).reshape(128, -1))
    wout_t = np.ascontiguousarray(
        W_out.reshape(g.CT, 128, g.CT, 128).transpose(3, 0, 2, 1).reshape(128, -1))
    td_r = np.ascontiguousarray(td.reshape(g.CT, 128).T)
    tf_r = np.ascontiguousarray(tf.reshape(g.CT, 128).T)
    ident = np.eye(128, dtype=np.float32)

    in_maps = []
    for core in range(NCORES):
        b, s = core // 2, core % 2
        xc = np.ascontiguousarray(
            x[b, s * g.TOKS:(s + 1) * g.TOKS].astype(np.float32))
        msk = np.zeros((128, NCORES * 3 * g.CT), np.float32)
        init_add = np.zeros((128, 3 * g.CT), np.float32)
        if s == 1:
            msk[:, (core - 1) * 3 * g.CT:core * 3 * g.CT] = 1.0
        else:
            init_add[:, 2 * g.CT:3 * g.CT] = -1e38
        in_maps.append({
            "x_in": xc,
            "grec": core_sl(g_arr, core),
            "urec": core_sl(u_arr, core),
            "drec": core_sl(d_arr, core),
            "wrvk_f": wrvk_t, "wout_f": wout_t,
            "td_r": td_r, "tf_r": tf_r, "ident": ident,
            "msk": msk, "init_add": init_add,
        })
    return in_maps


def build_nc(g):
    CT, IT, OT, L, NI, SW, LNW, HALF, DW, GH, NG, TB, BIT = (
        g.CT, g.IT, g.OT, g.L, g.NI, g.SW, g.LNW, g.HALF, g.DW, g.GH, g.NG,
        g.TB, g.BIT)
    CL = CT * L
    SAMP = g.ITC * CT * 128

    nc = bass.Bass("TRN2", target_bir_lowering=False)

    x_in = nc.dram_tensor("x_in", [g.TOKS, g.C], F32, kind="ExternalInput")
    grec = nc.dram_tensor("grec", [128, 8 * SAMP], F32, kind="ExternalInput")
    urec = nc.dram_tensor("urec", [128, 4 * SAMP], F32, kind="ExternalInput")
    drec = nc.dram_tensor("drec", [128, 8 * SAMP], F32, kind="ExternalInput")
    wrvk_f = nc.dram_tensor("wrvk_f", [128, OT * CT * 128], F32, kind="ExternalInput")
    wout_f = nc.dram_tensor("wout_f", [128, CT * CT * 128], F32, kind="ExternalInput")
    td_rD = nc.dram_tensor("td_r", [128, CT], F32, kind="ExternalInput")
    tf_rD = nc.dram_tensor("tf_r", [128, CT], F32, kind="ExternalInput")
    identD = nc.dram_tensor("ident", [128, 128], F32, kind="ExternalInput")
    mskD = nc.dram_tensor("msk", [128, NCORES * 3 * CT], F32, kind="ExternalInput")
    initD = nc.dram_tensor("init_add", [128, 3 * CT], F32, kind="ExternalInput")
    x_out = nc.dram_tensor("x_out", [g.TOKS, g.C], F32, kind="ExternalOutput")

    g_sh = nc.dram_tensor("g_sh", [g.ITC * 128, CT * 128], BF16)
    u_sh = nc.dram_tensor("u_sh", [g.ITC * 128, CT * 128], BF16)
    d_sh = nc.dram_tensor("d_sh", [g.ITC * 128, CT * 128], BF16)
    HITC = max(1, g.ITC // 2)
    NSPLIT = 2 if g.ITC >= 2 else 1
    g_full_a = nc.dram_tensor("g_full_a", [NCORES * HITC * 128, CT * 128], BF16, addr_space="Shared")
    g_full_b = nc.dram_tensor("g_full_b", [NCORES * HITC * 128, CT * 128], BF16, addr_space="Shared")
    u_full_a = nc.dram_tensor("u_full_a", [NCORES * HITC * 128, CT * 128], BF16, addr_space="Shared")
    u_full_b = nc.dram_tensor("u_full_b", [NCORES * HITC * 128, CT * 128], BF16, addr_space="Shared")
    d_full = nc.dram_tensor("d_full", [IT * 128, CT * 128], BF16, addr_space="Shared")
    wrvk_b = nc.dram_tensor("wrvk_b", [128, OT * CT * 128], BF16)
    wout_b = nc.dram_tensor("wout_b", [128, CT * CT * 128], BF16)
    inter_s = nc.dram_tensor("inter_s", [IT, 128, HALF], BF16)
    kpk = nc.dram_tensor("kpk", [g.NH, 128, g.NSW * NI * CL], F32)
    vpk = nc.dram_tensor("vpk", [g.NH, 128, g.NSW * NI * CL], F32)
    tpk = nc.dram_tensor("tpk", [g.NH, 128, g.NSW * NI * CL], BF16)
    xr_sp = nc.dram_tensor("xr_sp", [g.NH, 128, CT * HALF], F32)
    ysk = nc.dram_tensor("ysk", [128, g.NSW * CT * SW], BF16)
    st_in = nc.dram_tensor("st_in", [128, 3 * CT], F32)
    st_out = nc.dram_tensor("st_out", [NCORES * 128, 3 * CT], F32)

    sch = Sched()

    with ExitStack() as _ctx:
        EC = _ctx.enter_context
        XR = EC(nc.sbuf_tensor("XR", [128, CT * HALF], F32))
        XN = EC(nc.sbuf_tensor("XN", [128, CT * HALF], BF16))
        XSQ = EC(nc.sbuf_tensor("XSQ", [128, 2 * LNW], BF16))
        KP = EC(nc.sbuf_tensor("KP", [128, NI * CL], F32))
        VP = EC(nc.sbuf_tensor("VP", [128, NI * CL], F32))
        TP = EC(nc.sbuf_tensor("TP", [128, NI * CL], BF16))
        YS = EC(nc.sbuf_tensor("YS", [128, 2 * CT * g.OPW], BF16))
        IBST = EC(nc.sbuf_tensor("IBST", [128, 2 * BIT * HALF], BF16))
        IBO = EC(nc.sbuf_tensor("IBO", [128, 2 * HALF], BF16))
        SG = EC(nc.sbuf_tensor("SG", [128, TB], F32))
        WG = EC(nc.sbuf_tensor("WG", [128, 2 * CT * 128], BF16))
        WU = EC(nc.sbuf_tensor("WU", [128, 2 * CT * 128], BF16))
        WD = EC(nc.sbuf_tensor("WD", [128, 2 * BIT * GH * 128], BF16))
        WR = EC(nc.sbuf_tensor("WR", [128, 2 * CT * 128], BF16))
        KVST = EC(nc.sbuf_tensor("KVST", [128, 2 * g.EBS], F32))
        RST = EC(nc.sbuf_tensor("RST", [128, 2 * g.EBS], BF16))
        EZK = EC(nc.sbuf_tensor("EZK", [128, NI * 2 * CL], F32))
        PQH = EC(nc.sbuf_tensor("PQH", [128, 2 * 2 * CL], F32))
        E3W = EC(nc.sbuf_tensor("E3W", [128, NI * CL], F32))
        TMPS = EC(nc.sbuf_tensor("TMPS", [128, CL], F32))
        SSM = EC(nc.sbuf_tensor("SSM", [128, 28 * CT], F32))
        LNT = EC(nc.sbuf_tensor("LNT", [128, LNW], F32))
        CEWZ = EC(nc.sbuf_tensor("CEWZ", [128, 2 * CL], F32))
        CEU = EC(nc.sbuf_tensor("CEU", [128, CL], F32))
        ONESL = EC(nc.sbuf_tensor("ONESL", [128, L], F32))
        ONESBF = EC(nc.sbuf_tensor("ONESBF", [128, 1], BF16))
        ONER = EC(nc.sbuf_tensor("ONER", [1, 128], F32))
        IDENT = EC(nc.sbuf_tensor("IDENT", [128, 128], F32))
        ROWS = EC(nc.sbuf_tensor("ROWS", [1, 2 * LNW], F32))
        EPSC = EC(nc.sbuf_tensor("EPSC", [1, 1], F32))
        LHC = EC(nc.sbuf_tensor("LHC", [128, 1], F32))
        GTH = EC(nc.sbuf_tensor("GTH", [128, NCORES * 3 * CT], F32))
        MSKS = EC(nc.sbuf_tensor("MSKS", [128, NCORES * 3 * CT], F32))
        INITA = EC(nc.sbuf_tensor("INITA", [128, 3 * CT], F32))
        XST = EC(nc.sbuf_tensor("XST", [128, g.C], F32))
        pA = EC(nc.psum_tensor("pA", [128, 1024], F32))
        pB = EC(nc.psum_tensor("pB", [128, 1024], F32))
        pC = EC(nc.psum_tensor("pC", [128, 1024], F32))
        pD = EC(nc.psum_tensor("pD", [128, 1024], F32))
        s_pe = EC(nc.semaphore("s_pe"))
        s_dve = EC(nc.semaphore("s_dve"))
        s_act = EC(nc.semaphore("s_act"))
        s_pool = EC(nc.semaphore("s_pool"))
        s_sp = EC(nc.semaphore("s_sp"))
        s_cc0 = EC(nc.semaphore("s_cc0"))
        s_cc1 = EC(nc.semaphore("s_cc1"))
        s_cc2 = EC(nc.semaphore("s_cc2"))
        s_cc3 = EC(nc.semaphore("s_cc3"))
        s_cc4 = EC(nc.semaphore("s_cc4"))
        s_cc5 = EC(nc.semaphore("s_cc5"))
        s_cc6 = EC(nc.semaphore("s_cc6"))
        s_cc7 = EC(nc.semaphore("s_cc7"))
        block = EC(nc.Block())
        def ptoks(base, width):
            n = max(1, (width + 511) // 512)
            return [PB(base + i) for i in range(n)]

        sems = {"pe": s_pe, "dve": s_dve, "act": s_act, "pool": s_pool,
                "sp": s_sp,
                "cc0": s_cc0, "cc1": s_cc1, "cc2": s_cc2, "cc3": s_cc3,
                "cc4": s_cc4, "cc5": s_cc5, "cc6": s_cc6, "cc7": s_cc7}

        def A(eng, fn, reads=(), writes=(), tag=""):
            sch.add(eng, fn, reads, writes, "c", tag)

        def D(eng, chan, fn, reads=(), writes=(), tag=""):
            sch.add(eng, fn, reads, writes, "d", tag, chan=chan)

        def CC(fn, reads=(), writes=(), tag=""):
            sch.add("pool", fn, reads, writes, "cc", tag)

        PB = lambda i: "pb%d" % i

        def sl(i):
            return SSM[:, i * CT:(i + 1) * CT]


        D("sp", "ini", lambda e: e.dma_start(out=IDENT[:, :], in_=identD[:, :]),
          writes=["IDENT"])
        D("sp", "ini", lambda e: e.dma_start(out=MSKS[:, :], in_=mskD[:, :]),
          writes=["MSKS"])
        D("sp", "ini", lambda e: e.dma_start(out=INITA[:, :], in_=initD[:, :]),
          writes=["INITA"])
        D("sp", "ini", lambda e: e.dma_start(out=sl(20), in_=td_rD[:, :]), writes=["ssm20"])
        D("sp", "ini", lambda e: e.dma_start(out=sl(21), in_=tf_rD[:, :]), writes=["ssm21"])
        A("pool", lambda e: e.memset(sl(4), 0.0), writes=["st0"])
        A("pool", lambda e: e.memset(sl(5), 0.0), writes=["st0"])
        A("pool", lambda e: e.memset(sl(8), -1e38), writes=["st0"])
        A("pool", lambda e: e.memset(sl(6), 0.0), writes=["st1"])
        A("pool", lambda e: e.memset(sl(7), 0.0), writes=["st1"])
        A("pool", lambda e: e.memset(sl(9), -1e38), writes=["st1"])
        A("pool", lambda e: e.memset(ONESL[:, :], 1.0), writes=["ONESL"])
        A("pool", lambda e: e.memset(EPSC[:, :], 1e-5), writes=["EPSC"])
        A("pool", lambda e: e.memset(LHC[:, :], float(np.log(0.5))), writes=["LHC"])
        A("pool", lambda e: e.memset(ONESBF[:, :], 1.0), writes=["ONESBF"])
        A("pool", lambda e: e.memset(ONER[:, :], 1.0), writes=["ONER"])
        A("act", lambda e: e.activation(sl(18), sl(20), AF.Exp),
          reads=["ssm20"], writes=["ssm18"])
        A("dve", lambda e: e.tensor_scalar_mul(sl(0), sl(18), -1.0),
          reads=["ssm18"], writes=["ssm0"])
        A("act", lambda e: e.activation(sl(2), sl(0), AF.Exp),
          reads=["ssm0"], writes=["ssm2"])
        A("dve", lambda e: e.tensor_scalar_mul(sl(1), sl(0), float(L)),
          reads=["ssm0"], writes=["ssm1"])
        A("act", lambda e: e.activation(sl(3), sl(21), AF.Exp),
          reads=["ssm21"], writes=["ssm3"])
        for ct in range(CT):
            A("dve", lambda e, ct=ct: e.tensor_scalar_mul(
                CEWZ[:, ct * L:(ct + 1) * L], ONESL[:, :], sl(2)[:, ct:ct + 1]),
              reads=["ONESL", "ssm2"], writes=["CEWZ"])
            A("pool", lambda e, ct=ct: e.tensor_scalar_mul(
                CEU[:, ct * L:(ct + 1) * L], ONESL[:, :], sl(3)[:, ct:ct + 1]),
              reads=["ONESL", "ssm3"], writes=["CEU"])
        A("dve", lambda e: e.tensor_copy(CEWZ[:, CL:2 * CL], CEWZ[:, 0:CL]),
          reads=["CEWZ"], writes=["CEWZ"])
        A("dve", lambda e: e.memset(
            CEWZ[:, :].rearrange("p (c l) -> p c l", l=L)[:, :, 0:1], 0.0),
          writes=["CEWZ"])

        RCHW = g.RCHW

        def recon(rec_dram, n_own, coef_src, sh_dram, wtag):
            nch = SAMP // RCHW
            ibf = IBST[:, :].bitcast(F32)
            wgf = WG[:, :].bitcast(F32)
            bufs = [ibf[:, 0:8 * RCHW], ibf[:, 8 * RCHW:16 * RCHW],
                    wgf[:, 0:8 * RCHW], wgf[:, 8 * RCHW:16 * RCHW]]
            queues = ["act", "sp", "act", "sp"]
            for s in range(nch):
                bi = s % 4
                buf = bufs[bi]
                btok = "rs%d" % bi
                q = queues[bi]

                def ld(e, s=s, buf=buf, rec=rec_dram, n_own=n_own):
                    return e.dma_start(
                        out=buf[:, 0:n_own * RCHW],
                        in_=rec[:, s * n_own * RCHW:(s + 1) * n_own * RCHW])
                D(q, btok, ld, writes=[btok], tag=wtag + "ld")
                if coef_src is not None:
                    def ldc(e, s=s, buf=buf, cd=coef_src):
                        return e.dma_start(
                            out=buf[:, 4 * RCHW:8 * RCHW],
                            in_=cd[:, s * 8 * RCHW + 4 * RCHW:
                                   (s + 1) * 8 * RCHW])
                    D(q, btok, ldc, writes=[btok], tag=wtag + "ldc")
                perit = CT * 128
                it_loc = (s * RCHW) // perit
                coff = (s * RCHW) % perit
                wo = WD[:, coff:coff + RCHW]
                par = s % 3
                ceng = "pool" if par == 2 else "dve"
                wtok = "rw%d" % par
                t0 = PQH[:, par * RCHW:(par + 1) * RCHW]
                t1 = EZK[:, par * RCHW:(par + 1) * RCHW]
                rt0, rt1 = "rt0_%d" % par, "rt1_%d" % par
                a_ = lambda i, buf=buf: buf[:, i * RCHW:(i + 1) * RCHW]
                A(ceng, lambda e, a=a_, t0=t0: e.tensor_mul(t0, a(0), a(4)),
                  reads=[btok], writes=[rt0], tag=wtag)
                A(ceng, lambda e, a=a_, t1=t1: e.tensor_mul(t1, a(1), a(5)),
                  reads=[btok], writes=[rt1], tag=wtag)
                A(ceng, lambda e, t0=t0, t1=t1: e.tensor_add(t0, t0, t1),
                  reads=[rt0, rt1], writes=[rt0], tag=wtag)
                A(ceng, lambda e, a=a_, t1=t1: e.tensor_mul(t1, a(2), a(6)),
                  reads=[btok, rt0], writes=[rt1], tag=wtag)
                A(ceng, lambda e, t0=t0, t1=t1: e.tensor_add(t0, t0, t1),
                  reads=[rt0, rt1], writes=[rt0], tag=wtag)
                A(ceng, lambda e, a=a_, t1=t1: e.tensor_mul(t1, a(3), a(7)),
                  reads=[btok, rt0], writes=[rt1], tag=wtag)
                A(ceng, lambda e, wo=wo, t0=t0, t1=t1: e.tensor_add(wo, t0, t1),
                  reads=[rt0, rt1], writes=[wtok], tag=wtag)
                if coff + RCHW == perit:
                    def stw(e, it_loc=it_loc, sh=sh_dram):
                        return e.dma_start(
                            out=sh[it_loc * 128:(it_loc + 1) * 128, :],
                            in_=WD[:, 0:perit])
                    D("sp", "sst0", stw, reads=["rw0", "rw1", "rw2"],
                      writes=["%s_sh%d" % (wtag, it_loc)], tag=wtag + "st")
            return ["%s_sh%d" % (wtag, i) for i in range(g.ITC)]


        def xr_sl(ct, t0, n):
            return XR[:, ct * HALF + t0:ct * HALF + t0 + n]

        def xr_tok(ct, t0):
            return "XR_%d_%d" % (ct, t0 // DW)

        def xr_toks_range(ct, t0, n):
            return ["XR_%d_%d" % (ct, d) for d in
                    range(t0 // DW, (t0 + n - 1) // DW + 1)]

        def xn_sl(ct, t0, n):
            return XN[:, ct * HALF + t0:ct * HALF + t0 + n]

        def xn_tok(ct, w):
            return "XN_%d_%d" % (ct, w)

        def xn_toks_range(ct, t0, n):
            return [xn_tok(ct, w) for w in range(t0 // LNW, (t0 + n - 1) // LNW + 1)]

        def ln_pass(h, w, pid):
            t0 = w * LNW
            px = "ln%s_h%d_w%d" % (pid, h, w)
            rd = lambda i: ROWS[0:1, i * LNW:(i + 1) * LNW]
            for ct in range(CT):
                A("dve", lambda e, ct=ct: e.tensor_copy(
                    xn_sl(ct, t0, LNW), xr_sl(ct, t0, LNW)),
                  reads=[xr_tok(ct, t0)], writes=[xn_tok(ct, w)], tag=px)
            for ct in range(CT):
                xq = XSQ[:, (ct % 2) * LNW:(ct % 2 + 1) * LNW]
                xqt = "XSQ%d" % (ct % 2)
                A("pool", lambda e, ct=ct, xq=xq: e.tensor_mul(
                    xq, xn_sl(ct, t0, LNW), xn_sl(ct, t0, LNW)),
                  reads=[xn_tok(ct, w)], writes=[xqt], tag=px)
                A("pe", lambda e, ct=ct: e.matmul(
                    pB[0:1, 0:LNW], ONESBF[:, :], xn_sl(ct, t0, LNW),
                    start=(ct == 0), stop=(ct == CT - 1)),
                  reads=[xn_tok(ct, w), "ONESBF"], writes=[PB(2)], tag=px)
                A("pe", lambda e, ct=ct, xq=xq: e.matmul(
                    pB[0:1, 512:512 + LNW], ONESBF[:, :], xq,
                    start=(ct == 0), stop=(ct == CT - 1)),
                  reads=[xqt, "ONESBF"], writes=[PB(3)], tag=px)
            A("dve", lambda e: e.tensor_scalar_mul(rd(0), pB[0:1, 0:LNW], 1.0 / g.C),
              reads=[PB(2)], writes=["row0"], tag=px)
            A("dve", lambda e: e.tensor_scalar_mul(rd(1), pB[0:1, 512:512 + LNW], 1.0 / g.C),
              reads=[PB(3)], writes=["row1"], tag=px)
            A("dve", lambda e: e.tensor_mul(LNT[0:1, :], rd(0), rd(0)),
              reads=["row0"], writes=["LNT"], tag=px)
            A("dve", lambda e: e.tensor_sub(rd(1), rd(1), LNT[0:1, :]),
              reads=["row1", "LNT"], writes=["row1"], tag=px)
            A("act", lambda e: e.activation(rd(1), rd(1), AF.Ln, bias=EPSC[:, 0:1]),
              reads=["row1", "EPSC"], writes=["row1"], tag=px)
            A("act", lambda e: e.activation(rd(1), rd(1), AF.Exp, scale=-0.5),
              reads=["row1"], writes=["row1"], tag=px)
            A("dve", lambda e: e.tensor_mul(rd(0), rd(0), rd(1)),
              reads=["row0", "row1"], writes=["row0"], tag=px)
            A("pe", lambda e: e.matmul(pB[:, LNW:2 * LNW], ONER[:, :], rd(1),
                                       start=True, stop=True),
              reads=["row1", "ONER"], writes=[PB(2)], tag=px)
            A("pe", lambda e: e.matmul(pB[:, 512 + LNW:512 + 2 * LNW], ONER[:, :], rd(0),
                                       start=True, stop=True),
              reads=["row0", "ONER"], writes=[PB(3)], tag=px)
            for ct in range(CT):
                A("dve", lambda e, ct=ct: e.tensor_mul(
                    LNT[:, :], xr_sl(ct, t0, LNW), pB[:, LNW:2 * LNW]),
                  reads=[xr_tok(ct, t0), PB(2)], writes=["LNT"], tag=px)
                A("dve", lambda e, ct=ct: e.tensor_sub(
                    xn_sl(ct, t0, LNW), LNT[:, :], pB[:, 512 + LNW:512 + 2 * LNW]),
                  reads=["LNT", PB(3)], writes=[xn_tok(ct, w)], tag=px)

        GU_ORDER = [r * g.ITC + ph * HITC + l
                    for ph in range(NSPLIT) for r in range(NCORES)
                    for l in range(HITC)]
        assert sorted(GU_ORDER) == list(range(IT))

        def gu_src(it):
            r, l = it // g.ITC, it % g.ITC
            ph = l // HITC
            row = (r * HITC + (l % HITC)) * 128
            return ph, row

        def gu_sweep_gen(h, call):
            px = "gu%d_h%d" % (call, h)
            NTB = HALF // TB
            for ii, it in enumerate(GU_ORDER):
                yield
                gb = WG[:, (ii % 2) * CT * 128:(ii % 2 + 1) * CT * 128]
                ub = WU[:, (ii % 2) * CT * 128:(ii % 2 + 1) * CT * 128]
                gbt, ubt = "WG%d" % (ii % 2), "WU%d" % (ii % 2)
                ph, row = gu_src(it)
                gsrc = g_full_a if ph == 0 else g_full_b
                usrc = u_full_a if ph == 0 else u_full_b
                gtok = "g_fullA" if ph == 0 else "g_fullB"
                utok = "u_fullA" if ph == 0 else "u_fullB"
                D("sp", gbt, lambda e, row=row, gb=gb, gsrc=gsrc: e.dma_start(
                    out=gb, in_=gsrc[row:row + 128, :]),
                  reads=[gtok], writes=[gbt], tag=px + "ld")
                D("sp", ubt, lambda e, row=row, ub=ub, usrc=usrc: e.dma_start(
                    out=ub, in_=usrc[row:row + 128, :]),
                  reads=[utok], writes=[ubt], tag=px + "ld")
                gp = pA if ii % 2 == 0 else pC
                up = pB if ii % 2 == 0 else pD
                gpt = ptoks(0 if ii % 2 == 0 else 4, TB)
                upt = ptoks(2 if ii % 2 == 0 else 6, TB)
                NB = (TB + 511) // 512
                BW = min(512, TB)
                for tb in range(NTB):
                    for ht in range(CT):
                        xr_ = xn_toks_range(ht, tb * TB, TB)
                        for nb in range(NB):
                            A("pe", lambda e, ht=ht, tb=tb, nb=nb, gp=gp, gb=gb:
                              e.matmul(
                                gp[:, nb * BW:(nb + 1) * BW],
                                gb[:, ht * 128:(ht + 1) * 128],
                                XN[:, ht * HALF + tb * TB + nb * BW:
                                   ht * HALF + tb * TB + (nb + 1) * BW],
                                start=(ht == 0), stop=(ht == CT - 1)),
                              reads=[gbt] + xr_, writes=gpt, tag=px)
                    for ht in range(CT):
                        xr_ = xn_toks_range(ht, tb * TB, TB)
                        for nb in range(NB):
                            A("pe", lambda e, ht=ht, tb=tb, nb=nb, up=up, ub=ub:
                              e.matmul(
                                up[:, nb * BW:(nb + 1) * BW],
                                ub[:, ht * 128:(ht + 1) * 128],
                                XN[:, ht * HALF + tb * TB + nb * BW:
                                   ht * HALF + tb * TB + (nb + 1) * BW],
                                start=(ht == 0), stop=(ht == CT - 1)),
                              reads=[ubt] + xr_, writes=upt, tag=px)
                    ib = IBO[:, (ii % 2) * HALF + tb * TB:(ii % 2) * HALF + (tb + 1) * TB]
                    A("act", lambda e, gp=gp: e.activation(SG[:, 0:TB], gp[:, 0:TB], AF.Silu),
                      reads=gpt, writes=["SG"], tag=px)
                    A("dve", lambda e, up=up, ib=ib: e.tensor_mul(ib, SG[:, 0:TB], up[:, 0:TB]),
                      reads=["SG"] + upt, writes=["IBO%d" % (it % 2)], tag=px)
                D("pool", "int", lambda e, it=it: e.dma_start(
                    out=inter_s[it, :, :],
                    in_=IBO[:, (it % 2) * HALF:(it % 2 + 1) * HALF]),
                  reads=["IBO%d" % (it % 2)], writes=["int_%d" % it], tag=px + "st")

        def down_pass(h, call):
            px = "dn%d_h%d" % (call, h)
            for gr in range(NG):
                if GH == 4:
                    pmap4 = [pA, pB, pC, pD]
                    pbase = 0
                else:
                    pmap4 = ([pA, pB] if gr % 2 == 0 else [pC, pD]) * 2
                    pbase = 0 if gr % 2 == 0 else 4
                for itb in range(IT // BIT):
                    wdb = WD[:, (itb % 2) * BIT * GH * 128:
                             (itb % 2 + 1) * BIT * GH * 128]
                    wdt = "WD%d" % (itb % 2)
                    D("sp", wdt, lambda e, itb=itb, gr=gr, wdb=wdb: e.dma_start(
                        out=wdb.rearrange("p (i x) -> p i x", i=BIT),
                        in_=d_full[itb * BIT * 128:(itb + 1) * BIT * 128,
                                   gr * GH * 128:(gr + 1) * GH * 128]
                        .rearrange("(i p) x -> p i x", p=128)),
                      reads=["d_full"], writes=[wdt], tag=px + "ld")
                    ib4 = IBST[:, (itb % 2) * BIT * HALF:(itb % 2 + 1) * BIT * HALF]
                    ibt = "IBST%d" % (itb % 2)
                    D("sp", ibt, lambda e, itb=itb, ib4=ib4: e.dma_start(
                        out=ib4.rearrange("p (i x) -> p i x", i=BIT),
                        in_=inter_s[itb * BIT:(itb + 1) * BIT, :, :]
                        .rearrange("i p x -> p i x")),
                      reads=["int_%d" % (itb * BIT + ii) for ii in range(BIT)],
                      writes=[ibt], tag=px + "ldi")
                    for ii in range(BIT):
                        it = itb * BIT + ii
                        for hh in range(GH):
                            for wd in range(g.NDW):
                                dtile = pmap4[hh % 4 if GH == 4 else hh % 2][
                                    :, (wd % 2) * 512:(wd % 2) * 512 + DW]
                                A("pe", lambda e, ii=ii, hh=hh, it=it, wd=wd,
                                    wdb=wdb, ib4=ib4, dtile=dtile: e.matmul(
                                    dtile,
                                    wdb[:, (ii * GH + hh) * 128:
                                        (ii * GH + hh + 1) * 128],
                                    ib4[:, ii * HALF + wd * DW:
                                        ii * HALF + (wd + 1) * DW],
                                    start=(it == 0), stop=(it == IT - 1)),
                                  reads=[wdt, ibt],
                                  writes=[PB((2 * hh + (wd % 2)) if GH == 4 else
                                             (pbase + 2 * (hh % 2) + (wd % 2)))],
                                  tag=px)
                for hh in range(GH):
                    for wd in range(g.NDW):
                        ht = gr * GH + hh
                        dtile = pmap4[hh % 4 if GH == 4 else hh % 2][
                            :, (wd % 2) * 512:(wd % 2) * 512 + DW]
                        A("dve", lambda e, ht=ht, wd=wd, dtile=dtile:
                          e.scalar_tensor_tensor(
                            xr_sl(ht, wd * DW, DW), dtile, 0.5,
                            xr_sl(ht, wd * DW, DW), ALU.mult, ALU.add),
                          reads=[PB((2 * hh + (wd % 2)) if GH == 4 else
                                    (pbase + 2 * (hh % 2) + (wd % 2))),
                                 xr_tok(ht, wd * DW)],
                          writes=[xr_tok(ht, wd * DW)], tag=px + "ev")

        def rvk_gen(h):
            px = "rvk_h%d" % h
            EB, EBS = g.EB, g.EBS
            NSUB = EB // EBS
            ot_order = ([2 * CT + i for i in range(CT)] +
                        [CT + i for i in range(CT)] + list(range(CT)))
            for oi, ot in enumerate(ot_order):
                yield
                wb = WR[:, (oi % 2) * CT * 128:(oi % 2 + 1) * CT * 128]
                wbt = "WR%d" % (oi % 2)
                D("sp", wbt, lambda e, ot=ot, wb=wb: e.dma_start(
                    out=wb, in_=wrvk_b[:, ot * CT * 128:(ot + 1) * CT * 128]),
                  reads=["wrvkb%d" % ot], writes=[wbt], tag=px + "ld")
                pp = [pA, pB, pC, pD][oi % 4]
                ppt = ptoks([0, 2, 4, 6][oi % 4], EB)
                kind = ot // CT
                ct = ot % CT
                NBE = (EB + 511) // 512
                BWE = min(512, EB)
                for ht in range(CT):
                    xr_ = xn_toks_range(ht, 0, HALF)
                    for nb in range(NBE):
                        A("pe", lambda e, ht=ht, nb=nb, pp=pp, wb=wb: e.matmul(
                            pp[:, nb * BWE:(nb + 1) * BWE],
                            wb[:, ht * 128:(ht + 1) * 128],
                            XN[:, ht * HALF + nb * BWE:ht * HALF + (nb + 1) * BWE],
                            start=(ht == 0), stop=(ht == CT - 1)),
                          reads=[wbt] + xr_, writes=ppt, tag=px)
                for sub in range(NSUB):
                    bsl = sub % 2
                    if kind == 0:
                        stg = RST[:, bsl * EBS:(bsl + 1) * EBS]
                        stok = "RST%d" % bsl
                        A("act", lambda e, stg=stg, pp=pp, sub=sub: e.activation(
                            stg, pp[:, sub * EBS:(sub + 1) * EBS], AF.Tanh,
                            scale=0.5),
                          reads=ppt, writes=[stok], tag=px)
                        dstp = tpk
                    else:
                        stg = KVST[:, bsl * EBS:(bsl + 1) * EBS]
                        stok = "KVST%d" % bsl
                        A("act", lambda e, stg=stg, pp=pp, sub=sub: e.copy(
                            stg, pp[:, sub * EBS:(sub + 1) * EBS]),
                          reads=ppt, writes=[stok], tag=px)
                        dstp = vpk if kind == 1 else kpk
                    wpersb = EBS // SW

                    def stp(e, stg=stg, ot=ot, ct=ct, h=h, sub=sub, dstp=dstp,
                            wpersb=wpersb):
                        dst = dstp[h, :, :].rearrange(
                            "p (wj c l) -> p wj c l", c=CT, l=L)[
                            :, sub * wpersb * NI:(sub + 1) * wpersb * NI, ct, :]
                        return e.dma_start(
                            out=dst,
                            in_=stg.rearrange("p (wj l) -> p wj l", l=L))
                    D("pool", "pk", stp, reads=[stok],
                      writes=["%s_h%d_%d_%d" % (("tpk", "vpk", "kpk")[kind], h, ct, sub)],
                      tag=px + "st")

        def rvk_sweep(h, side=(), start=0):
            gen = rvk_gen(h)
            i = 0
            side = list(side)
            si = 0
            n_after = OT - start
            every = max(1, n_after // max(1, len(side))) if side else 1
            for _ in gen:
                i += 1
                if side and i > start and (i - start) % every == 0 and si < len(side):
                    side[si]()
                    si += 1
            while si < len(side):
                side[si]()
                si += 1

        def pack_toks(pfx, h):
            return ["%s_h%d_%d_%d" % (pfx, h, ct, sub)
                    for ct in range(CT) for sub in range(g.EB // g.EBS)]

        sidx = [0]

        def kv_load(h, w, full):
            NICL = NI * CL
            D("pool", "KP", lambda e, h=h, w=w: e.dma_start(
                out=KP[:, :], in_=kpk[h, :, w * NICL:(w + 1) * NICL]),
              reads=pack_toks("kpk", h), writes=["KP"], tag="kvld")
            D("pool", "VP", lambda e, h=h, w=w: e.dma_start(
                out=VP[:, :], in_=vpk[h, :, w * NICL:(w + 1) * NICL]),
              reads=pack_toks("vpk", h), writes=["VP"], tag="kvld")
            if full:
                D("pool", "TP", lambda e, h=h, w=w: e.dma_start(
                    out=TP[:, :], in_=tpk[h, :, w * NICL:(w + 1) * NICL]),
                  reads=pack_toks("tpk", h), writes=["TP"], tag="kvld")

        def scan_window(h, w, full, to_scratch=False):
            kv_load(h, w, full)
            px = "sc%s_h%d_w%d" % ("F" if full else "S", h, w)
            pars = [(sidx[0] + j) % 2 for j in range(NI)]
            ezv_ = lambda j: EZK[:, j * 2 * CL:j * 2 * CL + CL]
            ekk_ = lambda j: EZK[:, j * 2 * CL + CL:(j + 1) * 2 * CL]
            ph_ = lambda j: PQH[:, pars[j] * 2 * CL:pars[j] * 2 * CL + CL]
            qh_ = lambda j: PQH[:, pars[j] * 2 * CL + CL:(pars[j] + 1) * 2 * CL]
            e3_ = lambda j: E3W[:, j * CL:(j + 1) * CL]
            lam_sl = [13, 18]
            r3 = lambda ap: ap.rearrange("p (c l) -> p c l", l=L)
            bco = lambda ap: ap.rearrange("p (c o) -> p c o", o=1)
            for j in range(NI):
                kj = KP[:, j * CL:(j + 1) * CL]
                A("dve", lambda e, j=j, kj=kj: e.reduce_max(
                    sl(10 + j), r3(kj), axis=AX.X),
                  reads=["KP"], writes=["cmk%d" % j], tag=px)
            for j in range(NI):
                sbp, nbp = pars[j], 1 - pars[j]
                Mp, Mn = sl(8 + sbp), sl(8 + nbp)
                A("pool", lambda e, Mp=Mp: e.tensor_add(sl(12), Mp, sl(1)),
                  reads=["st%d" % sbp, "ssm1"], writes=["ssm12"], tag=px)
                A("dve", lambda e, Mn=Mn, j=j: e.tensor_max(Mn, sl(12), sl(10 + j)),
                  reads=["ssm12", "cmk%d" % j], writes=["st%d" % nbp], tag=px)
                A("dve", lambda e, Mp=Mp, Mn=Mn, j=j: e.tensor_sub(
                    sl(lam_sl[j]), Mp, Mn),
                  reads=["st%d" % sbp, "st%d" % nbp], writes=["lam%d" % j], tag=px)
                A("act", lambda e, j=j: e.activation(
                    sl(lam_sl[j]), sl(lam_sl[j]), AF.Exp),
                  reads=["lam%d" % j], writes=["lam%d" % j], tag=px)
            for j in range(NI):
                kj = KP[:, j * CL:(j + 1) * CL]
                Mn = sl(8 + (1 - pars[j]))
                A("dve", lambda e, j=j, kj=kj, Mn=Mn: e.tensor_tensor(
                    r3(ekk_(j)), r3(kj),
                    bco(Mn).broadcast_to([128, CT, L]), ALU.subtract),
                  reads=["KP", "st%d" % (1 - pars[j])], writes=["ekk%d" % j],
                  tag=px)
            eks = EZK[:, :].rearrange("p (j x) -> p j x", j=NI)[:, :, CL:2 * CL]
            A("act", lambda e: e.activation(eks, eks, AF.Exp),
              reads=["ekk%d" % j2 for j2 in range(NI)],
              writes=["ekk%d" % j2 for j2 in range(NI)], tag=px)
            ezs = EZK[:, :].rearrange("p (j x) -> p j x", j=NI)[:, :, 0:CL]
            vps = VP[:, :].rearrange("p (j x) -> p j x", j=NI)
            A("dve", lambda e: e.tensor_tensor(ezs, eks, vps, ALU.mult),
              reads=["ekk%d" % j2 for j2 in range(NI)] + ["VP"],
              writes=["ezv%d" % j2 for j2 in range(NI)], tag=px)
            if full:
                e3s = E3W[:, :].rearrange("p (j x) -> p j x", j=NI)
                ceub = CEU[:, :].rearrange("p (o x) -> p o x", o=1)                     .broadcast_to([128, NI, CL])
                A("pool", lambda e: e.tensor_tensor(e3s, eks, ceub, ALU.mult),
                  reads=["ekk%d" % j2 for j2 in range(NI)] + ["CEU"],
                  writes=["e3w"], tag=px)
            for j in range(NI):
                sbp, nbp = pars[j], 1 - pars[j]
                Pp, Qp = sl(4 + sbp * 2), sl(5 + sbp * 2)
                Pn, Qn = sl(4 + nbp * 2), sl(5 + nbp * 2)
                stok, ntok = "st%d" % sbp, "st%d" % nbp
                A("dve", lambda e, j=j, Pp=Pp: e.tensor_mul(sl(14), Pp, sl(lam_sl[j])),
                  reads=[stok, "lam%d" % j], writes=["ssm14"], tag=px)
                A("dve", lambda e, j=j, Qp=Qp: e.tensor_mul(sl(15), Qp, sl(lam_sl[j])),
                  reads=[stok, "lam%d" % j], writes=["ssm15"], tag=px)
                A("dve", lambda e: e.tensor_mul(sl(16), sl(14), sl(2)),
                  reads=["ssm14", "ssm2"], writes=["ssm16"], tag=px)
                A("dve", lambda e: e.tensor_mul(sl(17), sl(15), sl(2)),
                  reads=["ssm15", "ssm2"], writes=["ssm17"], tag=px)
                ez3, ek3 = r3(ezv_(j)), r3(ekk_(j))
                A("dve", lambda e, ez3=ez3: e.tensor_add(
                    ez3[:, :, 0:1], ez3[:, :, 0:1], bco(sl(16))),
                  reads=["ezv%d" % j, "ssm16"], writes=["ezv%d" % j], tag=px)
                A("dve", lambda e, ek3=ek3: e.tensor_add(
                    ek3[:, :, 0:1], ek3[:, :, 0:1], bco(sl(17))),
                  reads=["ekk%d" % j, "ssm17"], writes=["ekk%d" % j], tag=px)
                A("dve", lambda e, j=j: e.tensor_tensor_scan(
                    PQH[:, pars[j] * 2 * CL:(pars[j] + 1) * 2 * CL],
                    CEWZ[:, :], EZK[:, j * 2 * CL:(j + 1) * 2 * CL], 0.0,
                    ALU.mult, ALU.add),
                  reads=["CEWZ", "ezv%d" % j, "ekk%d" % j],
                  writes=["pqh%d" % pars[j]], tag=px)
                A("dve", lambda e, j=j, Pn=Pn: e.tensor_copy(
                    bco(Pn), r3(ph_(j))[:, :, L - 1:L]),
                  reads=["pqh%d" % pars[j]], writes=[ntok], tag=px)
                A("pool", lambda e, j=j, Qn=Qn: e.tensor_copy(
                    bco(Qn), r3(qh_(j))[:, :, L - 1:L]),
                  reads=["pqh%d" % pars[j]], writes=[ntok], tag=px)
                if full:
                    A("pool", lambda e, j=j: e.tensor_copy(sl(22 + j), sl(14)),
                      reads=["ssm14"], writes=["ip%d" % j], tag=px)
                    A("pool", lambda e, j=j: e.tensor_copy(sl(24 + j), sl(15)),
                      reads=["ssm15"], writes=["iq%d" % j], tag=px)
                sidx[0] += 1
            if not full:
                return
            b = (w // 2) % 2
            for j in range(NI):
                PSH, QSH = ezv_(j), ekk_(j)
                ps3, qs3 = r3(PSH), r3(QSH)
                ph3, qh3 = r3(ph_(j)), r3(qh_(j))
                A("dve", lambda e, ps3=ps3, ph3=ph3: e.tensor_copy(
                    ps3[:, :, 1:L], ph3[:, :, 0:L - 1]),
                  reads=["pqh%d" % pars[j], "ezv%d" % j], writes=["ezv%d" % j],
                  tag=px)
                A("dve", lambda e, ps3=ps3, j=j: e.tensor_copy(
                    ps3[:, :, 0:1], bco(sl(22 + j))),
                  reads=["ip%d" % j], writes=["ezv%d" % j], tag=px)
                A("pool", lambda e, qs3=qs3, qh3=qh3: e.tensor_copy(
                    qs3[:, :, 1:L], qh3[:, :, 0:L - 1]),
                  reads=["pqh%d" % pars[j], "ekk%d" % j], writes=["ekk%d" % j],
                  tag=px)
                A("pool", lambda e, qs3=qs3, j=j: e.tensor_copy(
                    qs3[:, :, 0:1], bco(sl(24 + j))),
                  reads=["iq%d" % j], writes=["ekk%d" % j], tag=px)
                vj = VP[:, j * CL:(j + 1) * CL]
                A("dve", lambda e, j=j, vj=vj: e.tensor_mul(TMPS[:, :], e3_(j), vj),
                  reads=["e3w", "VP"], writes=["tms"], tag=px)
                A("dve", lambda e, PSH=PSH: e.tensor_add(PSH, PSH, TMPS[:, :]),
                  reads=["ezv%d" % j, "tms"], writes=["ezv%d" % j], tag=px)
                A("pool", lambda e, QSH=QSH, j=j: e.tensor_add(QSH, QSH, e3_(j)),
                  reads=["ekk%d" % j, "e3w"], writes=["ekk%d" % j], tag=px)
                A("act", lambda e, QSH=QSH: e.activation(QSH, QSH, AF.Ln),
                  reads=["ekk%d" % j], writes=["ekk%d" % j], tag=px)
                A("act", lambda e, QSH=QSH: e.activation(
                    QSH, QSH, AF.Exp, scale=-1.0, bias=LHC[:, 0:1]),
                  reads=["ekk%d" % j, "LHC"], writes=["ekk%d" % j], tag=px)
                A("dve", lambda e, PSH=PSH, QSH=QSH: e.tensor_mul(PSH, PSH, QSH),
                  reads=["ezv%d" % j, "ekk%d" % j], writes=["ezv%d" % j], tag=px)
                tj = TP[:, j * CL:(j + 1) * CL]
                A("pool", lambda e, PSH=PSH, tj=tj: e.tensor_mul(TMPS[:, :], PSH, tj),
                  reads=["ezv%d" % j, "TP"], writes=["tms"], tag=px)
                ysb = YS[:, b * CT * g.OPW:(b + 1) * CT * g.OPW]
                ys3 = ysb.rearrange("p (c s) -> p c s", s=g.OPW)[
                    :, :, (w % 2) * SW + j * L:(w % 2) * SW + (j + 1) * L]
                A("dve", lambda e, ys3=ys3, PSH=PSH: e.tensor_add(
                    ys3, r3(PSH), r3(TMPS[:, :])),
                  reads=["ezv%d" % j, "tms"],
                  writes=["YS%d_%d_%d" % (b, w % 2, j)], tag=px)
            if to_scratch:
                D("pool", "ysst", lambda e, w=w, b=b: e.dma_start(
                    out=ysk[:, :].rearrange(
                        "p (wp c s) -> p wp c s", c=CT, s=g.OPW)[
                        :, w // 2, :, (w % 2) * SW:(w % 2 + 1) * SW],
                    in_=YS[:, b * CT * g.OPW:(b + 1) * CT * g.OPW].rearrange(
                        "p (c o) -> p c o", o=g.OPW)[:, :, (w % 2) * SW:
                                                     (w % 2 + 1) * SW]),
                  reads=["YS%d_%d_%d" % (b, w % 2, j) for j in range(NI)],
                  writes=["ysk_%d" % w], tag="ysst")

        def weave(main_gen, side_steps, every):
            i = 0
            side = list(side_steps)
            si = 0
            for _ in main_gen:
                i += 1
                if si < len(side) and i % every == 0:
                    side[si]()
                    si += 1
            while si < len(side):
                side[si]()
                si += 1

        def gu_sweep(h, call, side=()):
            weave(gu_sweep_gen(h, call), side, max(1, IT // max(1, len(side) + 1)))

        def outproj(h, wp, from_scratch=False):
            px = "op_h%d_p%d" % (h, wp)
            t0 = wp * g.OPW
            b = wp % 2
            ysb = YS[:, b * CT * g.OPW:(b + 1) * CT * g.OPW]
            ystoks = ["YS%d_%d_%d" % (b, k, jj)
                      for k in range(2) for jj in range(NI)]
            if from_scratch:
                D("pool", "ysld", lambda e, wp=wp, ysb=ysb: e.dma_start(
                    out=ysb, in_=ysk[:, wp * CT * g.OPW:(wp + 1) * CT * g.OPW]),
                  reads=["ysk_%d" % (2 * wp), "ysk_%d" % (2 * wp + 1)],
                  writes=ystoks, tag=px + "ysld")
            for ot in range(CT):
                wb = WR[:, (ot % 2) * CT * 128:(ot % 2 + 1) * CT * 128]
                wbt = "WR%d" % (ot % 2)
                D("sp", wbt, lambda e, ot=ot, wb=wb: e.dma_start(
                    out=wb, in_=wout_b[:, ot * CT * 128:(ot + 1) * CT * 128]),
                  reads=["woutb%d" % ot], writes=[wbt], tag=px + "ld")
                pp = pC if ot % 2 == 0 else pD
                ppt = PB(4 if ot % 2 == 0 else 6)
                for ct in range(CT):
                    A("pe", lambda e, ct=ct, pp=pp, wb=wb, ysb=ysb: e.matmul(
                        pp[:, 0:g.OPW], wb[:, ct * 128:(ct + 1) * 128],
                        ysb[:, ct * g.OPW:(ct + 1) * g.OPW],
                        start=(ct == 0), stop=(ct == CT - 1)),
                      reads=[wbt] + ystoks, writes=[ppt], tag=px)
                A("dve", lambda e, ot=ot, pp=pp, t0=t0: e.tensor_add(
                    xr_sl(ot, t0, g.OPW), xr_sl(ot, t0, g.OPW), pp[:, 0:g.OPW]),
                  reads=[ppt, xr_tok(ot, t0)], writes=[xr_tok(ot, t0)],
                  tag=px + "ev")

        def x_load(h):
            px = "xin_h%d" % h
            for tt in range(HALF // 128):
                D("sp", "XST", lambda e, tt=tt, h=h: e.dma_start(
                    out=XST[:, :],
                    in_=x_in[h * HALF + tt * 128:h * HALF + (tt + 1) * 128, :]),
                  writes=["XST"], tag=px)
                for ct in range(CT):
                    A("pe", lambda e, ct=ct: e.transpose(
                        pA[:, 0:128], XST[:, ct * 128:(ct + 1) * 128], IDENT[:, :]),
                      reads=["XST", "IDENT"], writes=[PB(0)], tag=px)
                    A("act", lambda e, ct=ct, tt=tt: e.copy(
                        xr_sl(ct, tt * 128, 128), pA[:, 0:128]),
                      reads=[PB(0)], writes=xr_toks_range(ct, tt * 128, 128),
                      tag=px)

        def x_store(h):
            px = "xout_h%d" % h
            outs = []
            for tt in range(HALF // 128):
                for ct in range(CT):
                    A("pe", lambda e, ct=ct, tt=tt: e.transpose(
                        pA[:, 128:256], xr_sl(ct, tt * 128, 128), IDENT[:, :]),
                      reads=xr_toks_range(ct, tt * 128, 128) + ["IDENT"],
                      writes=[PB(0)], tag=px)
                    A("act", lambda e, ct=ct: e.copy(
                        XST[:, ct * 128:(ct + 1) * 128], pA[:, 128:256]),
                      reads=[PB(0)], writes=["XST"], tag=px)
                tok = "xout_%d_%d" % (h, tt)
                D("pool", "xo", lambda e, tt=tt, h=h: e.dma_start(
                    out=x_out[h * HALF + tt * 128:h * HALF + (tt + 1) * 128, :],
                    in_=XST[:, :]),
                  reads=["XST"], writes=[tok], tag=px)
                outs.append(tok)
            return outs

        x_load(0)
        for w in range(g.NLW):
            ln_pass(0, w, "1")
        gsh_toks = recon(grec, 8, None, g_sh, "g")
        HB = HITC * 128
        CC(lambda e: e.collective_compute(
            "AllGather", ALU.bypass, replica_groups=[list(range(NCORES))],
            ins=[g_sh[0:HB, :]], outs=[g_full_a.ap().opt()]),
           reads=gsh_toks[:HITC], writes=["g_fullA"], tag="AGg1")
        ush_toks = recon(urec, 4, grec, u_sh, "u")
        CC(lambda e: e.collective_compute(
            "AllGather", ALU.bypass, replica_groups=[list(range(NCORES))],
            ins=[u_sh[0:HB, :]], outs=[u_full_a.ap().opt()]),
           reads=ush_toks[:HITC], writes=["u_fullA"], tag="AGu1")
        if NSPLIT == 2:
            CC(lambda e: e.collective_compute(
                "AllGather", ALU.bypass, replica_groups=[list(range(NCORES))],
                ins=[g_sh[HB:2 * HB, :]], outs=[g_full_b.ap().opt()]),
               reads=gsh_toks[HITC:], writes=["g_fullB"], tag="AGg2")
            CC(lambda e: e.collective_compute(
                "AllGather", ALU.bypass, replica_groups=[list(range(NCORES))],
                ins=[u_sh[HB:2 * HB, :]], outs=[u_full_b.ap().opt()]),
               reads=ush_toks[HITC:], writes=["u_fullB"], tag="AGu2")
        dsh_toks = recon(drec, 8, None, d_sh, "d")
        CC(lambda e: e.collective_compute(
            "AllGather", ALU.bypass, replica_groups=[list(range(NCORES))],
            ins=[d_sh.ap().opt()], outs=[d_full.ap().opt()]),
           reads=dsh_toks, writes=["d_full"], tag="AGd")
        def cast_w(src_, dst, nblk, tag):
            for b_ in range(nblk):
                buf = WR[:, (b_ % 2) * CT * 128:(b_ % 2 + 1) * CT * 128]
                btok = "WR%d" % (b_ % 2)
                D("pool", btok + "c", lambda e, b_=b_, buf=buf, src_=src_:
                  e.dma_start(
                    out=buf, in_=src_[:, b_ * CT * 128:(b_ + 1) * CT * 128]),
                  writes=[btok], tag=tag)
                D("act", "cst", lambda e, b_=b_, buf=buf, dst=dst: e.dma_start(
                    out=dst[:, b_ * CT * 128:(b_ + 1) * CT * 128], in_=buf),
                  reads=[btok], writes=["%s%d" % (tag, b_)], tag=tag)

        cast_w(wrvk_f, wrvk_b, OT, "wrvkb")
        cast_w(wout_f, wout_b, CT, "woutb")
        gu_sweep(0, 1)
        down_pass(0, 1)
        for w in range(g.NLW):
            ln_pass(0, w, "2")
        rvk_sweep(0)
        D("pool", "spl", lambda e: e.dma_start(out=xr_sp[0, :, :], in_=XR[:, :]),
          reads=["XR_%d_%d" % (ct, d) for ct in range(CT) for d in range(g.NDW)],
          writes=["xr_sp0"], tag="spill0")
        x_load(1)
        for w in range(g.NLW):
            ln_pass(1, w, "1")
        gu_sweep(1, 1, side=[
            (lambda w=w: scan_window(0, w, False)) for w in range(g.NSW)])
        down_pass(1, 1)
        for w in range(g.NLW):
            ln_pass(1, w, "2")
        rvk_sweep(1, side=[
            (lambda w=w: scan_window(1, w, False)) for w in range(g.NSW)],
            start=2 * CT + 2)
        D("pool", "spl", lambda e: e.dma_start(out=xr_sp[1, :, :], in_=XR[:, :]),
          reads=["XR_%d_%d" % (ct, d) for ct in range(CT) for d in range(g.NDW)],
          writes=["xr_sp1"], tag="spill1")

        sb = sidx[0] % 2
        A("dve", lambda e, sb=sb: e.tensor_copy(GTH[:, 0:CT], sl(4 + sb * 2)),
          reads=["st%d" % sb], writes=["stpk"], tag="B")
        A("dve", lambda e, sb=sb: e.tensor_copy(GTH[:, CT:2 * CT], sl(5 + sb * 2)),
          reads=["st%d" % sb], writes=["stpk"], tag="B")
        A("dve", lambda e, sb=sb: e.tensor_copy(GTH[:, 2 * CT:3 * CT], sl(8 + sb)),
          reads=["st%d" % sb], writes=["stpk"], tag="B")
        D("pool", "stio", lambda e: e.dma_start(out=st_in[:, :], in_=GTH[:, 0:3 * CT]),
          reads=["stpk"], writes=["st_in"], tag="B")
        CC(lambda e: e.collective_compute(
            "AllGather", ALU.bypass, replica_groups=[list(range(NCORES))],
            ins=[st_in.ap().opt()], outs=[st_out.ap().opt()]),
           reads=["st_in"], writes=["st_out"], tag="AGst")
        D("sp", "gth", lambda e: e.dma_start(
            out=GTH[:, :].rearrange("p (r f) -> p r f", r=NCORES),
            in_=st_out[:, :].rearrange("(r p) f -> p r f", p=128)),
          reads=["st_out"], writes=["stpk"], tag="B")
        A("dve", lambda e: e.tensor_mul(GTH[:, :], GTH[:, :], MSKS[:, :]),
          reads=["stpk", "MSKS"], writes=["stpk"], tag="B")
        A("dve", lambda e: e.tensor_reduce(
            LNT[:, 0:3 * CT],
            GTH[:, :].rearrange("p (r f) -> p f r", f=3 * CT), AX.X, ALU.add),
          reads=["stpk"], writes=["LNT"], tag="B")
        A("dve", lambda e: e.tensor_add(LNT[:, 0:3 * CT], LNT[:, 0:3 * CT],
                                        INITA[:, :]),
          reads=["LNT", "INITA"], writes=["LNT"], tag="B")
        sidx[0] = 0
        A("dve", lambda e: e.tensor_copy(sl(4), LNT[:, 0:CT]),
          reads=["LNT"], writes=["st0"], tag="B")
        A("dve", lambda e: e.tensor_copy(sl(5), LNT[:, CT:2 * CT]),
          reads=["LNT"], writes=["st0"], tag="B")
        A("dve", lambda e: e.tensor_copy(sl(8), LNT[:, 2 * CT:3 * CT]),
          reads=["LNT"], writes=["st0"], tag="B")

        out_toks = []
        D("sp", "rld", lambda e: e.dma_start(out=XR[:, :], in_=xr_sp[0, :, :]),
          reads=["xr_sp0"],
          writes=["XR_%d_%d" % (ct, d) for ct in range(CT) for d in range(g.NDW)],
          tag="rld0")
        for w in range(g.NSW):
            scan_window(0, w, True)
            if w % 2 == 1:
                outproj(0, w // 2)
        for w in range(g.NLW):
            ln_pass(0, w, "3")
        gu_sweep(0, 2, side=[
            (lambda w=w: scan_window(1, w, True, to_scratch=True))
            for w in range(g.NSW)])
        down_pass(0, 2)
        out_toks += x_store(0)
        D("sp", "rld", lambda e: e.dma_start(out=XR[:, :], in_=xr_sp[1, :, :]),
          reads=["xr_sp1"],
          writes=["XR_%d_%d" % (ct, d) for ct in range(CT) for d in range(g.NDW)],
          tag="rld1")
        for wp in range(g.NSW // 2):
            outproj(1, wp, from_scratch=True)
        for w in range(g.NLW):
            ln_pass(1, w, "3")
        gu_sweep(1, 2)
        down_pass(1, 2)
        out_toks += x_store(1)

        A("pool", lambda e: e.engine_nop(), reads=out_toks, tag="fence")

        for ch in sorted(sch.chan_count):
            sems[ch] = EC(nc.semaphore("sem_" + ch.replace(":", "_")))
        sch.plan()

        @block.tensor
        def _(eng):
            sch.emit_engine("pe", eng, sems)

        @block.vector
        def _(eng):
            sch.emit_engine("dve", eng, sems)

        @block.scalar
        def _(eng):
            sch.emit_engine("act", eng, sems)

        @block.gpsimd
        def _(eng):
            sch.emit_engine("pool", eng, sems)

        @block.sync
        def _(eng):
            sch.emit_engine("sp", eng, sems)

    return nc, sch


def kernel(**inputs):
    g = CFG_REAL
    in_maps = host_prep(g, inputs)
    nc, _ = build_nc(g)
    res = run_bass_kernel_spmd(nc, in_maps, core_ids=list(range(NCORES)))
    out = np.empty((4, 2 * g.TOKS, g.C), np.float32)
    for core in range(NCORES):
        b, s = core // 2, core % 2
        out[b, s * g.TOKS:(s + 1) * g.TOKS] = res.results[core]["x_out"]
    return out




# version 25
# speedup vs baseline: 1.0106x; 1.0018x over previous
import numpy as np
import ml_dtypes
from contextlib import ExitStack

import concourse.bass as bass
import concourse.mybir as mybir
from concourse.bass_utils import run_bass_kernel_spmd

F32 = mybir.dt.float32
BF16 = mybir.dt.bfloat16
AF = mybir.ActivationFunctionType
ALU = mybir.AluOpType
AX = mybir.AxisListType
BF = ml_dtypes.bfloat16

NCORES = 8


class Cfg:
    def __init__(self, C, I, TOKS, HALF, LNW, SW, L, DW, GH, BIT, COMP=128.9):
        self.C = C
        self.I = I
        self.CT = C // 128
        self.IT = I // 128
        self.ITC = self.IT // NCORES
        self.RVK = 3 * C
        self.OT = self.RVK // 128
        self.TOKS = TOKS
        self.HALF = HALF
        self.NH = TOKS // HALF
        self.LNW = LNW
        self.NLW = HALF // LNW
        self.SW = SW
        self.NSW = HALF // SW
        self.L = L
        self.NI = SW // L
        self.OPW = 2 * SW
        self.DW = DW
        self.NDW = HALF // DW
        self.GH = GH
        self.NG = self.CT // GH
        self.TB = min(1024, HALF)
        self.BIT = BIT
        self.EB = min(1024, HALF)
        self.EBS = min(256, HALF)
        self.NEB = HALF // self.EB
        self.COMP = COMP
        self.RCHW = min(128, BIT * HALF // 16)
        assert SW % L == 0 and HALF % SW == 0 and HALF % LNW == 0
        assert HALF % DW == 0 and self.CT % GH == 0 and HALF % 128 == 0
        assert self.IT % (NCORES * 1) == 0 and self.IT % BIT == 0


CFG_REAL = Cfg(C=2048, I=8192, TOKS=2048, HALF=1024, LNW=256, SW=64, L=32,
               DW=512, GH=4, BIT=2)
CFG_MINI = Cfg(C=256, I=1024, TOKS=256, HALF=128, LNW=32, SW=16, L=8,
               DW=64, GH=1, BIT=2)


ENGS = ("pe", "dve", "act", "pool", "sp")
DMAQ = {"sp": "qsp", "act": "qact", "pool": "qpool"}


class Op:
    __slots__ = ("eng", "emit", "reads", "writes", "kind", "idx", "qidx",
                 "deps", "inc", "waits", "drain", "tag", "chan")

    def __init__(self, eng, emit, reads, writes, kind, tag="", chan=None):
        self.eng, self.emit, self.reads, self.writes = eng, emit, list(reads), list(writes)
        self.kind, self.tag, self.chan = kind, tag, chan
        self.idx = self.qidx = -1
        self.deps, self.waits = [], []
        self.inc = self.drain = False


class Sched:

    def __init__(self):
        self.ops = []
        self.last_w = {}
        self.readers = {}
        self.eng_ops = {e: [] for e in ENGS}
        self.chan_count = {}
        self.chan_last = {}
        self.cc_count = 0

    def add(self, eng, emit, reads=(), writes=(), kind="c", tag="", chan=None):
        op = Op(eng, emit, reads, writes, kind, tag, chan)
        if kind == "c":
            dom = eng
            op.idx = len(self.eng_ops[eng])
            me = (dom, op.idx)
        elif kind == "d":
            assert chan is not None
            dom = "q:" + chan
            op.qidx = self.chan_count.get(dom, 0)
            self.chan_count[dom] = op.qidx + 1
            op.idx = len(self.eng_ops[eng])
            me = (dom, op.qidx)
        else:
            dom = "cc%d" % self.cc_count
            op.qidx = self.cc_count
            self.cc_count += 1
            op.idx = len(self.eng_ops[eng])
            me = (dom, 0)
        self.eng_ops[eng].append(op)
        deps = set()
        for t in op.reads:
            w = self.last_w.get(t)
            if w is not None:
                deps.add(w)
        for t in op.writes:
            w = self.last_w.get(t)
            if w is not None:
                deps.add(w)
            for r in self.readers.get(t, ()):
                deps.add(r)
        if kind == "d":
            prev = self.chan_last.get(dom)
            if prev is not None:
                deps.add(prev)
            self.chan_last[dom] = me
        deps.discard(me)
        op.deps = sorted(deps)
        for t in op.reads:
            self.readers.setdefault(t, []).append(me)
        for t in op.writes:
            self.last_w[t] = me
            self.readers[t] = []
        self.ops.append((op, me))
        return op

    def plan(self):
        eff = {}
        issue = {e: {} for e in ENGS}
        run = {e: {} for e in ENGS}
        need_inc = set()
        for op, me in self.ops:
            e = op.eng
            clk = issue[e]
            if e == "pe":
                for k, v in run[e].items():
                    if clk.get(k, -1) < v:
                        clk[k] = v
                if clk.get(e, -1) < op.idx - 1:
                    clk[e] = op.idx - 1
            mywaits = []
            for dep in op.deps:
                ddom, didx = dep
                if clk.get(ddom, -1) >= didx:
                    continue
                if ddom == e:
                    op.drain = True
                    for k, v in run[e].items():
                        if clk.get(k, -1) < v:
                            clk[k] = v
                    if clk.get(e, -1) < op.idx - 1:
                        clk[e] = op.idx - 1
                    if clk.get(ddom, -1) < didx:
                        raise RuntimeError("drain cannot cover dep")
                else:
                    mywaits.append(dep)
                    need_inc.add(dep)
                    for k, v in eff.get(dep, {}).items():
                        if clk.get(k, -1) < v:
                            clk[k] = v
                    if clk.get(ddom, -1) < didx:
                        clk[ddom] = didx
            op.waits = mywaits
            myeff = dict(clk)
            myeff[me[0]] = max(myeff.get(me[0], -1), me[1])
            eff[me] = myeff
            run_eff = myeff if op.kind == "c" else clk
            re_ = run[e]
            for k, v in run_eff.items():
                if re_.get(k, -1) < v:
                    re_[k] = v
        for op, me in self.ops:
            if me in need_inc:
                op.inc = True
            if op.kind in ("d", "cc"):
                op.inc = True
        self.inc_prefix = {}
        for e in ENGS:
            cnt, pref = 0, []
            for op in self.eng_ops[e]:
                if op.kind == "c" and op.inc:
                    cnt += 1
                pref.append(cnt)
            self.inc_prefix[e] = pref

    def wait_value(self, dep):
        dom, idx = dep
        if dom in ENGS:
            return (dom, self.inc_prefix[dom][idx])
        if dom.startswith("q:"):
            return (dom, 16 * (idx + 1))
        return (dom, 1)

    def emit_engine(self, ename, eng, sems):
        waited = {}
        for op in self.eng_ops[ename]:
            if op.drain:
                eng.drain()
            for dep in op.waits:
                dom, val = self.wait_value(dep)
                if waited.get(dom, -1) >= val:
                    continue
                eng.wait_ge(sems[dom], val)
                waited[dom] = val
            ins = op.emit(eng)
            if op.kind == "c":
                if op.inc:
                    ins.then_inc(sems[ename], 1)
            elif op.kind == "d":
                ins.then_inc(sems["q:" + op.chan], 16)
            else:
                ins.then_inc(sems["cc%d" % op.qidx], 1)

    def stats(self):
        n_drain = sum(1 for o, _ in self.ops if o.drain)
        n_wait = sum(len(o.waits) for o, _ in self.ops)
        per_eng = {e: len(self.eng_ops[e]) for e in ENGS}
        return dict(total=len(self.ops), drains=n_drain, waits=n_wait, **per_eng)


def spline_grid(n_ctrl, n):
    t = np.linspace(0.0, n_ctrl - 1.0, n, dtype=np.float64)
    i = np.clip(np.floor(t).astype(np.int64), 0, n_ctrl - 2)
    f = (t - i).astype(np.float32)
    idx = np.stack([np.clip(i + k, 0, n_ctrl - 1) for k in (-1, 0, 1, 2)], 0)
    f2, f3 = f * f, f * f * f
    coeffs = np.stack([
        0.5 * (-f + 2 * f2 - f3),
        0.5 * (2 - 5 * f2 + 3 * f3),
        0.5 * (f + 4 * f2 - 3 * f3),
        0.5 * (-f2 + f3),
    ]).astype(np.float32)
    return idx, coeffs


def host_prep(cfg, inputs):
    g = cfg
    x = inputs["L0_attn_x"]
    W_rvk = inputs["L0_attn_W_rvk"].astype(np.float32)
    W_out = inputs["L0_attn_W_out"].astype(np.float32)
    td = inputs["L0_attn_time_decay"].astype(np.float32)
    tf = inputs["L0_attn_time_first"].astype(np.float32)

    n = g.I * g.C
    idx, coeffs = spline_grid(inputs["L0_ffn_gate_cp"].shape[0], n)

    def tile_gu(a):
        return a.reshape(g.IT, 128, g.CT, 128).transpose(0, 3, 2, 1)

    def tile_dn(a):
        return a.reshape(g.CT, 128, g.IT, 128).transpose(2, 3, 0, 1)

    gate_cp = inputs["L0_ffn_gate_cp"].astype(np.float32)
    up_cp = inputs["L0_ffn_up_cp"].astype(np.float32)
    down_cp = inputs["L0_ffn_down_cp"].astype(np.float32)
    n2 = g.C * g.I
    idx2, coeffs2 = idx, coeffs
    g_arr = [tile_gu(gate_cp[idx[k]]) for k in range(4)] + \
            [tile_gu(coeffs[k]) for k in range(4)]
    u_arr = [tile_gu(up_cp[idx[k]]) for k in range(4)]
    d_arr = [tile_dn(down_cp[idx2[k]]) for k in range(4)] + \
            [tile_dn(coeffs2[k]) for k in range(4)]

    def core_sl(arrs, r):
        per = np.stack(
            [a[g.ITC * r:g.ITC * (r + 1)].transpose(1, 0, 2, 3).reshape(128, -1)
             for a in arrs], axis=1)
        n_arr = per.shape[1]
        samp = per.shape[2]
        nch = samp // g.RCHW
        per = per.reshape(128, n_arr, nch, g.RCHW).transpose(0, 2, 1, 3)
        return np.ascontiguousarray(per.reshape(128, -1))

    wrvk_t = np.ascontiguousarray(
        W_rvk.reshape(g.OT, 128, g.CT, 128).transpose(3, 0, 2, 1).reshape(128, -1))
    wout_t = np.ascontiguousarray(
        W_out.reshape(g.CT, 128, g.CT, 128).transpose(3, 0, 2, 1).reshape(128, -1))
    td_r = np.ascontiguousarray(td.reshape(g.CT, 128).T)
    tf_r = np.ascontiguousarray(tf.reshape(g.CT, 128).T)
    ident = np.eye(128, dtype=np.float32)

    in_maps = []
    for core in range(NCORES):
        b, s = core // 2, core % 2
        xc = np.ascontiguousarray(
            x[b, s * g.TOKS:(s + 1) * g.TOKS].astype(np.float32))
        msk = np.zeros((128, NCORES * 3 * g.CT), np.float32)
        init_add = np.zeros((128, 3 * g.CT), np.float32)
        if s == 1:
            msk[:, (core - 1) * 3 * g.CT:core * 3 * g.CT] = 1.0
        else:
            init_add[:, 2 * g.CT:3 * g.CT] = -1e38
        in_maps.append({
            "x_in": xc,
            "grec": core_sl(g_arr, core),
            "urec": core_sl(u_arr, core),
            "drec": core_sl(d_arr, core),
            "wrvk_f": wrvk_t, "wout_f": wout_t,
            "td_r": td_r, "tf_r": tf_r, "ident": ident,
            "msk": msk, "init_add": init_add,
        })
    return in_maps


def build_nc(g):
    CT, IT, OT, L, NI, SW, LNW, HALF, DW, GH, NG, TB, BIT = (
        g.CT, g.IT, g.OT, g.L, g.NI, g.SW, g.LNW, g.HALF, g.DW, g.GH, g.NG,
        g.TB, g.BIT)
    CL = CT * L
    SAMP = g.ITC * CT * 128

    nc = bass.Bass("TRN2", target_bir_lowering=False)

    x_in = nc.dram_tensor("x_in", [g.TOKS, g.C], F32, kind="ExternalInput")
    grec = nc.dram_tensor("grec", [128, 8 * SAMP], F32, kind="ExternalInput")
    urec = nc.dram_tensor("urec", [128, 4 * SAMP], F32, kind="ExternalInput")
    drec = nc.dram_tensor("drec", [128, 8 * SAMP], F32, kind="ExternalInput")
    wrvk_f = nc.dram_tensor("wrvk_f", [128, OT * CT * 128], F32, kind="ExternalInput")
    wout_f = nc.dram_tensor("wout_f", [128, CT * CT * 128], F32, kind="ExternalInput")
    td_rD = nc.dram_tensor("td_r", [128, CT], F32, kind="ExternalInput")
    tf_rD = nc.dram_tensor("tf_r", [128, CT], F32, kind="ExternalInput")
    identD = nc.dram_tensor("ident", [128, 128], F32, kind="ExternalInput")
    mskD = nc.dram_tensor("msk", [128, NCORES * 3 * CT], F32, kind="ExternalInput")
    initD = nc.dram_tensor("init_add", [128, 3 * CT], F32, kind="ExternalInput")
    x_out = nc.dram_tensor("x_out", [g.TOKS, g.C], F32, kind="ExternalOutput")

    g_sh = nc.dram_tensor("g_sh", [g.ITC * 128, CT * 128], BF16)
    u_sh = nc.dram_tensor("u_sh", [g.ITC * 128, CT * 128], BF16)
    d_sh = nc.dram_tensor("d_sh", [g.ITC * 128, CT * 128], BF16)
    HITC = max(1, g.ITC // 2)
    NSPLIT = 2 if g.ITC >= 2 else 1
    g_full_a = nc.dram_tensor("g_full_a", [NCORES * HITC * 128, CT * 128], BF16, addr_space="Shared")
    g_full_b = nc.dram_tensor("g_full_b", [NCORES * HITC * 128, CT * 128], BF16, addr_space="Shared")
    u_full_a = nc.dram_tensor("u_full_a", [NCORES * HITC * 128, CT * 128], BF16, addr_space="Shared")
    u_full_b = nc.dram_tensor("u_full_b", [NCORES * HITC * 128, CT * 128], BF16, addr_space="Shared")
    d_full = nc.dram_tensor("d_full", [IT * 128, CT * 128], BF16, addr_space="Shared")
    wrvk_b = nc.dram_tensor("wrvk_b", [128, OT * CT * 128], BF16)
    wout_b = nc.dram_tensor("wout_b", [128, CT * CT * 128], BF16)
    inter_s = nc.dram_tensor("inter_s", [IT, 128, HALF], BF16)
    kpk = nc.dram_tensor("kpk", [g.NH, 128, g.NSW * NI * CL], F32)
    vpk = nc.dram_tensor("vpk", [g.NH, 128, g.NSW * NI * CL], F32)
    tpk = nc.dram_tensor("tpk", [g.NH, 128, g.NSW * NI * CL], BF16)
    xr_sp = nc.dram_tensor("xr_sp", [g.NH, 128, CT * HALF], F32)
    ysk = nc.dram_tensor("ysk", [128, g.NSW * CT * SW], BF16)
    st_in = nc.dram_tensor("st_in", [128, 3 * CT], F32)
    st_out = nc.dram_tensor("st_out", [NCORES * 128, 3 * CT], F32)

    sch = Sched()

    with ExitStack() as _ctx:
        EC = _ctx.enter_context
        XR = EC(nc.sbuf_tensor("XR", [128, CT * HALF], F32))
        XN = EC(nc.sbuf_tensor("XN", [128, CT * HALF], BF16))
        XSQ = EC(nc.sbuf_tensor("XSQ", [128, 2 * LNW], BF16))
        KP = EC(nc.sbuf_tensor("KP", [128, NI * CL], F32))
        VP = EC(nc.sbuf_tensor("VP", [128, NI * CL], F32))
        TP = EC(nc.sbuf_tensor("TP", [128, NI * CL], BF16))
        YS = EC(nc.sbuf_tensor("YS", [128, 2 * CT * g.OPW], BF16))
        IBST = EC(nc.sbuf_tensor("IBST", [128, 2 * BIT * HALF], BF16))
        IBO = EC(nc.sbuf_tensor("IBO", [128, 2 * HALF], BF16))
        SG = EC(nc.sbuf_tensor("SG", [128, TB], F32))
        WG = EC(nc.sbuf_tensor("WG", [128, 2 * CT * 128], BF16))
        WU = EC(nc.sbuf_tensor("WU", [128, 2 * CT * 128], BF16))
        WD = EC(nc.sbuf_tensor("WD", [128, 2 * BIT * GH * 128], BF16))
        WR = EC(nc.sbuf_tensor("WR", [128, 2 * CT * 128], BF16))
        KVST = EC(nc.sbuf_tensor("KVST", [128, 2 * g.EBS], F32))
        RST = EC(nc.sbuf_tensor("RST", [128, 2 * g.EBS], BF16))
        EZK = EC(nc.sbuf_tensor("EZK", [128, NI * 2 * CL], F32))
        PQH = EC(nc.sbuf_tensor("PQH", [128, 2 * 2 * CL], F32))
        E3W = EC(nc.sbuf_tensor("E3W", [128, NI * CL], F32))
        TMPS = EC(nc.sbuf_tensor("TMPS", [128, CL], F32))
        SSM = EC(nc.sbuf_tensor("SSM", [128, 28 * CT], F32))
        LNT = EC(nc.sbuf_tensor("LNT", [128, LNW], F32))
        CEWZ = EC(nc.sbuf_tensor("CEWZ", [128, 2 * CL], F32))
        CEU = EC(nc.sbuf_tensor("CEU", [128, CL], F32))
        ONESL = EC(nc.sbuf_tensor("ONESL", [128, L], F32))
        ONESBF = EC(nc.sbuf_tensor("ONESBF", [128, 1], BF16))
        ONER = EC(nc.sbuf_tensor("ONER", [1, 128], F32))
        IDENT = EC(nc.sbuf_tensor("IDENT", [128, 128], F32))
        ROWS = EC(nc.sbuf_tensor("ROWS", [1, 2 * LNW], F32))
        EPSC = EC(nc.sbuf_tensor("EPSC", [1, 1], F32))
        LHC = EC(nc.sbuf_tensor("LHC", [128, 1], F32))
        GTH = EC(nc.sbuf_tensor("GTH", [128, NCORES * 3 * CT], F32))
        MSKS = EC(nc.sbuf_tensor("MSKS", [128, NCORES * 3 * CT], F32))
        INITA = EC(nc.sbuf_tensor("INITA", [128, 3 * CT], F32))
        XST = EC(nc.sbuf_tensor("XST", [128, g.C], F32))
        pA = EC(nc.psum_tensor("pA", [128, 1024], F32))
        pB = EC(nc.psum_tensor("pB", [128, 1024], F32))
        pC = EC(nc.psum_tensor("pC", [128, 1024], F32))
        pD = EC(nc.psum_tensor("pD", [128, 1024], F32))
        s_pe = EC(nc.semaphore("s_pe"))
        s_dve = EC(nc.semaphore("s_dve"))
        s_act = EC(nc.semaphore("s_act"))
        s_pool = EC(nc.semaphore("s_pool"))
        s_sp = EC(nc.semaphore("s_sp"))
        s_cc0 = EC(nc.semaphore("s_cc0"))
        s_cc1 = EC(nc.semaphore("s_cc1"))
        s_cc2 = EC(nc.semaphore("s_cc2"))
        s_cc3 = EC(nc.semaphore("s_cc3"))
        s_cc4 = EC(nc.semaphore("s_cc4"))
        s_cc5 = EC(nc.semaphore("s_cc5"))
        s_cc6 = EC(nc.semaphore("s_cc6"))
        s_cc7 = EC(nc.semaphore("s_cc7"))
        block = EC(nc.Block())
        def ptoks(base, width):
            n = max(1, (width + 511) // 512)
            return [PB(base + i) for i in range(n)]

        sems = {"pe": s_pe, "dve": s_dve, "act": s_act, "pool": s_pool,
                "sp": s_sp,
                "cc0": s_cc0, "cc1": s_cc1, "cc2": s_cc2, "cc3": s_cc3,
                "cc4": s_cc4, "cc5": s_cc5, "cc6": s_cc6, "cc7": s_cc7}

        def A(eng, fn, reads=(), writes=(), tag=""):
            sch.add(eng, fn, reads, writes, "c", tag)

        def D(eng, chan, fn, reads=(), writes=(), tag=""):
            sch.add(eng, fn, reads, writes, "d", tag, chan=chan)

        def CC(fn, reads=(), writes=(), tag=""):
            sch.add("pool", fn, reads, writes, "cc", tag)

        PB = lambda i: "pb%d" % i

        def sl(i):
            return SSM[:, i * CT:(i + 1) * CT]


        D("sp", "ini", lambda e: e.dma_start(out=IDENT[:, :], in_=identD[:, :]),
          writes=["IDENT"])
        D("sp", "ini", lambda e: e.dma_start(out=MSKS[:, :], in_=mskD[:, :]),
          writes=["MSKS"])
        D("sp", "ini", lambda e: e.dma_start(out=INITA[:, :], in_=initD[:, :]),
          writes=["INITA"])
        D("sp", "ini", lambda e: e.dma_start(out=sl(20), in_=td_rD[:, :]), writes=["ssm20"])
        D("sp", "ini", lambda e: e.dma_start(out=sl(21), in_=tf_rD[:, :]), writes=["ssm21"])
        A("pool", lambda e: e.memset(sl(4), 0.0), writes=["st0"])
        A("pool", lambda e: e.memset(sl(5), 0.0), writes=["st0"])
        A("pool", lambda e: e.memset(sl(8), -1e38), writes=["st0"])
        A("pool", lambda e: e.memset(sl(6), 0.0), writes=["st1"])
        A("pool", lambda e: e.memset(sl(7), 0.0), writes=["st1"])
        A("pool", lambda e: e.memset(sl(9), -1e38), writes=["st1"])
        A("pool", lambda e: e.memset(ONESL[:, :], 1.0), writes=["ONESL"])
        A("pool", lambda e: e.memset(EPSC[:, :], 1e-5), writes=["EPSC"])
        A("pool", lambda e: e.memset(LHC[:, :], float(np.log(0.5))), writes=["LHC"])
        A("pool", lambda e: e.memset(ONESBF[:, :], 1.0), writes=["ONESBF"])
        A("pool", lambda e: e.memset(ONER[:, :], 1.0), writes=["ONER"])
        A("act", lambda e: e.activation(sl(18), sl(20), AF.Exp),
          reads=["ssm20"], writes=["ssm18"])
        A("dve", lambda e: e.tensor_scalar_mul(sl(0), sl(18), -1.0),
          reads=["ssm18"], writes=["ssm0"])
        A("act", lambda e: e.activation(sl(2), sl(0), AF.Exp),
          reads=["ssm0"], writes=["ssm2"])
        A("dve", lambda e: e.tensor_scalar_mul(sl(1), sl(0), float(L)),
          reads=["ssm0"], writes=["ssm1"])
        A("act", lambda e: e.activation(sl(3), sl(21), AF.Exp),
          reads=["ssm21"], writes=["ssm3"])
        for ct in range(CT):
            A("dve", lambda e, ct=ct: e.tensor_scalar_mul(
                CEWZ[:, ct * L:(ct + 1) * L], ONESL[:, :], sl(2)[:, ct:ct + 1]),
              reads=["ONESL", "ssm2"], writes=["CEWZ"])
            A("pool", lambda e, ct=ct: e.tensor_scalar_mul(
                CEU[:, ct * L:(ct + 1) * L], ONESL[:, :], sl(3)[:, ct:ct + 1]),
              reads=["ONESL", "ssm3"], writes=["CEU"])
        A("dve", lambda e: e.tensor_copy(CEWZ[:, CL:2 * CL], CEWZ[:, 0:CL]),
          reads=["CEWZ"], writes=["CEWZ"])
        A("dve", lambda e: e.memset(
            CEWZ[:, :].rearrange("p (c l) -> p c l", l=L)[:, :, 0:1], 0.0),
          writes=["CEWZ"])

        RCHW = g.RCHW

        _rcnt = [0]

        def recon(rec_dram, n_own, coef_src, sh_dram, wtag, lo, hi):
            ibf = IBST[:, :].bitcast(F32)
            wgf = WG[:, :].bitcast(F32)
            bufs = [ibf[:, 0:8 * RCHW], ibf[:, 8 * RCHW:16 * RCHW],
                    wgf[:, 0:8 * RCHW], wgf[:, 8 * RCHW:16 * RCHW]]
            queues = ["act", "sp", "act", "sp"]
            perit = CT * 128
            nper = perit // RCHW
            for s in range(lo * nper, hi * nper):
                gs = _rcnt[0]
                _rcnt[0] += 1
                bi = gs % 4
                buf = bufs[bi]
                btok = "rs%d" % bi
                q = queues[bi]

                def ld(e, s=s, buf=buf, rec=rec_dram, n_own=n_own):
                    return e.dma_start(
                        out=buf[:, 0:n_own * RCHW],
                        in_=rec[:, s * n_own * RCHW:(s + 1) * n_own * RCHW])
                D(q, btok, ld, writes=[btok], tag=wtag + "ld")
                if coef_src is not None:
                    def ldc(e, s=s, buf=buf, cd=coef_src):
                        return e.dma_start(
                            out=buf[:, 4 * RCHW:8 * RCHW],
                            in_=cd[:, s * 8 * RCHW + 4 * RCHW:
                                   (s + 1) * 8 * RCHW])
                    D(q, btok, ldc, writes=[btok], tag=wtag + "ldc")
                it_loc = s // nper
                coff = (s % nper) * RCHW
                wo = WD[:, coff:coff + RCHW]
                par = gs % 3
                ceng = "pool" if par == 2 else "dve"
                wtok = "rw%d" % par
                t0 = PQH[:, par * RCHW:(par + 1) * RCHW]
                t1 = EZK[:, par * RCHW:(par + 1) * RCHW]
                rt0, rt1 = "rt0_%d" % par, "rt1_%d" % par
                a_ = lambda i, buf=buf: buf[:, i * RCHW:(i + 1) * RCHW]
                A(ceng, lambda e, a=a_, t0=t0: e.tensor_mul(t0, a(0), a(4)),
                  reads=[btok], writes=[rt0], tag=wtag)
                A(ceng, lambda e, a=a_, t1=t1: e.tensor_mul(t1, a(1), a(5)),
                  reads=[btok], writes=[rt1], tag=wtag)
                A(ceng, lambda e, t0=t0, t1=t1: e.tensor_add(t0, t0, t1),
                  reads=[rt0, rt1], writes=[rt0], tag=wtag)
                A(ceng, lambda e, a=a_, t1=t1: e.tensor_mul(t1, a(2), a(6)),
                  reads=[btok, rt0], writes=[rt1], tag=wtag)
                A(ceng, lambda e, t0=t0, t1=t1: e.tensor_add(t0, t0, t1),
                  reads=[rt0, rt1], writes=[rt0], tag=wtag)
                A(ceng, lambda e, a=a_, t1=t1: e.tensor_mul(t1, a(3), a(7)),
                  reads=[btok, rt0], writes=[rt1], tag=wtag)
                A(ceng, lambda e, wo=wo, t0=t0, t1=t1: e.tensor_add(wo, t0, t1),
                  reads=[rt0, rt1], writes=[wtok], tag=wtag)
                if coff + RCHW == perit:
                    def stw(e, it_loc=it_loc, sh=sh_dram):
                        return e.dma_start(
                            out=sh[it_loc * 128:(it_loc + 1) * 128, :],
                            in_=WD[:, 0:perit])
                    D("sp", "sst0", stw, reads=["rw0", "rw1", "rw2"],
                      writes=["%s_sh%d" % (wtag, it_loc)], tag=wtag + "st")
            return ["%s_sh%d" % (wtag, i) for i in range(lo, hi)]


        def xr_sl(ct, t0, n):
            return XR[:, ct * HALF + t0:ct * HALF + t0 + n]

        def xr_tok(ct, t0):
            return "XR_%d_%d" % (ct, t0 // DW)

        def xr_toks_range(ct, t0, n):
            return ["XR_%d_%d" % (ct, d) for d in
                    range(t0 // DW, (t0 + n - 1) // DW + 1)]

        def xn_sl(ct, t0, n):
            return XN[:, ct * HALF + t0:ct * HALF + t0 + n]

        def xn_tok(ct, w):
            return "XN_%d_%d" % (ct, w)

        def xn_toks_range(ct, t0, n):
            return [xn_tok(ct, w) for w in range(t0 // LNW, (t0 + n - 1) // LNW + 1)]

        def ln_pass(h, w, pid):
            t0 = w * LNW
            px = "ln%s_h%d_w%d" % (pid, h, w)
            rd = lambda i: ROWS[0:1, i * LNW:(i + 1) * LNW]
            for ct in range(CT):
                A("dve", lambda e, ct=ct: e.tensor_copy(
                    xn_sl(ct, t0, LNW), xr_sl(ct, t0, LNW)),
                  reads=[xr_tok(ct, t0)], writes=[xn_tok(ct, w)], tag=px)
            for ct in range(CT):
                xq = XSQ[:, (ct % 2) * LNW:(ct % 2 + 1) * LNW]
                xqt = "XSQ%d" % (ct % 2)
                A("pool", lambda e, ct=ct, xq=xq: e.tensor_mul(
                    xq, xn_sl(ct, t0, LNW), xn_sl(ct, t0, LNW)),
                  reads=[xn_tok(ct, w)], writes=[xqt], tag=px)
                A("pe", lambda e, ct=ct: e.matmul(
                    pB[0:1, 0:LNW], ONESBF[:, :], xn_sl(ct, t0, LNW),
                    start=(ct == 0), stop=(ct == CT - 1)),
                  reads=[xn_tok(ct, w), "ONESBF"], writes=[PB(2)], tag=px)
                A("pe", lambda e, ct=ct, xq=xq: e.matmul(
                    pB[0:1, 512:512 + LNW], ONESBF[:, :], xq,
                    start=(ct == 0), stop=(ct == CT - 1)),
                  reads=[xqt, "ONESBF"], writes=[PB(3)], tag=px)
            A("dve", lambda e: e.tensor_scalar_mul(rd(0), pB[0:1, 0:LNW], 1.0 / g.C),
              reads=[PB(2)], writes=["row0"], tag=px)
            A("dve", lambda e: e.tensor_scalar_mul(rd(1), pB[0:1, 512:512 + LNW], 1.0 / g.C),
              reads=[PB(3)], writes=["row1"], tag=px)
            A("dve", lambda e: e.tensor_mul(LNT[0:1, :], rd(0), rd(0)),
              reads=["row0"], writes=["LNT"], tag=px)
            A("dve", lambda e: e.tensor_sub(rd(1), rd(1), LNT[0:1, :]),
              reads=["row1", "LNT"], writes=["row1"], tag=px)
            A("act", lambda e: e.activation(rd(1), rd(1), AF.Ln, bias=EPSC[:, 0:1]),
              reads=["row1", "EPSC"], writes=["row1"], tag=px)
            A("act", lambda e: e.activation(rd(1), rd(1), AF.Exp, scale=-0.5),
              reads=["row1"], writes=["row1"], tag=px)
            A("dve", lambda e: e.tensor_mul(rd(0), rd(0), rd(1)),
              reads=["row0", "row1"], writes=["row0"], tag=px)
            A("pe", lambda e: e.matmul(pB[:, LNW:2 * LNW], ONER[:, :], rd(1),
                                       start=True, stop=True),
              reads=["row1", "ONER"], writes=[PB(2)], tag=px)
            A("pe", lambda e: e.matmul(pB[:, 512 + LNW:512 + 2 * LNW], ONER[:, :], rd(0),
                                       start=True, stop=True),
              reads=["row0", "ONER"], writes=[PB(3)], tag=px)
            for ct in range(CT):
                A("dve", lambda e, ct=ct: e.tensor_mul(
                    LNT[:, :], xr_sl(ct, t0, LNW), pB[:, LNW:2 * LNW]),
                  reads=[xr_tok(ct, t0), PB(2)], writes=["LNT"], tag=px)
                A("dve", lambda e, ct=ct: e.tensor_sub(
                    xn_sl(ct, t0, LNW), LNT[:, :], pB[:, 512 + LNW:512 + 2 * LNW]),
                  reads=["LNT", PB(3)], writes=[xn_tok(ct, w)], tag=px)

        GU_ORDER = [r * g.ITC + ph * HITC + l
                    for ph in range(NSPLIT) for r in range(NCORES)
                    for l in range(HITC)]
        assert sorted(GU_ORDER) == list(range(IT))

        def gu_src(it):
            r, l = it // g.ITC, it % g.ITC
            ph = l // HITC
            row = (r * HITC + (l % HITC)) * 128
            return ph, row

        def gu_sweep_gen(h, call):
            px = "gu%d_h%d" % (call, h)
            NTB = HALF // TB
            for ii, it in enumerate(GU_ORDER):
                yield
                gb = WG[:, (ii % 2) * CT * 128:(ii % 2 + 1) * CT * 128]
                ub = WU[:, (ii % 2) * CT * 128:(ii % 2 + 1) * CT * 128]
                gbt, ubt = "WG%d" % (ii % 2), "WU%d" % (ii % 2)
                ph, row = gu_src(it)
                gsrc = g_full_a if ph == 0 else g_full_b
                usrc = u_full_a if ph == 0 else u_full_b
                gtok = "g_fullA" if ph == 0 else "g_fullB"
                utok = "u_fullA" if ph == 0 else "u_fullB"
                D("sp", gbt, lambda e, row=row, gb=gb, gsrc=gsrc: e.dma_start(
                    out=gb, in_=gsrc[row:row + 128, :]),
                  reads=[gtok], writes=[gbt], tag=px + "ld")
                D("sp", ubt, lambda e, row=row, ub=ub, usrc=usrc: e.dma_start(
                    out=ub, in_=usrc[row:row + 128, :]),
                  reads=[utok], writes=[ubt], tag=px + "ld")
                gp = pA if ii % 2 == 0 else pC
                up = pB if ii % 2 == 0 else pD
                gpt = ptoks(0 if ii % 2 == 0 else 4, TB)
                upt = ptoks(2 if ii % 2 == 0 else 6, TB)
                NB = (TB + 511) // 512
                BW = min(512, TB)
                for tb in range(NTB):
                    for ht in range(CT):
                        xr_ = xn_toks_range(ht, tb * TB, TB)
                        for nb in range(NB):
                            A("pe", lambda e, ht=ht, tb=tb, nb=nb, gp=gp, gb=gb:
                              e.matmul(
                                gp[:, nb * BW:(nb + 1) * BW],
                                gb[:, ht * 128:(ht + 1) * 128],
                                XN[:, ht * HALF + tb * TB + nb * BW:
                                   ht * HALF + tb * TB + (nb + 1) * BW],
                                start=(ht == 0), stop=(ht == CT - 1)),
                              reads=[gbt] + xr_, writes=gpt, tag=px)
                    for ht in range(CT):
                        xr_ = xn_toks_range(ht, tb * TB, TB)
                        for nb in range(NB):
                            A("pe", lambda e, ht=ht, tb=tb, nb=nb, up=up, ub=ub:
                              e.matmul(
                                up[:, nb * BW:(nb + 1) * BW],
                                ub[:, ht * 128:(ht + 1) * 128],
                                XN[:, ht * HALF + tb * TB + nb * BW:
                                   ht * HALF + tb * TB + (nb + 1) * BW],
                                start=(ht == 0), stop=(ht == CT - 1)),
                              reads=[ubt] + xr_, writes=upt, tag=px)
                    ib = IBO[:, (ii % 2) * HALF + tb * TB:(ii % 2) * HALF + (tb + 1) * TB]
                    A("act", lambda e, gp=gp: e.activation(SG[:, 0:TB], gp[:, 0:TB], AF.Silu),
                      reads=gpt, writes=["SG"], tag=px)
                    A("dve", lambda e, up=up, ib=ib: e.tensor_mul(ib, SG[:, 0:TB], up[:, 0:TB]),
                      reads=["SG"] + upt, writes=["IBO%d" % (it % 2)], tag=px)
                D("pool", "int", lambda e, it=it: e.dma_start(
                    out=inter_s[it, :, :],
                    in_=IBO[:, (it % 2) * HALF:(it % 2 + 1) * HALF]),
                  reads=["IBO%d" % (it % 2)], writes=["int_%d" % it], tag=px + "st")

        def down_pass(h, call):
            px = "dn%d_h%d" % (call, h)
            for gr in range(NG):
                if GH == 4:
                    pmap4 = [pA, pB, pC, pD]
                    pbase = 0
                else:
                    pmap4 = ([pA, pB] if gr % 2 == 0 else [pC, pD]) * 2
                    pbase = 0 if gr % 2 == 0 else 4
                for itb in range(IT // BIT):
                    wdb = WD[:, (itb % 2) * BIT * GH * 128:
                             (itb % 2 + 1) * BIT * GH * 128]
                    wdt = "WD%d" % (itb % 2)
                    D("sp", wdt, lambda e, itb=itb, gr=gr, wdb=wdb: e.dma_start(
                        out=wdb.rearrange("p (i x) -> p i x", i=BIT),
                        in_=d_full[itb * BIT * 128:(itb + 1) * BIT * 128,
                                   gr * GH * 128:(gr + 1) * GH * 128]
                        .rearrange("(i p) x -> p i x", p=128)),
                      reads=["d_full"], writes=[wdt], tag=px + "ld")
                    ib4 = IBST[:, (itb % 2) * BIT * HALF:(itb % 2 + 1) * BIT * HALF]
                    ibt = "IBST%d" % (itb % 2)
                    D("sp", ibt, lambda e, itb=itb, ib4=ib4: e.dma_start(
                        out=ib4.rearrange("p (i x) -> p i x", i=BIT),
                        in_=inter_s[itb * BIT:(itb + 1) * BIT, :, :]
                        .rearrange("i p x -> p i x")),
                      reads=["int_%d" % (itb * BIT + ii) for ii in range(BIT)],
                      writes=[ibt], tag=px + "ldi")
                    for ii in range(BIT):
                        it = itb * BIT + ii
                        for hh in range(GH):
                            for wd in range(g.NDW):
                                dtile = pmap4[hh % 4 if GH == 4 else hh % 2][
                                    :, (wd % 2) * 512:(wd % 2) * 512 + DW]
                                A("pe", lambda e, ii=ii, hh=hh, it=it, wd=wd,
                                    wdb=wdb, ib4=ib4, dtile=dtile: e.matmul(
                                    dtile,
                                    wdb[:, (ii * GH + hh) * 128:
                                        (ii * GH + hh + 1) * 128],
                                    ib4[:, ii * HALF + wd * DW:
                                        ii * HALF + (wd + 1) * DW],
                                    start=(it == 0), stop=(it == IT - 1)),
                                  reads=[wdt, ibt],
                                  writes=[PB((2 * hh + (wd % 2)) if GH == 4 else
                                             (pbase + 2 * (hh % 2) + (wd % 2)))],
                                  tag=px)
                for hh in range(GH):
                    for wd in range(g.NDW):
                        ht = gr * GH + hh
                        dtile = pmap4[hh % 4 if GH == 4 else hh % 2][
                            :, (wd % 2) * 512:(wd % 2) * 512 + DW]
                        A("dve", lambda e, ht=ht, wd=wd, dtile=dtile:
                          e.scalar_tensor_tensor(
                            xr_sl(ht, wd * DW, DW), dtile, 0.5,
                            xr_sl(ht, wd * DW, DW), ALU.mult, ALU.add),
                          reads=[PB((2 * hh + (wd % 2)) if GH == 4 else
                                    (pbase + 2 * (hh % 2) + (wd % 2))),
                                 xr_tok(ht, wd * DW)],
                          writes=[xr_tok(ht, wd * DW)], tag=px + "ev")

        def rvk_gen(h):
            px = "rvk_h%d" % h
            EB, EBS = g.EB, g.EBS
            NSUB = EB // EBS
            ot_order = ([2 * CT + i for i in range(CT)] +
                        [CT + i for i in range(CT)] + list(range(CT)))
            for oi, ot in enumerate(ot_order):
                yield
                wb = WR[:, (oi % 2) * CT * 128:(oi % 2 + 1) * CT * 128]
                wbt = "WR%d" % (oi % 2)
                D("sp", wbt, lambda e, ot=ot, wb=wb: e.dma_start(
                    out=wb, in_=wrvk_b[:, ot * CT * 128:(ot + 1) * CT * 128]),
                  reads=["wrvkb%d" % ot], writes=[wbt], tag=px + "ld")
                pp = [pA, pB, pC, pD][oi % 4]
                ppt = ptoks([0, 2, 4, 6][oi % 4], EB)
                kind = ot // CT
                ct = ot % CT
                NBE = (EB + 511) // 512
                BWE = min(512, EB)
                for ht in range(CT):
                    xr_ = xn_toks_range(ht, 0, HALF)
                    for nb in range(NBE):
                        A("pe", lambda e, ht=ht, nb=nb, pp=pp, wb=wb: e.matmul(
                            pp[:, nb * BWE:(nb + 1) * BWE],
                            wb[:, ht * 128:(ht + 1) * 128],
                            XN[:, ht * HALF + nb * BWE:ht * HALF + (nb + 1) * BWE],
                            start=(ht == 0), stop=(ht == CT - 1)),
                          reads=[wbt] + xr_, writes=ppt, tag=px)
                for sub in range(NSUB):
                    bsl = sub % 2
                    if kind == 0:
                        stg = RST[:, bsl * EBS:(bsl + 1) * EBS]
                        stok = "RST%d" % bsl
                        A("act", lambda e, stg=stg, pp=pp, sub=sub: e.activation(
                            stg, pp[:, sub * EBS:(sub + 1) * EBS], AF.Tanh,
                            scale=0.5),
                          reads=ppt, writes=[stok], tag=px)
                        dstp = tpk
                    else:
                        stg = KVST[:, bsl * EBS:(bsl + 1) * EBS]
                        stok = "KVST%d" % bsl
                        A("act", lambda e, stg=stg, pp=pp, sub=sub: e.copy(
                            stg, pp[:, sub * EBS:(sub + 1) * EBS]),
                          reads=ppt, writes=[stok], tag=px)
                        dstp = vpk if kind == 1 else kpk
                    wpersb = EBS // SW

                    def stp(e, stg=stg, ot=ot, ct=ct, h=h, sub=sub, dstp=dstp,
                            wpersb=wpersb):
                        dst = dstp[h, :, :].rearrange(
                            "p (wj c l) -> p wj c l", c=CT, l=L)[
                            :, sub * wpersb * NI:(sub + 1) * wpersb * NI, ct, :]
                        return e.dma_start(
                            out=dst,
                            in_=stg.rearrange("p (wj l) -> p wj l", l=L))
                    D("pool", "pk", stp, reads=[stok],
                      writes=["%s_h%d_%d_%d" % (("tpk", "vpk", "kpk")[kind], h, ct, sub)],
                      tag=px + "st")

        def rvk_sweep(h, side=(), start=0):
            gen = rvk_gen(h)
            i = 0
            side = list(side)
            si = 0
            n_after = OT - start
            every = max(1, n_after // max(1, len(side))) if side else 1
            for _ in gen:
                i += 1
                if side and i > start and (i - start) % every == 0 and si < len(side):
                    side[si]()
                    si += 1
            while si < len(side):
                side[si]()
                si += 1

        def pack_toks(pfx, h):
            return ["%s_h%d_%d_%d" % (pfx, h, ct, sub)
                    for ct in range(CT) for sub in range(g.EB // g.EBS)]

        sidx = [0]

        def kv_load(h, w, full):
            NICL = NI * CL
            D("pool", "KP", lambda e, h=h, w=w: e.dma_start(
                out=KP[:, :], in_=kpk[h, :, w * NICL:(w + 1) * NICL]),
              reads=pack_toks("kpk", h), writes=["KP"], tag="kvld")
            D("pool", "VP", lambda e, h=h, w=w: e.dma_start(
                out=VP[:, :], in_=vpk[h, :, w * NICL:(w + 1) * NICL]),
              reads=pack_toks("vpk", h), writes=["VP"], tag="kvld")
            if full:
                D("pool", "TP", lambda e, h=h, w=w: e.dma_start(
                    out=TP[:, :], in_=tpk[h, :, w * NICL:(w + 1) * NICL]),
                  reads=pack_toks("tpk", h), writes=["TP"], tag="kvld")

        def scan_window(h, w, full, to_scratch=False):
            kv_load(h, w, full)
            px = "sc%s_h%d_w%d" % ("F" if full else "S", h, w)
            pars = [(sidx[0] + j) % 2 for j in range(NI)]
            ezv_ = lambda j: EZK[:, j * 2 * CL:j * 2 * CL + CL]
            ekk_ = lambda j: EZK[:, j * 2 * CL + CL:(j + 1) * 2 * CL]
            ph_ = lambda j: PQH[:, pars[j] * 2 * CL:pars[j] * 2 * CL + CL]
            qh_ = lambda j: PQH[:, pars[j] * 2 * CL + CL:(pars[j] + 1) * 2 * CL]
            e3_ = lambda j: E3W[:, j * CL:(j + 1) * CL]
            lam_sl = [13, 18]
            r3 = lambda ap: ap.rearrange("p (c l) -> p c l", l=L)
            bco = lambda ap: ap.rearrange("p (c o) -> p c o", o=1)
            for j in range(NI):
                kj = KP[:, j * CL:(j + 1) * CL]
                A("dve", lambda e, j=j, kj=kj: e.reduce_max(
                    sl(10 + j), r3(kj), axis=AX.X),
                  reads=["KP"], writes=["cmk%d" % j], tag=px)
            for j in range(NI):
                sbp, nbp = pars[j], 1 - pars[j]
                Mp, Mn = sl(8 + sbp), sl(8 + nbp)
                A("pool", lambda e, Mp=Mp: e.tensor_add(sl(12), Mp, sl(1)),
                  reads=["st%d" % sbp, "ssm1"], writes=["ssm12"], tag=px)
                A("dve", lambda e, Mn=Mn, j=j: e.tensor_max(Mn, sl(12), sl(10 + j)),
                  reads=["ssm12", "cmk%d" % j], writes=["st%d" % nbp], tag=px)
                A("dve", lambda e, Mp=Mp, Mn=Mn, j=j: e.tensor_sub(
                    sl(lam_sl[j]), Mp, Mn),
                  reads=["st%d" % sbp, "st%d" % nbp], writes=["lam%d" % j], tag=px)
                A("act", lambda e, j=j: e.activation(
                    sl(lam_sl[j]), sl(lam_sl[j]), AF.Exp),
                  reads=["lam%d" % j], writes=["lam%d" % j], tag=px)
            for j in range(NI):
                kj = KP[:, j * CL:(j + 1) * CL]
                Mn = sl(8 + (1 - pars[j]))
                A("dve", lambda e, j=j, kj=kj, Mn=Mn: e.tensor_tensor(
                    r3(ekk_(j)), r3(kj),
                    bco(Mn).broadcast_to([128, CT, L]), ALU.subtract),
                  reads=["KP", "st%d" % (1 - pars[j])], writes=["ekk%d" % j],
                  tag=px)
            eks = EZK[:, :].rearrange("p (j x) -> p j x", j=NI)[:, :, CL:2 * CL]
            A("act", lambda e: e.activation(eks, eks, AF.Exp),
              reads=["ekk%d" % j2 for j2 in range(NI)],
              writes=["ekk%d" % j2 for j2 in range(NI)], tag=px)
            ezs = EZK[:, :].rearrange("p (j x) -> p j x", j=NI)[:, :, 0:CL]
            vps = VP[:, :].rearrange("p (j x) -> p j x", j=NI)
            A("dve", lambda e: e.tensor_tensor(ezs, eks, vps, ALU.mult),
              reads=["ekk%d" % j2 for j2 in range(NI)] + ["VP"],
              writes=["ezv%d" % j2 for j2 in range(NI)], tag=px)
            if full:
                e3s = E3W[:, :].rearrange("p (j x) -> p j x", j=NI)
                ceub = CEU[:, :].rearrange("p (o x) -> p o x", o=1)                     .broadcast_to([128, NI, CL])
                A("pool", lambda e: e.tensor_tensor(e3s, eks, ceub, ALU.mult),
                  reads=["ekk%d" % j2 for j2 in range(NI)] + ["CEU"],
                  writes=["e3w"], tag=px)
            for j in range(NI):
                sbp, nbp = pars[j], 1 - pars[j]
                Pp, Qp = sl(4 + sbp * 2), sl(5 + sbp * 2)
                Pn, Qn = sl(4 + nbp * 2), sl(5 + nbp * 2)
                stok, ntok = "st%d" % sbp, "st%d" % nbp
                A("dve", lambda e, j=j, Pp=Pp: e.tensor_mul(sl(14), Pp, sl(lam_sl[j])),
                  reads=[stok, "lam%d" % j], writes=["ssm14"], tag=px)
                A("dve", lambda e, j=j, Qp=Qp: e.tensor_mul(sl(15), Qp, sl(lam_sl[j])),
                  reads=[stok, "lam%d" % j], writes=["ssm15"], tag=px)
                A("dve", lambda e: e.tensor_mul(sl(16), sl(14), sl(2)),
                  reads=["ssm14", "ssm2"], writes=["ssm16"], tag=px)
                A("dve", lambda e: e.tensor_mul(sl(17), sl(15), sl(2)),
                  reads=["ssm15", "ssm2"], writes=["ssm17"], tag=px)
                ez3, ek3 = r3(ezv_(j)), r3(ekk_(j))
                A("dve", lambda e, ez3=ez3: e.tensor_add(
                    ez3[:, :, 0:1], ez3[:, :, 0:1], bco(sl(16))),
                  reads=["ezv%d" % j, "ssm16"], writes=["ezv%d" % j], tag=px)
                A("dve", lambda e, ek3=ek3: e.tensor_add(
                    ek3[:, :, 0:1], ek3[:, :, 0:1], bco(sl(17))),
                  reads=["ekk%d" % j, "ssm17"], writes=["ekk%d" % j], tag=px)
                A("dve", lambda e, j=j: e.tensor_tensor_scan(
                    PQH[:, pars[j] * 2 * CL:(pars[j] + 1) * 2 * CL],
                    CEWZ[:, :], EZK[:, j * 2 * CL:(j + 1) * 2 * CL], 0.0,
                    ALU.mult, ALU.add),
                  reads=["CEWZ", "ezv%d" % j, "ekk%d" % j],
                  writes=["pqh%d" % pars[j]], tag=px)
                A("dve", lambda e, j=j, Pn=Pn: e.tensor_copy(
                    bco(Pn), r3(ph_(j))[:, :, L - 1:L]),
                  reads=["pqh%d" % pars[j]], writes=[ntok], tag=px)
                A("pool", lambda e, j=j, Qn=Qn: e.tensor_copy(
                    bco(Qn), r3(qh_(j))[:, :, L - 1:L]),
                  reads=["pqh%d" % pars[j]], writes=[ntok], tag=px)
                if full:
                    A("pool", lambda e, j=j: e.tensor_copy(sl(22 + j), sl(14)),
                      reads=["ssm14"], writes=["ip%d" % j], tag=px)
                    A("pool", lambda e, j=j: e.tensor_copy(sl(24 + j), sl(15)),
                      reads=["ssm15"], writes=["iq%d" % j], tag=px)
                sidx[0] += 1
            if not full:
                return
            b = (w // 2) % 2
            for j in range(NI):
                PSH, QSH = ezv_(j), ekk_(j)
                ps3, qs3 = r3(PSH), r3(QSH)
                ph3, qh3 = r3(ph_(j)), r3(qh_(j))
                A("dve", lambda e, ps3=ps3, ph3=ph3: e.tensor_copy(
                    ps3[:, :, 1:L], ph3[:, :, 0:L - 1]),
                  reads=["pqh%d" % pars[j], "ezv%d" % j], writes=["ezv%d" % j],
                  tag=px)
                A("dve", lambda e, ps3=ps3, j=j: e.tensor_copy(
                    ps3[:, :, 0:1], bco(sl(22 + j))),
                  reads=["ip%d" % j], writes=["ezv%d" % j], tag=px)
                A("pool", lambda e, qs3=qs3, qh3=qh3: e.tensor_copy(
                    qs3[:, :, 1:L], qh3[:, :, 0:L - 1]),
                  reads=["pqh%d" % pars[j], "ekk%d" % j], writes=["ekk%d" % j],
                  tag=px)
                A("pool", lambda e, qs3=qs3, j=j: e.tensor_copy(
                    qs3[:, :, 0:1], bco(sl(24 + j))),
                  reads=["iq%d" % j], writes=["ekk%d" % j], tag=px)
                vj = VP[:, j * CL:(j + 1) * CL]
                A("dve", lambda e, j=j, vj=vj: e.tensor_mul(TMPS[:, :], e3_(j), vj),
                  reads=["e3w", "VP"], writes=["tms"], tag=px)
                A("dve", lambda e, PSH=PSH: e.tensor_add(PSH, PSH, TMPS[:, :]),
                  reads=["ezv%d" % j, "tms"], writes=["ezv%d" % j], tag=px)
                A("pool", lambda e, QSH=QSH, j=j: e.tensor_add(QSH, QSH, e3_(j)),
                  reads=["ekk%d" % j, "e3w"], writes=["ekk%d" % j], tag=px)
                A("act", lambda e, QSH=QSH: e.activation(QSH, QSH, AF.Ln),
                  reads=["ekk%d" % j], writes=["ekk%d" % j], tag=px)
                A("act", lambda e, QSH=QSH: e.activation(
                    QSH, QSH, AF.Exp, scale=-1.0, bias=LHC[:, 0:1]),
                  reads=["ekk%d" % j, "LHC"], writes=["ekk%d" % j], tag=px)
                A("dve", lambda e, PSH=PSH, QSH=QSH: e.tensor_mul(PSH, PSH, QSH),
                  reads=["ezv%d" % j, "ekk%d" % j], writes=["ezv%d" % j], tag=px)
                tj = TP[:, j * CL:(j + 1) * CL]
                A("pool", lambda e, PSH=PSH, tj=tj: e.tensor_mul(TMPS[:, :], PSH, tj),
                  reads=["ezv%d" % j, "TP"], writes=["tms"], tag=px)
                ysb = YS[:, b * CT * g.OPW:(b + 1) * CT * g.OPW]
                ys3 = ysb.rearrange("p (c s) -> p c s", s=g.OPW)[
                    :, :, (w % 2) * SW + j * L:(w % 2) * SW + (j + 1) * L]
                A("dve", lambda e, ys3=ys3, PSH=PSH: e.tensor_add(
                    ys3, r3(PSH), r3(TMPS[:, :])),
                  reads=["ezv%d" % j, "tms"],
                  writes=["YS%d_%d_%d" % (b, w % 2, j)], tag=px)
            if to_scratch:
                D("pool", "ysst", lambda e, w=w, b=b: e.dma_start(
                    out=ysk[:, :].rearrange(
                        "p (wp c s) -> p wp c s", c=CT, s=g.OPW)[
                        :, w // 2, :, (w % 2) * SW:(w % 2 + 1) * SW],
                    in_=YS[:, b * CT * g.OPW:(b + 1) * CT * g.OPW].rearrange(
                        "p (c o) -> p c o", o=g.OPW)[:, :, (w % 2) * SW:
                                                     (w % 2 + 1) * SW]),
                  reads=["YS%d_%d_%d" % (b, w % 2, j) for j in range(NI)],
                  writes=["ysk_%d" % w], tag="ysst")

        def weave(main_gen, side_steps, every):
            i = 0
            side = list(side_steps)
            si = 0
            for _ in main_gen:
                i += 1
                if si < len(side) and i % every == 0:
                    side[si]()
                    si += 1
            while si < len(side):
                side[si]()
                si += 1

        def gu_sweep(h, call, side=()):
            weave(gu_sweep_gen(h, call), side, max(1, IT // max(1, len(side) + 1)))

        def outproj(h, wp, from_scratch=False):
            px = "op_h%d_p%d" % (h, wp)
            t0 = wp * g.OPW
            b = wp % 2
            ysb = YS[:, b * CT * g.OPW:(b + 1) * CT * g.OPW]
            ystoks = ["YS%d_%d_%d" % (b, k, jj)
                      for k in range(2) for jj in range(NI)]
            if from_scratch:
                D("pool", "ysld", lambda e, wp=wp, ysb=ysb: e.dma_start(
                    out=ysb, in_=ysk[:, wp * CT * g.OPW:(wp + 1) * CT * g.OPW]),
                  reads=["ysk_%d" % (2 * wp), "ysk_%d" % (2 * wp + 1)],
                  writes=ystoks, tag=px + "ysld")
            for ot in range(CT):
                wb = WR[:, (ot % 2) * CT * 128:(ot % 2 + 1) * CT * 128]
                wbt = "WR%d" % (ot % 2)
                D("sp", wbt, lambda e, ot=ot, wb=wb: e.dma_start(
                    out=wb, in_=wout_b[:, ot * CT * 128:(ot + 1) * CT * 128]),
                  reads=["woutb%d" % ot], writes=[wbt], tag=px + "ld")
                pp = pC if ot % 2 == 0 else pD
                ppt = PB(4 if ot % 2 == 0 else 6)
                for ct in range(CT):
                    A("pe", lambda e, ct=ct, pp=pp, wb=wb, ysb=ysb: e.matmul(
                        pp[:, 0:g.OPW], wb[:, ct * 128:(ct + 1) * 128],
                        ysb[:, ct * g.OPW:(ct + 1) * g.OPW],
                        start=(ct == 0), stop=(ct == CT - 1)),
                      reads=[wbt] + ystoks, writes=[ppt], tag=px)
                A("dve", lambda e, ot=ot, pp=pp, t0=t0: e.tensor_add(
                    xr_sl(ot, t0, g.OPW), xr_sl(ot, t0, g.OPW), pp[:, 0:g.OPW]),
                  reads=[ppt, xr_tok(ot, t0)], writes=[xr_tok(ot, t0)],
                  tag=px + "ev")

        def x_load(h):
            px = "xin_h%d" % h
            for tt in range(HALF // 128):
                D("sp", "XST", lambda e, tt=tt, h=h: e.dma_start(
                    out=XST[:, :],
                    in_=x_in[h * HALF + tt * 128:h * HALF + (tt + 1) * 128, :]),
                  writes=["XST"], tag=px)
                for ct in range(CT):
                    A("pe", lambda e, ct=ct: e.transpose(
                        pA[:, 0:128], XST[:, ct * 128:(ct + 1) * 128], IDENT[:, :]),
                      reads=["XST", "IDENT"], writes=[PB(0)], tag=px)
                    A("act", lambda e, ct=ct, tt=tt: e.copy(
                        xr_sl(ct, tt * 128, 128), pA[:, 0:128]),
                      reads=[PB(0)], writes=xr_toks_range(ct, tt * 128, 128),
                      tag=px)

        def x_store(h):
            px = "xout_h%d" % h
            outs = []
            for tt in range(HALF // 128):
                for ct in range(CT):
                    A("pe", lambda e, ct=ct, tt=tt: e.transpose(
                        pA[:, 128:256], xr_sl(ct, tt * 128, 128), IDENT[:, :]),
                      reads=xr_toks_range(ct, tt * 128, 128) + ["IDENT"],
                      writes=[PB(0)], tag=px)
                    A("act", lambda e, ct=ct: e.copy(
                        XST[:, ct * 128:(ct + 1) * 128], pA[:, 128:256]),
                      reads=[PB(0)], writes=["XST"], tag=px)
                tok = "xout_%d_%d" % (h, tt)
                D("pool", "xo", lambda e, tt=tt, h=h: e.dma_start(
                    out=x_out[h * HALF + tt * 128:h * HALF + (tt + 1) * 128, :],
                    in_=XST[:, :]),
                  reads=["XST"], writes=[tok], tag=px)
                outs.append(tok)
            return outs

        x_load(0)
        for w in range(g.NLW):
            ln_pass(0, w, "1")
        HB = HITC * 128
        gsh_toks, ush_toks = [], []
        for l in range(HITC):
            gsh_toks += recon(grec, 8, None, g_sh, "g", l, l + 1)
            ush_toks += recon(urec, 4, grec, u_sh, "u", l, l + 1)
        CC(lambda e: e.collective_compute(
            "AllGather", ALU.bypass, replica_groups=[list(range(NCORES))],
            ins=[g_sh[0:HB, :]], outs=[g_full_a.ap().opt()]),
           reads=gsh_toks[:HITC], writes=["g_fullA"], tag="AGg1")
        CC(lambda e: e.collective_compute(
            "AllGather", ALU.bypass, replica_groups=[list(range(NCORES))],
            ins=[u_sh[0:HB, :]], outs=[u_full_a.ap().opt()]),
           reads=ush_toks[:HITC], writes=["u_fullA"], tag="AGu1")
        for l in range(HITC, g.ITC):
            gsh_toks += recon(grec, 8, None, g_sh, "g", l, l + 1)
            ush_toks += recon(urec, 4, grec, u_sh, "u", l, l + 1)
        if NSPLIT == 2:
            CC(lambda e: e.collective_compute(
                "AllGather", ALU.bypass, replica_groups=[list(range(NCORES))],
                ins=[g_sh[HB:2 * HB, :]], outs=[g_full_b.ap().opt()]),
               reads=gsh_toks[HITC:], writes=["g_fullB"], tag="AGg2")
            CC(lambda e: e.collective_compute(
                "AllGather", ALU.bypass, replica_groups=[list(range(NCORES))],
                ins=[u_sh[HB:2 * HB, :]], outs=[u_full_b.ap().opt()]),
               reads=ush_toks[HITC:], writes=["u_fullB"], tag="AGu2")
        dsh_toks = recon(drec, 8, None, d_sh, "d", 0, g.ITC)
        CC(lambda e: e.collective_compute(
            "AllGather", ALU.bypass, replica_groups=[list(range(NCORES))],
            ins=[d_sh.ap().opt()], outs=[d_full.ap().opt()]),
           reads=dsh_toks, writes=["d_full"], tag="AGd")
        def cast_w(src_, dst, nblk, tag):
            for b_ in range(nblk):
                buf = WR[:, (b_ % 2) * CT * 128:(b_ % 2 + 1) * CT * 128]
                btok = "WR%d" % (b_ % 2)
                D("pool", btok + "c", lambda e, b_=b_, buf=buf, src_=src_:
                  e.dma_start(
                    out=buf, in_=src_[:, b_ * CT * 128:(b_ + 1) * CT * 128]),
                  writes=[btok], tag=tag)
                D("act", "cst", lambda e, b_=b_, buf=buf, dst=dst: e.dma_start(
                    out=dst[:, b_ * CT * 128:(b_ + 1) * CT * 128], in_=buf),
                  reads=[btok], writes=["%s%d" % (tag, b_)], tag=tag)

        cast_w(wrvk_f, wrvk_b, OT, "wrvkb")
        cast_w(wout_f, wout_b, CT, "woutb")
        gu_sweep(0, 1)
        down_pass(0, 1)
        for w in range(g.NLW):
            ln_pass(0, w, "2")
        rvk_sweep(0)
        D("pool", "spl", lambda e: e.dma_start(out=xr_sp[0, :, :], in_=XR[:, :]),
          reads=["XR_%d_%d" % (ct, d) for ct in range(CT) for d in range(g.NDW)],
          writes=["xr_sp0"], tag="spill0")
        x_load(1)
        for w in range(g.NLW):
            ln_pass(1, w, "1")
        gu_sweep(1, 1, side=[
            (lambda w=w: scan_window(0, w, False)) for w in range(g.NSW)])
        down_pass(1, 1)
        for w in range(g.NLW):
            ln_pass(1, w, "2")
        rvk_sweep(1, side=[
            (lambda w=w: scan_window(1, w, False)) for w in range(g.NSW)],
            start=2 * CT + 2)
        D("pool", "spl", lambda e: e.dma_start(out=xr_sp[1, :, :], in_=XR[:, :]),
          reads=["XR_%d_%d" % (ct, d) for ct in range(CT) for d in range(g.NDW)],
          writes=["xr_sp1"], tag="spill1")

        sb = sidx[0] % 2
        A("dve", lambda e, sb=sb: e.tensor_copy(GTH[:, 0:CT], sl(4 + sb * 2)),
          reads=["st%d" % sb], writes=["stpk"], tag="B")
        A("dve", lambda e, sb=sb: e.tensor_copy(GTH[:, CT:2 * CT], sl(5 + sb * 2)),
          reads=["st%d" % sb], writes=["stpk"], tag="B")
        A("dve", lambda e, sb=sb: e.tensor_copy(GTH[:, 2 * CT:3 * CT], sl(8 + sb)),
          reads=["st%d" % sb], writes=["stpk"], tag="B")
        D("pool", "stio", lambda e: e.dma_start(out=st_in[:, :], in_=GTH[:, 0:3 * CT]),
          reads=["stpk"], writes=["st_in"], tag="B")
        CC(lambda e: e.collective_compute(
            "AllGather", ALU.bypass, replica_groups=[list(range(NCORES))],
            ins=[st_in.ap().opt()], outs=[st_out.ap().opt()]),
           reads=["st_in"], writes=["st_out"], tag="AGst")
        D("sp", "gth", lambda e: e.dma_start(
            out=GTH[:, :].rearrange("p (r f) -> p r f", r=NCORES),
            in_=st_out[:, :].rearrange("(r p) f -> p r f", p=128)),
          reads=["st_out"], writes=["stpk"], tag="B")
        A("dve", lambda e: e.tensor_mul(GTH[:, :], GTH[:, :], MSKS[:, :]),
          reads=["stpk", "MSKS"], writes=["stpk"], tag="B")
        A("dve", lambda e: e.tensor_reduce(
            LNT[:, 0:3 * CT],
            GTH[:, :].rearrange("p (r f) -> p f r", f=3 * CT), AX.X, ALU.add),
          reads=["stpk"], writes=["LNT"], tag="B")
        A("dve", lambda e: e.tensor_add(LNT[:, 0:3 * CT], LNT[:, 0:3 * CT],
                                        INITA[:, :]),
          reads=["LNT", "INITA"], writes=["LNT"], tag="B")
        sidx[0] = 0
        A("dve", lambda e: e.tensor_copy(sl(4), LNT[:, 0:CT]),
          reads=["LNT"], writes=["st0"], tag="B")
        A("dve", lambda e: e.tensor_copy(sl(5), LNT[:, CT:2 * CT]),
          reads=["LNT"], writes=["st0"], tag="B")
        A("dve", lambda e: e.tensor_copy(sl(8), LNT[:, 2 * CT:3 * CT]),
          reads=["LNT"], writes=["st0"], tag="B")

        out_toks = []
        D("sp", "rld", lambda e: e.dma_start(out=XR[:, :], in_=xr_sp[0, :, :]),
          reads=["xr_sp0"],
          writes=["XR_%d_%d" % (ct, d) for ct in range(CT) for d in range(g.NDW)],
          tag="rld0")
        for w in range(g.NSW):
            scan_window(0, w, True)
            if w % 2 == 1:
                outproj(0, w // 2)
        for w in range(g.NLW):
            ln_pass(0, w, "3")
        gu_sweep(0, 2, side=[
            (lambda w=w: scan_window(1, w, True, to_scratch=True))
            for w in range(g.NSW)])
        down_pass(0, 2)
        out_toks += x_store(0)
        D("sp", "rld", lambda e: e.dma_start(out=XR[:, :], in_=xr_sp[1, :, :]),
          reads=["xr_sp1"],
          writes=["XR_%d_%d" % (ct, d) for ct in range(CT) for d in range(g.NDW)],
          tag="rld1")
        for wp in range(g.NSW // 2):
            outproj(1, wp, from_scratch=True)
        for w in range(g.NLW):
            ln_pass(1, w, "3")
        gu_sweep(1, 2)
        down_pass(1, 2)
        out_toks += x_store(1)

        A("pool", lambda e: e.engine_nop(), reads=out_toks, tag="fence")

        for ch in sorted(sch.chan_count):
            sems[ch] = EC(nc.semaphore("sem_" + ch.replace(":", "_")))
        sch.plan()

        @block.tensor
        def _(eng):
            sch.emit_engine("pe", eng, sems)

        @block.vector
        def _(eng):
            sch.emit_engine("dve", eng, sems)

        @block.scalar
        def _(eng):
            sch.emit_engine("act", eng, sems)

        @block.gpsimd
        def _(eng):
            sch.emit_engine("pool", eng, sems)

        @block.sync
        def _(eng):
            sch.emit_engine("sp", eng, sems)

    return nc, sch


def kernel(**inputs):
    g = CFG_REAL
    in_maps = host_prep(g, inputs)
    nc, _ = build_nc(g)
    res = run_bass_kernel_spmd(nc, in_maps, core_ids=list(range(NCORES)))
    out = np.empty((4, 2 * g.TOKS, g.C), np.float32)
    for core in range(NCORES):
        b, s = core // 2, core % 2
        out[b, s * g.TOKS:(s + 1) * g.TOKS] = res.results[core]["x_out"]
    return out




# version 26 (final)
# speedup vs baseline: 1.0174x; 1.0068x over previous
import numpy as np
import ml_dtypes
from contextlib import ExitStack

import concourse.bass as bass
import concourse.mybir as mybir
from concourse.bass_utils import run_bass_kernel_spmd

F32 = mybir.dt.float32
BF16 = mybir.dt.bfloat16
AF = mybir.ActivationFunctionType
ALU = mybir.AluOpType
AX = mybir.AxisListType
BF = ml_dtypes.bfloat16

NCORES = 8


class Cfg:
    def __init__(self, C, I, TOKS, HALF, LNW, SW, L, DW, GH, BIT, COMP=128.9):
        self.C = C
        self.I = I
        self.CT = C // 128
        self.IT = I // 128
        self.ITC = self.IT // NCORES
        self.RVK = 3 * C
        self.OT = self.RVK // 128
        self.TOKS = TOKS
        self.HALF = HALF
        self.NH = TOKS // HALF
        self.LNW = LNW
        self.NLW = HALF // LNW
        self.SW = SW
        self.NSW = HALF // SW
        self.L = L
        self.NI = SW // L
        self.OPW = 2 * SW
        self.DW = DW
        self.NDW = HALF // DW
        self.GH = GH
        self.NG = self.CT // GH
        self.TB = min(1024, HALF)
        self.BIT = BIT
        self.EB = min(1024, HALF)
        self.EBS = min(256, HALF)
        self.NEB = HALF // self.EB
        self.COMP = COMP
        self.RCHW = min(128, BIT * HALF // 16)
        assert SW % L == 0 and HALF % SW == 0 and HALF % LNW == 0
        assert HALF % DW == 0 and self.CT % GH == 0 and HALF % 128 == 0
        assert self.IT % (NCORES * 1) == 0 and self.IT % BIT == 0


CFG_REAL = Cfg(C=2048, I=8192, TOKS=2048, HALF=1024, LNW=256, SW=64, L=32,
               DW=512, GH=4, BIT=2)
CFG_MINI = Cfg(C=256, I=1024, TOKS=256, HALF=128, LNW=32, SW=16, L=8,
               DW=64, GH=1, BIT=2)


ENGS = ("pe", "dve", "act", "pool", "sp")
DMAQ = {"sp": "qsp", "act": "qact", "pool": "qpool"}


class Op:
    __slots__ = ("eng", "emit", "reads", "writes", "kind", "idx", "qidx",
                 "deps", "inc", "waits", "drain", "tag", "chan")

    def __init__(self, eng, emit, reads, writes, kind, tag="", chan=None):
        self.eng, self.emit, self.reads, self.writes = eng, emit, list(reads), list(writes)
        self.kind, self.tag, self.chan = kind, tag, chan
        self.idx = self.qidx = -1
        self.deps, self.waits = [], []
        self.inc = self.drain = False


class Sched:

    def __init__(self):
        self.ops = []
        self.last_w = {}
        self.readers = {}
        self.eng_ops = {e: [] for e in ENGS}
        self.chan_count = {}
        self.chan_last = {}
        self.cc_count = 0

    def add(self, eng, emit, reads=(), writes=(), kind="c", tag="", chan=None):
        op = Op(eng, emit, reads, writes, kind, tag, chan)
        if kind == "c":
            dom = eng
            op.idx = len(self.eng_ops[eng])
            me = (dom, op.idx)
        elif kind == "d":
            assert chan is not None
            dom = "q:" + chan
            op.qidx = self.chan_count.get(dom, 0)
            self.chan_count[dom] = op.qidx + 1
            op.idx = len(self.eng_ops[eng])
            me = (dom, op.qidx)
        else:
            dom = "cc%d" % self.cc_count
            op.qidx = self.cc_count
            self.cc_count += 1
            op.idx = len(self.eng_ops[eng])
            me = (dom, 0)
        self.eng_ops[eng].append(op)
        deps = set()
        for t in op.reads:
            w = self.last_w.get(t)
            if w is not None:
                deps.add(w)
        for t in op.writes:
            w = self.last_w.get(t)
            if w is not None:
                deps.add(w)
            for r in self.readers.get(t, ()):
                deps.add(r)
        if kind == "d":
            prev = self.chan_last.get(dom)
            if prev is not None:
                deps.add(prev)
            self.chan_last[dom] = me
        deps.discard(me)
        op.deps = sorted(deps)
        for t in op.reads:
            self.readers.setdefault(t, []).append(me)
        for t in op.writes:
            self.last_w[t] = me
            self.readers[t] = []
        self.ops.append((op, me))
        return op

    def plan(self):
        eff = {}
        issue = {e: {} for e in ENGS}
        run = {e: {} for e in ENGS}
        need_inc = set()
        for op, me in self.ops:
            e = op.eng
            clk = issue[e]
            if e == "pe":
                for k, v in run[e].items():
                    if clk.get(k, -1) < v:
                        clk[k] = v
                if clk.get(e, -1) < op.idx - 1:
                    clk[e] = op.idx - 1
            mywaits = []
            for dep in op.deps:
                ddom, didx = dep
                if clk.get(ddom, -1) >= didx:
                    continue
                if ddom == e:
                    op.drain = True
                    for k, v in run[e].items():
                        if clk.get(k, -1) < v:
                            clk[k] = v
                    if clk.get(e, -1) < op.idx - 1:
                        clk[e] = op.idx - 1
                    if clk.get(ddom, -1) < didx:
                        raise RuntimeError("drain cannot cover dep")
                else:
                    mywaits.append(dep)
                    need_inc.add(dep)
                    for k, v in eff.get(dep, {}).items():
                        if clk.get(k, -1) < v:
                            clk[k] = v
                    if clk.get(ddom, -1) < didx:
                        clk[ddom] = didx
            op.waits = mywaits
            myeff = dict(clk)
            myeff[me[0]] = max(myeff.get(me[0], -1), me[1])
            eff[me] = myeff
            run_eff = myeff if op.kind == "c" else clk
            re_ = run[e]
            for k, v in run_eff.items():
                if re_.get(k, -1) < v:
                    re_[k] = v
        for op, me in self.ops:
            if me in need_inc:
                op.inc = True
            if op.kind in ("d", "cc"):
                op.inc = True
        self.inc_prefix = {}
        for e in ENGS:
            cnt, pref = 0, []
            for op in self.eng_ops[e]:
                if op.kind == "c" and op.inc:
                    cnt += 1
                pref.append(cnt)
            self.inc_prefix[e] = pref

    def wait_value(self, dep):
        dom, idx = dep
        if dom in ENGS:
            return (dom, self.inc_prefix[dom][idx])
        if dom.startswith("q:"):
            return (dom, 16 * (idx + 1))
        return (dom, 1)

    def emit_engine(self, ename, eng, sems):
        waited = {}
        for op in self.eng_ops[ename]:
            if op.drain:
                eng.drain()
            for dep in op.waits:
                dom, val = self.wait_value(dep)
                if waited.get(dom, -1) >= val:
                    continue
                eng.wait_ge(sems[dom], val)
                waited[dom] = val
            ins = op.emit(eng)
            if op.kind == "c":
                if op.inc:
                    ins.then_inc(sems[ename], 1)
            elif op.kind == "d":
                ins.then_inc(sems["q:" + op.chan], 16)
            else:
                ins.then_inc(sems["cc%d" % op.qidx], 1)

    def stats(self):
        n_drain = sum(1 for o, _ in self.ops if o.drain)
        n_wait = sum(len(o.waits) for o, _ in self.ops)
        per_eng = {e: len(self.eng_ops[e]) for e in ENGS}
        return dict(total=len(self.ops), drains=n_drain, waits=n_wait, **per_eng)


def spline_grid(n_ctrl, n):
    t = np.linspace(0.0, n_ctrl - 1.0, n, dtype=np.float64)
    i = np.clip(np.floor(t).astype(np.int64), 0, n_ctrl - 2)
    f = (t - i).astype(np.float32)
    idx = np.stack([np.clip(i + k, 0, n_ctrl - 1) for k in (-1, 0, 1, 2)], 0)
    f2, f3 = f * f, f * f * f
    coeffs = np.stack([
        0.5 * (-f + 2 * f2 - f3),
        0.5 * (2 - 5 * f2 + 3 * f3),
        0.5 * (f + 4 * f2 - 3 * f3),
        0.5 * (-f2 + f3),
    ]).astype(np.float32)
    return idx, coeffs


def host_prep(cfg, inputs):
    g = cfg
    x = inputs["L0_attn_x"]
    W_rvk = inputs["L0_attn_W_rvk"].astype(np.float32)
    W_out = inputs["L0_attn_W_out"].astype(np.float32)
    td = inputs["L0_attn_time_decay"].astype(np.float32)
    tf = inputs["L0_attn_time_first"].astype(np.float32)

    n = g.I * g.C
    idx, coeffs = spline_grid(inputs["L0_ffn_gate_cp"].shape[0], n)

    def tile_gu(a):
        return a.reshape(g.IT, 128, g.CT, 128).transpose(0, 3, 2, 1)

    def tile_dn(a):
        return a.reshape(g.CT, 128, g.IT, 128).transpose(2, 3, 0, 1)

    gate_cp = inputs["L0_ffn_gate_cp"].astype(np.float32)
    up_cp = inputs["L0_ffn_up_cp"].astype(np.float32)
    down_cp = inputs["L0_ffn_down_cp"].astype(np.float32)
    n2 = g.C * g.I
    idx2, coeffs2 = idx, coeffs
    g_arr = [tile_gu(gate_cp[idx[k]]) for k in range(4)] + \
            [tile_gu(coeffs[k]) for k in range(4)]
    u_arr = [tile_gu(up_cp[idx[k]]) for k in range(4)]
    d_arr = [tile_dn(down_cp[idx2[k]]) for k in range(4)] + \
            [tile_dn(coeffs2[k]) for k in range(4)]

    def core_sl(arrs, r):
        per = np.stack(
            [a[g.ITC * r:g.ITC * (r + 1)].transpose(1, 0, 2, 3).reshape(128, -1)
             for a in arrs], axis=1)
        n_arr = per.shape[1]
        samp = per.shape[2]
        nch = samp // g.RCHW
        per = per.reshape(128, n_arr, nch, g.RCHW).transpose(0, 2, 1, 3)
        return np.ascontiguousarray(per.reshape(128, -1))

    wrvk_t = np.ascontiguousarray(
        W_rvk.reshape(g.OT, 128, g.CT, 128).transpose(3, 0, 2, 1).reshape(128, -1))
    wout_t = np.ascontiguousarray(
        W_out.reshape(g.CT, 128, g.CT, 128).transpose(3, 0, 2, 1).reshape(128, -1))
    td_r = np.ascontiguousarray(td.reshape(g.CT, 128).T)
    tf_r = np.ascontiguousarray(tf.reshape(g.CT, 128).T)
    ident = np.eye(128, dtype=np.float32)

    in_maps = []
    for core in range(NCORES):
        b, s = core // 2, core % 2
        xc = np.ascontiguousarray(
            x[b, s * g.TOKS:(s + 1) * g.TOKS].astype(np.float32))
        msk = np.zeros((128, NCORES * 3 * g.CT), np.float32)
        init_add = np.zeros((128, 3 * g.CT), np.float32)
        if s == 1:
            msk[:, (core - 1) * 3 * g.CT:core * 3 * g.CT] = 1.0
        else:
            init_add[:, 2 * g.CT:3 * g.CT] = -1e38
        in_maps.append({
            "x_in": xc,
            "grec": core_sl(g_arr, core),
            "urec": core_sl(u_arr, core),
            "drec": core_sl(d_arr, core),
            "wrvk_f": wrvk_t, "wout_f": wout_t,
            "td_r": td_r, "tf_r": tf_r, "ident": ident,
            "msk": msk, "init_add": init_add,
        })
    return in_maps


def build_nc(g):
    CT, IT, OT, L, NI, SW, LNW, HALF, DW, GH, NG, TB, BIT = (
        g.CT, g.IT, g.OT, g.L, g.NI, g.SW, g.LNW, g.HALF, g.DW, g.GH, g.NG,
        g.TB, g.BIT)
    CL = CT * L
    SAMP = g.ITC * CT * 128

    nc = bass.Bass("TRN2", target_bir_lowering=False)

    x_in = nc.dram_tensor("x_in", [g.TOKS, g.C], F32, kind="ExternalInput")
    grec = nc.dram_tensor("grec", [128, 8 * SAMP], F32, kind="ExternalInput")
    urec = nc.dram_tensor("urec", [128, 4 * SAMP], F32, kind="ExternalInput")
    drec = nc.dram_tensor("drec", [128, 8 * SAMP], F32, kind="ExternalInput")
    wrvk_f = nc.dram_tensor("wrvk_f", [128, OT * CT * 128], F32, kind="ExternalInput")
    wout_f = nc.dram_tensor("wout_f", [128, CT * CT * 128], F32, kind="ExternalInput")
    td_rD = nc.dram_tensor("td_r", [128, CT], F32, kind="ExternalInput")
    tf_rD = nc.dram_tensor("tf_r", [128, CT], F32, kind="ExternalInput")
    identD = nc.dram_tensor("ident", [128, 128], F32, kind="ExternalInput")
    mskD = nc.dram_tensor("msk", [128, NCORES * 3 * CT], F32, kind="ExternalInput")
    initD = nc.dram_tensor("init_add", [128, 3 * CT], F32, kind="ExternalInput")
    x_out = nc.dram_tensor("x_out", [g.TOKS, g.C], F32, kind="ExternalOutput")

    g_sh = nc.dram_tensor("g_sh", [g.ITC * 128, CT * 128], BF16)
    u_sh = nc.dram_tensor("u_sh", [g.ITC * 128, CT * 128], BF16)
    d_sh = nc.dram_tensor("d_sh", [g.ITC * 128, CT * 128], BF16)
    HITC = max(1, g.ITC // 2)
    NSPLIT = 2 if g.ITC >= 2 else 1
    g_full_a = nc.dram_tensor("g_full_a", [NCORES * HITC * 128, CT * 128], BF16, addr_space="Shared")
    g_full_b = nc.dram_tensor("g_full_b", [NCORES * HITC * 128, CT * 128], BF16, addr_space="Shared")
    u_full_a = nc.dram_tensor("u_full_a", [NCORES * HITC * 128, CT * 128], BF16, addr_space="Shared")
    u_full_b = nc.dram_tensor("u_full_b", [NCORES * HITC * 128, CT * 128], BF16, addr_space="Shared")
    d_full = nc.dram_tensor("d_full", [IT * 128, CT * 128], BF16, addr_space="Shared")
    wrvk_b = nc.dram_tensor("wrvk_b", [128, OT * CT * 128], BF16)
    wout_b = nc.dram_tensor("wout_b", [128, CT * CT * 128], BF16)
    inter_s = nc.dram_tensor("inter_s", [IT, 128, HALF], BF16)
    kpk = nc.dram_tensor("kpk", [g.NH, 128, g.NSW * NI * CL], F32)
    vpk = nc.dram_tensor("vpk", [g.NH, 128, g.NSW * NI * CL], F32)
    tpk = nc.dram_tensor("tpk", [g.NH, 128, g.NSW * NI * CL], BF16)
    xr_sp = nc.dram_tensor("xr_sp", [g.NH, 128, CT * HALF], F32)
    ysk = nc.dram_tensor("ysk", [128, g.NSW * CT * SW], BF16)
    st_in = nc.dram_tensor("st_in", [128, 3 * CT], F32)
    st_out = nc.dram_tensor("st_out", [NCORES * 128, 3 * CT], F32)

    sch = Sched()

    with ExitStack() as _ctx:
        EC = _ctx.enter_context
        XR = EC(nc.sbuf_tensor("XR", [128, CT * HALF], F32))
        XN = EC(nc.sbuf_tensor("XN", [128, CT * HALF], BF16))
        XSQ = EC(nc.sbuf_tensor("XSQ", [128, 2 * LNW], BF16))
        KP = EC(nc.sbuf_tensor("KP", [128, NI * CL], F32))
        VP = EC(nc.sbuf_tensor("VP", [128, NI * CL], F32))
        TP = EC(nc.sbuf_tensor("TP", [128, NI * CL], BF16))
        YS = EC(nc.sbuf_tensor("YS", [128, 2 * CT * g.OPW], BF16))
        IBST = EC(nc.sbuf_tensor("IBST", [128, 2 * BIT * HALF], BF16))
        IBO = EC(nc.sbuf_tensor("IBO", [128, 2 * HALF], BF16))
        SG = EC(nc.sbuf_tensor("SG", [128, TB], F32))
        WG = EC(nc.sbuf_tensor("WG", [128, 2 * CT * 128], BF16))
        WU = EC(nc.sbuf_tensor("WU", [128, 2 * CT * 128], BF16))
        WD = EC(nc.sbuf_tensor("WD", [128, 2 * BIT * GH * 128], BF16))
        WR = EC(nc.sbuf_tensor("WR", [128, 2 * CT * 128], BF16))
        KVST = EC(nc.sbuf_tensor("KVST", [128, 2 * g.EBS], F32))
        RST = EC(nc.sbuf_tensor("RST", [128, 2 * g.EBS], BF16))
        EZK = EC(nc.sbuf_tensor("EZK", [128, NI * 2 * CL], F32))
        PQH = EC(nc.sbuf_tensor("PQH", [128, max(2 * 2 * CL, 12 * g.RCHW)], F32))
        E3W = EC(nc.sbuf_tensor("E3W", [128, NI * CL], F32))
        TMPS = EC(nc.sbuf_tensor("TMPS", [128, CL], F32))
        SSM = EC(nc.sbuf_tensor("SSM", [128, 28 * CT], F32))
        LNT = EC(nc.sbuf_tensor("LNT", [128, LNW], F32))
        CEWZ = EC(nc.sbuf_tensor("CEWZ", [128, 2 * CL], F32))
        CEU = EC(nc.sbuf_tensor("CEU", [128, CL], F32))
        ONESL = EC(nc.sbuf_tensor("ONESL", [128, L], F32))
        ONESBF = EC(nc.sbuf_tensor("ONESBF", [128, 1], BF16))
        ONER = EC(nc.sbuf_tensor("ONER", [1, 128], F32))
        IDENT = EC(nc.sbuf_tensor("IDENT", [128, 128], F32))
        ROWS = EC(nc.sbuf_tensor("ROWS", [1, 2 * LNW], F32))
        EPSC = EC(nc.sbuf_tensor("EPSC", [1, 1], F32))
        LHC = EC(nc.sbuf_tensor("LHC", [128, 1], F32))
        GTH = EC(nc.sbuf_tensor("GTH", [128, NCORES * 3 * CT], F32))
        MSKS = EC(nc.sbuf_tensor("MSKS", [128, NCORES * 3 * CT], F32))
        INITA = EC(nc.sbuf_tensor("INITA", [128, 3 * CT], F32))
        XST = EC(nc.sbuf_tensor("XST", [128, g.C], F32))
        pA = EC(nc.psum_tensor("pA", [128, 1024], F32))
        pB = EC(nc.psum_tensor("pB", [128, 1024], F32))
        pC = EC(nc.psum_tensor("pC", [128, 1024], F32))
        pD = EC(nc.psum_tensor("pD", [128, 1024], F32))
        s_pe = EC(nc.semaphore("s_pe"))
        s_dve = EC(nc.semaphore("s_dve"))
        s_act = EC(nc.semaphore("s_act"))
        s_pool = EC(nc.semaphore("s_pool"))
        s_sp = EC(nc.semaphore("s_sp"))
        s_cc0 = EC(nc.semaphore("s_cc0"))
        s_cc1 = EC(nc.semaphore("s_cc1"))
        s_cc2 = EC(nc.semaphore("s_cc2"))
        s_cc3 = EC(nc.semaphore("s_cc3"))
        s_cc4 = EC(nc.semaphore("s_cc4"))
        s_cc5 = EC(nc.semaphore("s_cc5"))
        s_cc6 = EC(nc.semaphore("s_cc6"))
        s_cc7 = EC(nc.semaphore("s_cc7"))
        block = EC(nc.Block())
        def ptoks(base, width):
            n = max(1, (width + 511) // 512)
            return [PB(base + i) for i in range(n)]

        sems = {"pe": s_pe, "dve": s_dve, "act": s_act, "pool": s_pool,
                "sp": s_sp,
                "cc0": s_cc0, "cc1": s_cc1, "cc2": s_cc2, "cc3": s_cc3,
                "cc4": s_cc4, "cc5": s_cc5, "cc6": s_cc6, "cc7": s_cc7}

        def A(eng, fn, reads=(), writes=(), tag=""):
            sch.add(eng, fn, reads, writes, "c", tag)

        def D(eng, chan, fn, reads=(), writes=(), tag=""):
            sch.add(eng, fn, reads, writes, "d", tag, chan=chan)

        def CC(fn, reads=(), writes=(), tag=""):
            sch.add("pool", fn, reads, writes, "cc", tag)

        PB = lambda i: "pb%d" % i

        def sl(i):
            return SSM[:, i * CT:(i + 1) * CT]


        D("sp", "ini", lambda e: e.dma_start(out=IDENT[:, :], in_=identD[:, :]),
          writes=["IDENT"])
        D("sp", "ini", lambda e: e.dma_start(out=MSKS[:, :], in_=mskD[:, :]),
          writes=["MSKS"])
        D("sp", "ini", lambda e: e.dma_start(out=INITA[:, :], in_=initD[:, :]),
          writes=["INITA"])
        D("sp", "ini", lambda e: e.dma_start(out=sl(20), in_=td_rD[:, :]), writes=["ssm20"])
        D("sp", "ini", lambda e: e.dma_start(out=sl(21), in_=tf_rD[:, :]), writes=["ssm21"])
        A("pool", lambda e: e.memset(sl(4), 0.0), writes=["st0"])
        A("pool", lambda e: e.memset(sl(5), 0.0), writes=["st0"])
        A("pool", lambda e: e.memset(sl(8), -1e38), writes=["st0"])
        A("pool", lambda e: e.memset(sl(6), 0.0), writes=["st1"])
        A("pool", lambda e: e.memset(sl(7), 0.0), writes=["st1"])
        A("pool", lambda e: e.memset(sl(9), -1e38), writes=["st1"])
        A("pool", lambda e: e.memset(ONESL[:, :], 1.0), writes=["ONESL"])
        A("pool", lambda e: e.memset(EPSC[:, :], 1e-5), writes=["EPSC"])
        A("pool", lambda e: e.memset(LHC[:, :], float(np.log(0.5))), writes=["LHC"])
        A("pool", lambda e: e.memset(ONESBF[:, :], 1.0), writes=["ONESBF"])
        A("pool", lambda e: e.memset(ONER[:, :], 1.0), writes=["ONER"])
        A("act", lambda e: e.activation(sl(18), sl(20), AF.Exp),
          reads=["ssm20"], writes=["ssm18"])
        A("dve", lambda e: e.tensor_scalar_mul(sl(0), sl(18), -1.0),
          reads=["ssm18"], writes=["ssm0"])
        A("act", lambda e: e.activation(sl(2), sl(0), AF.Exp),
          reads=["ssm0"], writes=["ssm2"])
        A("dve", lambda e: e.tensor_scalar_mul(sl(1), sl(0), float(L)),
          reads=["ssm0"], writes=["ssm1"])
        A("act", lambda e: e.activation(sl(3), sl(21), AF.Exp),
          reads=["ssm21"], writes=["ssm3"])
        for ct in range(CT):
            A("dve", lambda e, ct=ct: e.tensor_scalar_mul(
                CEWZ[:, ct * L:(ct + 1) * L], ONESL[:, :], sl(2)[:, ct:ct + 1]),
              reads=["ONESL", "ssm2"], writes=["CEWZ"])
            A("pool", lambda e, ct=ct: e.tensor_scalar_mul(
                CEU[:, ct * L:(ct + 1) * L], ONESL[:, :], sl(3)[:, ct:ct + 1]),
              reads=["ONESL", "ssm3"], writes=["CEU"])
        A("dve", lambda e: e.tensor_copy(CEWZ[:, CL:2 * CL], CEWZ[:, 0:CL]),
          reads=["CEWZ"], writes=["CEWZ"])
        A("dve", lambda e: e.memset(
            CEWZ[:, :].rearrange("p (c l) -> p c l", l=L)[:, :, 0:1], 0.0),
          writes=["CEWZ"])

        RCHW = g.RCHW

        _rcnt = [0]

        def recon(rec_dram, n_own, coef_src, sh_dram, wtag, lo, hi):
            ibf = IBST[:, :].bitcast(F32)
            wgf = WG[:, :].bitcast(F32)
            wuf = WU[:, :].bitcast(F32)
            bufs = [ibf[:, 0:8 * RCHW], ibf[:, 8 * RCHW:16 * RCHW],
                    wgf[:, 0:8 * RCHW], wgf[:, 8 * RCHW:16 * RCHW],
                    wuf[:, 0:8 * RCHW], wuf[:, 8 * RCHW:16 * RCHW]]
            queues = ["act", "sp", "act", "sp", "act", "sp"]
            perit = CT * 128
            nper = perit // RCHW
            for s in range(lo * nper, hi * nper):
                gs = _rcnt[0]
                _rcnt[0] += 1
                bi = gs % 6
                buf = bufs[bi]
                btok = "rs%d" % bi
                q = queues[bi]

                def ld(e, s=s, buf=buf, rec=rec_dram, n_own=n_own):
                    return e.dma_start(
                        out=buf[:, 0:n_own * RCHW],
                        in_=rec[:, s * n_own * RCHW:(s + 1) * n_own * RCHW])
                D(q, btok, ld, writes=[btok], tag=wtag + "ld")
                if coef_src is not None:
                    def ldc(e, s=s, buf=buf, cd=coef_src):
                        return e.dma_start(
                            out=buf[:, 4 * RCHW:8 * RCHW],
                            in_=cd[:, s * 8 * RCHW + 4 * RCHW:
                                   (s + 1) * 8 * RCHW])
                    D(q, btok, ldc, writes=[btok], tag=wtag + "ldc")
                it_loc = s // nper
                coff = (s % nper) * RCHW
                wo = WD[:, coff:coff + RCHW]
                par = gs % 3
                ceng = "pool" if par == 2 else "dve"
                wtok = "rw%d" % par
                ts_ = [PQH[:, (par * 4 + i) * RCHW:(par * 4 + i + 1) * RCHW]
                       for i in range(4)]
                tt = ["rt%d_%d" % (i, par) for i in range(4)]
                a_ = lambda i, buf=buf: buf[:, i * RCHW:(i + 1) * RCHW]
                for i in range(4):
                    A(ceng, lambda e, a=a_, i=i, ts_=ts_: e.tensor_mul(
                        ts_[i], a(i), a(4 + i)),
                      reads=[btok], writes=[tt[i]], tag=wtag)
                A(ceng, lambda e, ts_=ts_: e.tensor_add(ts_[0], ts_[0], ts_[1]),
                  reads=[tt[0], tt[1]], writes=[tt[0]], tag=wtag)
                A(ceng, lambda e, ts_=ts_: e.tensor_add(ts_[2], ts_[2], ts_[3]),
                  reads=[tt[2], tt[3]], writes=[tt[2]], tag=wtag)
                A(ceng, lambda e, wo=wo, ts_=ts_: e.tensor_add(wo, ts_[0], ts_[2]),
                  reads=[tt[0], tt[2]], writes=[wtok], tag=wtag)
                if coff + RCHW == perit:
                    def stw(e, it_loc=it_loc, sh=sh_dram):
                        return e.dma_start(
                            out=sh[it_loc * 128:(it_loc + 1) * 128, :],
                            in_=WD[:, 0:perit])
                    D("sp", "sst0", stw, reads=["rw0", "rw1", "rw2"],
                      writes=["%s_sh%d" % (wtag, it_loc)], tag=wtag + "st")
            return ["%s_sh%d" % (wtag, i) for i in range(lo, hi)]


        def xr_sl(ct, t0, n):
            return XR[:, ct * HALF + t0:ct * HALF + t0 + n]

        def xr_tok(ct, t0):
            return "XR_%d_%d" % (ct, t0 // DW)

        def xr_toks_range(ct, t0, n):
            return ["XR_%d_%d" % (ct, d) for d in
                    range(t0 // DW, (t0 + n - 1) // DW + 1)]

        def xn_sl(ct, t0, n):
            return XN[:, ct * HALF + t0:ct * HALF + t0 + n]

        def xn_tok(ct, w):
            return "XN_%d_%d" % (ct, w)

        def xn_toks_range(ct, t0, n):
            return [xn_tok(ct, w) for w in range(t0 // LNW, (t0 + n - 1) // LNW + 1)]

        def ln_pass(h, w, pid):
            t0 = w * LNW
            px = "ln%s_h%d_w%d" % (pid, h, w)
            rd = lambda i: ROWS[0:1, i * LNW:(i + 1) * LNW]
            for ct in range(CT):
                A("dve", lambda e, ct=ct: e.tensor_copy(
                    xn_sl(ct, t0, LNW), xr_sl(ct, t0, LNW)),
                  reads=[xr_tok(ct, t0)], writes=[xn_tok(ct, w)], tag=px)
            for ct in range(CT):
                xq = XSQ[:, (ct % 2) * LNW:(ct % 2 + 1) * LNW]
                xqt = "XSQ%d" % (ct % 2)
                A("pool", lambda e, ct=ct, xq=xq: e.tensor_mul(
                    xq, xn_sl(ct, t0, LNW), xn_sl(ct, t0, LNW)),
                  reads=[xn_tok(ct, w)], writes=[xqt], tag=px)
                A("pe", lambda e, ct=ct: e.matmul(
                    pB[0:1, 0:LNW], ONESBF[:, :], xn_sl(ct, t0, LNW),
                    start=(ct == 0), stop=(ct == CT - 1)),
                  reads=[xn_tok(ct, w), "ONESBF"], writes=[PB(2)], tag=px)
                A("pe", lambda e, ct=ct, xq=xq: e.matmul(
                    pB[0:1, 512:512 + LNW], ONESBF[:, :], xq,
                    start=(ct == 0), stop=(ct == CT - 1)),
                  reads=[xqt, "ONESBF"], writes=[PB(3)], tag=px)
            A("dve", lambda e: e.tensor_scalar_mul(rd(0), pB[0:1, 0:LNW], 1.0 / g.C),
              reads=[PB(2)], writes=["row0"], tag=px)
            A("dve", lambda e: e.tensor_scalar_mul(rd(1), pB[0:1, 512:512 + LNW], 1.0 / g.C),
              reads=[PB(3)], writes=["row1"], tag=px)
            A("dve", lambda e: e.tensor_mul(LNT[0:1, :], rd(0), rd(0)),
              reads=["row0"], writes=["LNT"], tag=px)
            A("dve", lambda e: e.tensor_sub(rd(1), rd(1), LNT[0:1, :]),
              reads=["row1", "LNT"], writes=["row1"], tag=px)
            A("act", lambda e: e.activation(rd(1), rd(1), AF.Ln, bias=EPSC[:, 0:1]),
              reads=["row1", "EPSC"], writes=["row1"], tag=px)
            A("act", lambda e: e.activation(rd(1), rd(1), AF.Exp, scale=-0.5),
              reads=["row1"], writes=["row1"], tag=px)
            A("dve", lambda e: e.tensor_mul(rd(0), rd(0), rd(1)),
              reads=["row0", "row1"], writes=["row0"], tag=px)
            A("pe", lambda e: e.matmul(pB[:, LNW:2 * LNW], ONER[:, :], rd(1),
                                       start=True, stop=True),
              reads=["row1", "ONER"], writes=[PB(2)], tag=px)
            A("pe", lambda e: e.matmul(pB[:, 512 + LNW:512 + 2 * LNW], ONER[:, :], rd(0),
                                       start=True, stop=True),
              reads=["row0", "ONER"], writes=[PB(3)], tag=px)
            for ct in range(CT):
                A("dve", lambda e, ct=ct: e.tensor_mul(
                    LNT[:, :], xr_sl(ct, t0, LNW), pB[:, LNW:2 * LNW]),
                  reads=[xr_tok(ct, t0), PB(2)], writes=["LNT"], tag=px)
                A("dve", lambda e, ct=ct: e.tensor_sub(
                    xn_sl(ct, t0, LNW), LNT[:, :], pB[:, 512 + LNW:512 + 2 * LNW]),
                  reads=["LNT", PB(3)], writes=[xn_tok(ct, w)], tag=px)

        GU_ORDER = [r * g.ITC + ph * HITC + l
                    for ph in range(NSPLIT) for r in range(NCORES)
                    for l in range(HITC)]
        assert sorted(GU_ORDER) == list(range(IT))

        def gu_src(it):
            r, l = it // g.ITC, it % g.ITC
            ph = l // HITC
            row = (r * HITC + (l % HITC)) * 128
            return ph, row

        def gu_sweep_gen(h, call):
            px = "gu%d_h%d" % (call, h)
            NTB = HALF // TB
            for ii, it in enumerate(GU_ORDER):
                yield
                gb = WG[:, (ii % 2) * CT * 128:(ii % 2 + 1) * CT * 128]
                ub = WU[:, (ii % 2) * CT * 128:(ii % 2 + 1) * CT * 128]
                gbt, ubt = "WG%d" % (ii % 2), "WU%d" % (ii % 2)
                ph, row = gu_src(it)
                gsrc = g_full_a if ph == 0 else g_full_b
                usrc = u_full_a if ph == 0 else u_full_b
                gtok = "g_fullA" if ph == 0 else "g_fullB"
                utok = "u_fullA" if ph == 0 else "u_fullB"
                D("sp", gbt, lambda e, row=row, gb=gb, gsrc=gsrc: e.dma_start(
                    out=gb, in_=gsrc[row:row + 128, :]),
                  reads=[gtok], writes=[gbt], tag=px + "ld")
                D("sp", ubt, lambda e, row=row, ub=ub, usrc=usrc: e.dma_start(
                    out=ub, in_=usrc[row:row + 128, :]),
                  reads=[utok], writes=[ubt], tag=px + "ld")
                gp = pA if ii % 2 == 0 else pC
                up = pB if ii % 2 == 0 else pD
                gpt = ptoks(0 if ii % 2 == 0 else 4, TB)
                upt = ptoks(2 if ii % 2 == 0 else 6, TB)
                NB = (TB + 511) // 512
                BW = min(512, TB)
                for tb in range(NTB):
                    for ht in range(CT):
                        xr_ = xn_toks_range(ht, tb * TB, TB)
                        for nb in range(NB):
                            A("pe", lambda e, ht=ht, tb=tb, nb=nb, gp=gp, gb=gb:
                              e.matmul(
                                gp[:, nb * BW:(nb + 1) * BW],
                                gb[:, ht * 128:(ht + 1) * 128],
                                XN[:, ht * HALF + tb * TB + nb * BW:
                                   ht * HALF + tb * TB + (nb + 1) * BW],
                                start=(ht == 0), stop=(ht == CT - 1)),
                              reads=[gbt] + xr_, writes=gpt, tag=px)
                    for ht in range(CT):
                        xr_ = xn_toks_range(ht, tb * TB, TB)
                        for nb in range(NB):
                            A("pe", lambda e, ht=ht, tb=tb, nb=nb, up=up, ub=ub:
                              e.matmul(
                                up[:, nb * BW:(nb + 1) * BW],
                                ub[:, ht * 128:(ht + 1) * 128],
                                XN[:, ht * HALF + tb * TB + nb * BW:
                                   ht * HALF + tb * TB + (nb + 1) * BW],
                                start=(ht == 0), stop=(ht == CT - 1)),
                              reads=[ubt] + xr_, writes=upt, tag=px)
                    ib = IBO[:, (ii % 2) * HALF + tb * TB:(ii % 2) * HALF + (tb + 1) * TB]
                    A("act", lambda e, gp=gp: e.activation(SG[:, 0:TB], gp[:, 0:TB], AF.Silu),
                      reads=gpt, writes=["SG"], tag=px)
                    A("dve", lambda e, up=up, ib=ib: e.tensor_mul(ib, SG[:, 0:TB], up[:, 0:TB]),
                      reads=["SG"] + upt, writes=["IBO%d" % (it % 2)], tag=px)
                D("pool", "int", lambda e, it=it: e.dma_start(
                    out=inter_s[it, :, :],
                    in_=IBO[:, (it % 2) * HALF:(it % 2 + 1) * HALF]),
                  reads=["IBO%d" % (it % 2)], writes=["int_%d" % it], tag=px + "st")

        def down_pass(h, call):
            px = "dn%d_h%d" % (call, h)
            for gr in range(NG):
                if GH == 4:
                    pmap4 = [pA, pB, pC, pD]
                    pbase = 0
                else:
                    pmap4 = ([pA, pB] if gr % 2 == 0 else [pC, pD]) * 2
                    pbase = 0 if gr % 2 == 0 else 4
                for itb in range(IT // BIT):
                    wdb = WD[:, (itb % 2) * BIT * GH * 128:
                             (itb % 2 + 1) * BIT * GH * 128]
                    wdt = "WD%d" % (itb % 2)
                    D("sp", wdt, lambda e, itb=itb, gr=gr, wdb=wdb: e.dma_start(
                        out=wdb.rearrange("p (i x) -> p i x", i=BIT),
                        in_=d_full[itb * BIT * 128:(itb + 1) * BIT * 128,
                                   gr * GH * 128:(gr + 1) * GH * 128]
                        .rearrange("(i p) x -> p i x", p=128)),
                      reads=["d_full"], writes=[wdt], tag=px + "ld")
                    ib4 = IBST[:, (itb % 2) * BIT * HALF:(itb % 2 + 1) * BIT * HALF]
                    ibt = "IBST%d" % (itb % 2)
                    D("sp", ibt, lambda e, itb=itb, ib4=ib4: e.dma_start(
                        out=ib4.rearrange("p (i x) -> p i x", i=BIT),
                        in_=inter_s[itb * BIT:(itb + 1) * BIT, :, :]
                        .rearrange("i p x -> p i x")),
                      reads=["int_%d" % (itb * BIT + ii) for ii in range(BIT)],
                      writes=[ibt], tag=px + "ldi")
                    for ii in range(BIT):
                        it = itb * BIT + ii
                        for hh in range(GH):
                            for wd in range(g.NDW):
                                dtile = pmap4[hh % 4 if GH == 4 else hh % 2][
                                    :, (wd % 2) * 512:(wd % 2) * 512 + DW]
                                A("pe", lambda e, ii=ii, hh=hh, it=it, wd=wd,
                                    wdb=wdb, ib4=ib4, dtile=dtile: e.matmul(
                                    dtile,
                                    wdb[:, (ii * GH + hh) * 128:
                                        (ii * GH + hh + 1) * 128],
                                    ib4[:, ii * HALF + wd * DW:
                                        ii * HALF + (wd + 1) * DW],
                                    start=(it == 0), stop=(it == IT - 1)),
                                  reads=[wdt, ibt],
                                  writes=[PB((2 * hh + (wd % 2)) if GH == 4 else
                                             (pbase + 2 * (hh % 2) + (wd % 2)))],
                                  tag=px)
                for hh in range(GH):
                    for wd in range(g.NDW):
                        ht = gr * GH + hh
                        dtile = pmap4[hh % 4 if GH == 4 else hh % 2][
                            :, (wd % 2) * 512:(wd % 2) * 512 + DW]
                        A("dve", lambda e, ht=ht, wd=wd, dtile=dtile:
                          e.scalar_tensor_tensor(
                            xr_sl(ht, wd * DW, DW), dtile, 0.5,
                            xr_sl(ht, wd * DW, DW), ALU.mult, ALU.add),
                          reads=[PB((2 * hh + (wd % 2)) if GH == 4 else
                                    (pbase + 2 * (hh % 2) + (wd % 2))),
                                 xr_tok(ht, wd * DW)],
                          writes=[xr_tok(ht, wd * DW)], tag=px + "ev")

        def rvk_gen(h):
            px = "rvk_h%d" % h
            EB, EBS = g.EB, g.EBS
            NSUB = EB // EBS
            ot_order = ([2 * CT + i for i in range(CT)] +
                        [CT + i for i in range(CT)] + list(range(CT)))
            for oi, ot in enumerate(ot_order):
                yield
                wb = WR[:, (oi % 2) * CT * 128:(oi % 2 + 1) * CT * 128]
                wbt = "WR%d" % (oi % 2)
                D("sp", wbt, lambda e, ot=ot, wb=wb: e.dma_start(
                    out=wb, in_=wrvk_b[:, ot * CT * 128:(ot + 1) * CT * 128]),
                  reads=["wrvkb%d" % ot], writes=[wbt], tag=px + "ld")
                pp = [pA, pB, pC, pD][oi % 4]
                ppt = ptoks([0, 2, 4, 6][oi % 4], EB)
                kind = ot // CT
                ct = ot % CT
                NBE = (EB + 511) // 512
                BWE = min(512, EB)
                for ht in range(CT):
                    xr_ = xn_toks_range(ht, 0, HALF)
                    for nb in range(NBE):
                        A("pe", lambda e, ht=ht, nb=nb, pp=pp, wb=wb: e.matmul(
                            pp[:, nb * BWE:(nb + 1) * BWE],
                            wb[:, ht * 128:(ht + 1) * 128],
                            XN[:, ht * HALF + nb * BWE:ht * HALF + (nb + 1) * BWE],
                            start=(ht == 0), stop=(ht == CT - 1)),
                          reads=[wbt] + xr_, writes=ppt, tag=px)
                for sub in range(NSUB):
                    bsl = sub % 2
                    if kind == 0:
                        stg = RST[:, bsl * EBS:(bsl + 1) * EBS]
                        stok = "RST%d" % bsl
                        A("act", lambda e, stg=stg, pp=pp, sub=sub: e.activation(
                            stg, pp[:, sub * EBS:(sub + 1) * EBS], AF.Tanh,
                            scale=0.5),
                          reads=ppt, writes=[stok], tag=px)
                        dstp = tpk
                    else:
                        stg = KVST[:, bsl * EBS:(bsl + 1) * EBS]
                        stok = "KVST%d" % bsl
                        A("act", lambda e, stg=stg, pp=pp, sub=sub: e.copy(
                            stg, pp[:, sub * EBS:(sub + 1) * EBS]),
                          reads=ppt, writes=[stok], tag=px)
                        dstp = vpk if kind == 1 else kpk
                    wpersb = EBS // SW

                    def stp(e, stg=stg, ot=ot, ct=ct, h=h, sub=sub, dstp=dstp,
                            wpersb=wpersb):
                        dst = dstp[h, :, :].rearrange(
                            "p (wj c l) -> p wj c l", c=CT, l=L)[
                            :, sub * wpersb * NI:(sub + 1) * wpersb * NI, ct, :]
                        return e.dma_start(
                            out=dst,
                            in_=stg.rearrange("p (wj l) -> p wj l", l=L))
                    D("pool", "pk", stp, reads=[stok],
                      writes=["%s_h%d_%d_%d" % (("tpk", "vpk", "kpk")[kind], h, ct, sub)],
                      tag=px + "st")

        def rvk_sweep(h, side=(), start=0):
            gen = rvk_gen(h)
            i = 0
            side = list(side)
            si = 0
            n_after = OT - start
            every = max(1, n_after // max(1, len(side))) if side else 1
            for _ in gen:
                i += 1
                if side and i > start and (i - start) % every == 0 and si < len(side):
                    side[si]()
                    si += 1
            while si < len(side):
                side[si]()
                si += 1

        def pack_toks(pfx, h):
            return ["%s_h%d_%d_%d" % (pfx, h, ct, sub)
                    for ct in range(CT) for sub in range(g.EB // g.EBS)]

        sidx = [0]

        def kv_load(h, w, full):
            NICL = NI * CL
            D("pool", "KP", lambda e, h=h, w=w: e.dma_start(
                out=KP[:, :], in_=kpk[h, :, w * NICL:(w + 1) * NICL]),
              reads=pack_toks("kpk", h), writes=["KP"], tag="kvld")
            D("pool", "VP", lambda e, h=h, w=w: e.dma_start(
                out=VP[:, :], in_=vpk[h, :, w * NICL:(w + 1) * NICL]),
              reads=pack_toks("vpk", h), writes=["VP"], tag="kvld")
            if full:
                D("pool", "TP", lambda e, h=h, w=w: e.dma_start(
                    out=TP[:, :], in_=tpk[h, :, w * NICL:(w + 1) * NICL]),
                  reads=pack_toks("tpk", h), writes=["TP"], tag="kvld")

        def scan_window(h, w, full, to_scratch=False):
            kv_load(h, w, full)
            px = "sc%s_h%d_w%d" % ("F" if full else "S", h, w)
            pars = [(sidx[0] + j) % 2 for j in range(NI)]
            ezv_ = lambda j: EZK[:, j * 2 * CL:j * 2 * CL + CL]
            ekk_ = lambda j: EZK[:, j * 2 * CL + CL:(j + 1) * 2 * CL]
            ph_ = lambda j: PQH[:, pars[j] * 2 * CL:pars[j] * 2 * CL + CL]
            qh_ = lambda j: PQH[:, pars[j] * 2 * CL + CL:(pars[j] + 1) * 2 * CL]
            e3_ = lambda j: E3W[:, j * CL:(j + 1) * CL]
            lam_sl = [13, 18]
            r3 = lambda ap: ap.rearrange("p (c l) -> p c l", l=L)
            bco = lambda ap: ap.rearrange("p (c o) -> p c o", o=1)
            for j in range(NI):
                kj = KP[:, j * CL:(j + 1) * CL]
                A("dve", lambda e, j=j, kj=kj: e.reduce_max(
                    sl(10 + j), r3(kj), axis=AX.X),
                  reads=["KP"], writes=["cmk%d" % j], tag=px)
            for j in range(NI):
                sbp, nbp = pars[j], 1 - pars[j]
                Mp, Mn = sl(8 + sbp), sl(8 + nbp)
                A("pool", lambda e, Mp=Mp: e.tensor_add(sl(12), Mp, sl(1)),
                  reads=["st%d" % sbp, "ssm1"], writes=["ssm12"], tag=px)
                A("dve", lambda e, Mn=Mn, j=j: e.tensor_max(Mn, sl(12), sl(10 + j)),
                  reads=["ssm12", "cmk%d" % j], writes=["st%d" % nbp], tag=px)
                A("dve", lambda e, Mp=Mp, Mn=Mn, j=j: e.tensor_sub(
                    sl(lam_sl[j]), Mp, Mn),
                  reads=["st%d" % sbp, "st%d" % nbp], writes=["lam%d" % j], tag=px)
                A("act", lambda e, j=j: e.activation(
                    sl(lam_sl[j]), sl(lam_sl[j]), AF.Exp),
                  reads=["lam%d" % j], writes=["lam%d" % j], tag=px)
            for j in range(NI):
                kj = KP[:, j * CL:(j + 1) * CL]
                Mn = sl(8 + (1 - pars[j]))
                A("dve", lambda e, j=j, kj=kj, Mn=Mn: e.tensor_tensor(
                    r3(ekk_(j)), r3(kj),
                    bco(Mn).broadcast_to([128, CT, L]), ALU.subtract),
                  reads=["KP", "st%d" % (1 - pars[j])], writes=["ekk%d" % j],
                  tag=px)
            eks = EZK[:, :].rearrange("p (j x) -> p j x", j=NI)[:, :, CL:2 * CL]
            A("act", lambda e: e.activation(eks, eks, AF.Exp),
              reads=["ekk%d" % j2 for j2 in range(NI)],
              writes=["ekk%d" % j2 for j2 in range(NI)], tag=px)
            ezs = EZK[:, :].rearrange("p (j x) -> p j x", j=NI)[:, :, 0:CL]
            vps = VP[:, :].rearrange("p (j x) -> p j x", j=NI)
            A("dve", lambda e: e.tensor_tensor(ezs, eks, vps, ALU.mult),
              reads=["ekk%d" % j2 for j2 in range(NI)] + ["VP"],
              writes=["ezv%d" % j2 for j2 in range(NI)], tag=px)
            if full:
                e3s = E3W[:, :].rearrange("p (j x) -> p j x", j=NI)
                ceub = CEU[:, :].rearrange("p (o x) -> p o x", o=1)                     .broadcast_to([128, NI, CL])
                A("pool", lambda e: e.tensor_tensor(e3s, eks, ceub, ALU.mult),
                  reads=["ekk%d" % j2 for j2 in range(NI)] + ["CEU"],
                  writes=["e3w"], tag=px)
            for j in range(NI):
                sbp, nbp = pars[j], 1 - pars[j]
                Pp, Qp = sl(4 + sbp * 2), sl(5 + sbp * 2)
                Pn, Qn = sl(4 + nbp * 2), sl(5 + nbp * 2)
                stok, ntok = "st%d" % sbp, "st%d" % nbp
                A("dve", lambda e, j=j, Pp=Pp: e.tensor_mul(sl(14), Pp, sl(lam_sl[j])),
                  reads=[stok, "lam%d" % j], writes=["ssm14"], tag=px)
                A("dve", lambda e, j=j, Qp=Qp: e.tensor_mul(sl(15), Qp, sl(lam_sl[j])),
                  reads=[stok, "lam%d" % j], writes=["ssm15"], tag=px)
                A("dve", lambda e: e.tensor_mul(sl(16), sl(14), sl(2)),
                  reads=["ssm14", "ssm2"], writes=["ssm16"], tag=px)
                A("dve", lambda e: e.tensor_mul(sl(17), sl(15), sl(2)),
                  reads=["ssm15", "ssm2"], writes=["ssm17"], tag=px)
                ez3, ek3 = r3(ezv_(j)), r3(ekk_(j))
                A("dve", lambda e, ez3=ez3: e.tensor_add(
                    ez3[:, :, 0:1], ez3[:, :, 0:1], bco(sl(16))),
                  reads=["ezv%d" % j, "ssm16"], writes=["ezv%d" % j], tag=px)
                A("dve", lambda e, ek3=ek3: e.tensor_add(
                    ek3[:, :, 0:1], ek3[:, :, 0:1], bco(sl(17))),
                  reads=["ekk%d" % j, "ssm17"], writes=["ekk%d" % j], tag=px)
                A("dve", lambda e, j=j: e.tensor_tensor_scan(
                    PQH[:, pars[j] * 2 * CL:(pars[j] + 1) * 2 * CL],
                    CEWZ[:, :], EZK[:, j * 2 * CL:(j + 1) * 2 * CL], 0.0,
                    ALU.mult, ALU.add),
                  reads=["CEWZ", "ezv%d" % j, "ekk%d" % j],
                  writes=["pqh%d" % pars[j]], tag=px)
                A("dve", lambda e, j=j, Pn=Pn: e.tensor_copy(
                    bco(Pn), r3(ph_(j))[:, :, L - 1:L]),
                  reads=["pqh%d" % pars[j]], writes=[ntok], tag=px)
                A("pool", lambda e, j=j, Qn=Qn: e.tensor_copy(
                    bco(Qn), r3(qh_(j))[:, :, L - 1:L]),
                  reads=["pqh%d" % pars[j]], writes=[ntok], tag=px)
                if full:
                    A("pool", lambda e, j=j: e.tensor_copy(sl(22 + j), sl(14)),
                      reads=["ssm14"], writes=["ip%d" % j], tag=px)
                    A("pool", lambda e, j=j: e.tensor_copy(sl(24 + j), sl(15)),
                      reads=["ssm15"], writes=["iq%d" % j], tag=px)
                sidx[0] += 1
            if not full:
                return
            b = (w // 2) % 2
            for j in range(NI):
                PSH, QSH = ezv_(j), ekk_(j)
                ps3, qs3 = r3(PSH), r3(QSH)
                ph3, qh3 = r3(ph_(j)), r3(qh_(j))
                A("dve", lambda e, ps3=ps3, ph3=ph3: e.tensor_copy(
                    ps3[:, :, 1:L], ph3[:, :, 0:L - 1]),
                  reads=["pqh%d" % pars[j], "ezv%d" % j], writes=["ezv%d" % j],
                  tag=px)
                A("dve", lambda e, ps3=ps3, j=j: e.tensor_copy(
                    ps3[:, :, 0:1], bco(sl(22 + j))),
                  reads=["ip%d" % j], writes=["ezv%d" % j], tag=px)
                A("pool", lambda e, qs3=qs3, qh3=qh3: e.tensor_copy(
                    qs3[:, :, 1:L], qh3[:, :, 0:L - 1]),
                  reads=["pqh%d" % pars[j], "ekk%d" % j], writes=["ekk%d" % j],
                  tag=px)
                A("pool", lambda e, qs3=qs3, j=j: e.tensor_copy(
                    qs3[:, :, 0:1], bco(sl(24 + j))),
                  reads=["iq%d" % j], writes=["ekk%d" % j], tag=px)
                vj = VP[:, j * CL:(j + 1) * CL]
                A("dve", lambda e, j=j, vj=vj: e.tensor_mul(TMPS[:, :], e3_(j), vj),
                  reads=["e3w", "VP"], writes=["tms"], tag=px)
                A("dve", lambda e, PSH=PSH: e.tensor_add(PSH, PSH, TMPS[:, :]),
                  reads=["ezv%d" % j, "tms"], writes=["ezv%d" % j], tag=px)
                A("pool", lambda e, QSH=QSH, j=j: e.tensor_add(QSH, QSH, e3_(j)),
                  reads=["ekk%d" % j, "e3w"], writes=["ekk%d" % j], tag=px)
                A("act", lambda e, QSH=QSH: e.activation(QSH, QSH, AF.Ln),
                  reads=["ekk%d" % j], writes=["ekk%d" % j], tag=px)
                A("act", lambda e, QSH=QSH: e.activation(
                    QSH, QSH, AF.Exp, scale=-1.0, bias=LHC[:, 0:1]),
                  reads=["ekk%d" % j, "LHC"], writes=["ekk%d" % j], tag=px)
                A("dve", lambda e, PSH=PSH, QSH=QSH: e.tensor_mul(PSH, PSH, QSH),
                  reads=["ezv%d" % j, "ekk%d" % j], writes=["ezv%d" % j], tag=px)
                tj = TP[:, j * CL:(j + 1) * CL]
                A("pool", lambda e, PSH=PSH, tj=tj: e.tensor_mul(TMPS[:, :], PSH, tj),
                  reads=["ezv%d" % j, "TP"], writes=["tms"], tag=px)
                ysb = YS[:, b * CT * g.OPW:(b + 1) * CT * g.OPW]
                ys3 = ysb.rearrange("p (c s) -> p c s", s=g.OPW)[
                    :, :, (w % 2) * SW + j * L:(w % 2) * SW + (j + 1) * L]
                A("dve", lambda e, ys3=ys3, PSH=PSH: e.tensor_add(
                    ys3, r3(PSH), r3(TMPS[:, :])),
                  reads=["ezv%d" % j, "tms"],
                  writes=["YS%d_%d_%d" % (b, w % 2, j)], tag=px)
            if to_scratch:
                D("pool", "ysst", lambda e, w=w, b=b: e.dma_start(
                    out=ysk[:, :].rearrange(
                        "p (wp c s) -> p wp c s", c=CT, s=g.OPW)[
                        :, w // 2, :, (w % 2) * SW:(w % 2 + 1) * SW],
                    in_=YS[:, b * CT * g.OPW:(b + 1) * CT * g.OPW].rearrange(
                        "p (c o) -> p c o", o=g.OPW)[:, :, (w % 2) * SW:
                                                     (w % 2 + 1) * SW]),
                  reads=["YS%d_%d_%d" % (b, w % 2, j) for j in range(NI)],
                  writes=["ysk_%d" % w], tag="ysst")

        def weave(main_gen, side_steps, every):
            i = 0
            side = list(side_steps)
            si = 0
            for _ in main_gen:
                i += 1
                if si < len(side) and i % every == 0:
                    side[si]()
                    si += 1
            while si < len(side):
                side[si]()
                si += 1

        def gu_sweep(h, call, side=()):
            weave(gu_sweep_gen(h, call), side, max(1, IT // max(1, len(side) + 1)))

        def outproj(h, wp, from_scratch=False):
            px = "op_h%d_p%d" % (h, wp)
            t0 = wp * g.OPW
            b = wp % 2
            ysb = YS[:, b * CT * g.OPW:(b + 1) * CT * g.OPW]
            ystoks = ["YS%d_%d_%d" % (b, k, jj)
                      for k in range(2) for jj in range(NI)]
            if from_scratch:
                D("pool", "ysld", lambda e, wp=wp, ysb=ysb: e.dma_start(
                    out=ysb, in_=ysk[:, wp * CT * g.OPW:(wp + 1) * CT * g.OPW]),
                  reads=["ysk_%d" % (2 * wp), "ysk_%d" % (2 * wp + 1)],
                  writes=ystoks, tag=px + "ysld")
            for ot in range(CT):
                wb = WR[:, (ot % 2) * CT * 128:(ot % 2 + 1) * CT * 128]
                wbt = "WR%d" % (ot % 2)
                D("sp", wbt, lambda e, ot=ot, wb=wb: e.dma_start(
                    out=wb, in_=wout_b[:, ot * CT * 128:(ot + 1) * CT * 128]),
                  reads=["woutb%d" % ot], writes=[wbt], tag=px + "ld")
                pp = pC if ot % 2 == 0 else pD
                ppt = PB(4 if ot % 2 == 0 else 6)
                for ct in range(CT):
                    A("pe", lambda e, ct=ct, pp=pp, wb=wb, ysb=ysb: e.matmul(
                        pp[:, 0:g.OPW], wb[:, ct * 128:(ct + 1) * 128],
                        ysb[:, ct * g.OPW:(ct + 1) * g.OPW],
                        start=(ct == 0), stop=(ct == CT - 1)),
                      reads=[wbt] + ystoks, writes=[ppt], tag=px)
                A("dve", lambda e, ot=ot, pp=pp, t0=t0: e.tensor_add(
                    xr_sl(ot, t0, g.OPW), xr_sl(ot, t0, g.OPW), pp[:, 0:g.OPW]),
                  reads=[ppt, xr_tok(ot, t0)], writes=[xr_tok(ot, t0)],
                  tag=px + "ev")

        def x_load(h):
            px = "xin_h%d" % h
            for tt in range(HALF // 128):
                D("sp", "XST", lambda e, tt=tt, h=h: e.dma_start(
                    out=XST[:, :],
                    in_=x_in[h * HALF + tt * 128:h * HALF + (tt + 1) * 128, :]),
                  writes=["XST"], tag=px)
                for ct in range(CT):
                    A("pe", lambda e, ct=ct: e.transpose(
                        pA[:, 0:128], XST[:, ct * 128:(ct + 1) * 128], IDENT[:, :]),
                      reads=["XST", "IDENT"], writes=[PB(0)], tag=px)
                    A("act", lambda e, ct=ct, tt=tt: e.copy(
                        xr_sl(ct, tt * 128, 128), pA[:, 0:128]),
                      reads=[PB(0)], writes=xr_toks_range(ct, tt * 128, 128),
                      tag=px)

        def x_store(h):
            px = "xout_h%d" % h
            outs = []
            for tt in range(HALF // 128):
                for ct in range(CT):
                    A("pe", lambda e, ct=ct, tt=tt: e.transpose(
                        pA[:, 128:256], xr_sl(ct, tt * 128, 128), IDENT[:, :]),
                      reads=xr_toks_range(ct, tt * 128, 128) + ["IDENT"],
                      writes=[PB(0)], tag=px)
                    A("act", lambda e, ct=ct: e.copy(
                        XST[:, ct * 128:(ct + 1) * 128], pA[:, 128:256]),
                      reads=[PB(0)], writes=["XST"], tag=px)
                tok = "xout_%d_%d" % (h, tt)
                D("pool", "xo", lambda e, tt=tt, h=h: e.dma_start(
                    out=x_out[h * HALF + tt * 128:h * HALF + (tt + 1) * 128, :],
                    in_=XST[:, :]),
                  reads=["XST"], writes=[tok], tag=px)
                outs.append(tok)
            return outs

        x_load(0)
        for w in range(g.NLW):
            ln_pass(0, w, "1")
        HB = HITC * 128
        gsh_toks, ush_toks = [], []
        for l in range(HITC):
            gsh_toks += recon(grec, 8, None, g_sh, "g", l, l + 1)
            ush_toks += recon(urec, 4, grec, u_sh, "u", l, l + 1)
        CC(lambda e: e.collective_compute(
            "AllGather", ALU.bypass, replica_groups=[list(range(NCORES))],
            ins=[g_sh[0:HB, :]], outs=[g_full_a.ap().opt()]),
           reads=gsh_toks[:HITC], writes=["g_fullA"], tag="AGg1")
        CC(lambda e: e.collective_compute(
            "AllGather", ALU.bypass, replica_groups=[list(range(NCORES))],
            ins=[u_sh[0:HB, :]], outs=[u_full_a.ap().opt()]),
           reads=ush_toks[:HITC], writes=["u_fullA"], tag="AGu1")
        for l in range(HITC, g.ITC):
            gsh_toks += recon(grec, 8, None, g_sh, "g", l, l + 1)
            ush_toks += recon(urec, 4, grec, u_sh, "u", l, l + 1)
        if NSPLIT == 2:
            CC(lambda e: e.collective_compute(
                "AllGather", ALU.bypass, replica_groups=[list(range(NCORES))],
                ins=[g_sh[HB:2 * HB, :]], outs=[g_full_b.ap().opt()]),
               reads=gsh_toks[HITC:], writes=["g_fullB"], tag="AGg2")
            CC(lambda e: e.collective_compute(
                "AllGather", ALU.bypass, replica_groups=[list(range(NCORES))],
                ins=[u_sh[HB:2 * HB, :]], outs=[u_full_b.ap().opt()]),
               reads=ush_toks[HITC:], writes=["u_fullB"], tag="AGu2")
        dsh_toks = recon(drec, 8, None, d_sh, "d", 0, g.ITC)
        CC(lambda e: e.collective_compute(
            "AllGather", ALU.bypass, replica_groups=[list(range(NCORES))],
            ins=[d_sh.ap().opt()], outs=[d_full.ap().opt()]),
           reads=dsh_toks, writes=["d_full"], tag="AGd")
        def cast_w(src_, dst, nblk, tag):
            for b_ in range(nblk):
                buf = WR[:, (b_ % 2) * CT * 128:(b_ % 2 + 1) * CT * 128]
                btok = "WR%d" % (b_ % 2)
                D("pool", btok + "c", lambda e, b_=b_, buf=buf, src_=src_:
                  e.dma_start(
                    out=buf, in_=src_[:, b_ * CT * 128:(b_ + 1) * CT * 128]),
                  writes=[btok], tag=tag)
                D("act", "cst", lambda e, b_=b_, buf=buf, dst=dst: e.dma_start(
                    out=dst[:, b_ * CT * 128:(b_ + 1) * CT * 128], in_=buf),
                  reads=[btok], writes=["%s%d" % (tag, b_)], tag=tag)

        cast_w(wrvk_f, wrvk_b, OT, "wrvkb")
        cast_w(wout_f, wout_b, CT, "woutb")
        gu_sweep(0, 1)
        down_pass(0, 1)
        for w in range(g.NLW):
            ln_pass(0, w, "2")
        rvk_sweep(0)
        D("pool", "spl", lambda e: e.dma_start(out=xr_sp[0, :, :], in_=XR[:, :]),
          reads=["XR_%d_%d" % (ct, d) for ct in range(CT) for d in range(g.NDW)],
          writes=["xr_sp0"], tag="spill0")
        x_load(1)
        for w in range(g.NLW):
            ln_pass(1, w, "1")
        gu_sweep(1, 1, side=[
            (lambda w=w: scan_window(0, w, False)) for w in range(g.NSW)])
        down_pass(1, 1)
        for w in range(g.NLW):
            ln_pass(1, w, "2")
        rvk_sweep(1, side=[
            (lambda w=w: scan_window(1, w, False)) for w in range(g.NSW)],
            start=2 * CT + 2)
        D("pool", "spl", lambda e: e.dma_start(out=xr_sp[1, :, :], in_=XR[:, :]),
          reads=["XR_%d_%d" % (ct, d) for ct in range(CT) for d in range(g.NDW)],
          writes=["xr_sp1"], tag="spill1")

        sb = sidx[0] % 2
        A("dve", lambda e, sb=sb: e.tensor_copy(GTH[:, 0:CT], sl(4 + sb * 2)),
          reads=["st%d" % sb], writes=["stpk"], tag="B")
        A("dve", lambda e, sb=sb: e.tensor_copy(GTH[:, CT:2 * CT], sl(5 + sb * 2)),
          reads=["st%d" % sb], writes=["stpk"], tag="B")
        A("dve", lambda e, sb=sb: e.tensor_copy(GTH[:, 2 * CT:3 * CT], sl(8 + sb)),
          reads=["st%d" % sb], writes=["stpk"], tag="B")
        D("pool", "stio", lambda e: e.dma_start(out=st_in[:, :], in_=GTH[:, 0:3 * CT]),
          reads=["stpk"], writes=["st_in"], tag="B")
        CC(lambda e: e.collective_compute(
            "AllGather", ALU.bypass, replica_groups=[list(range(NCORES))],
            ins=[st_in.ap().opt()], outs=[st_out.ap().opt()]),
           reads=["st_in"], writes=["st_out"], tag="AGst")
        D("sp", "gth", lambda e: e.dma_start(
            out=GTH[:, :].rearrange("p (r f) -> p r f", r=NCORES),
            in_=st_out[:, :].rearrange("(r p) f -> p r f", p=128)),
          reads=["st_out"], writes=["stpk"], tag="B")
        A("dve", lambda e: e.tensor_mul(GTH[:, :], GTH[:, :], MSKS[:, :]),
          reads=["stpk", "MSKS"], writes=["stpk"], tag="B")
        A("dve", lambda e: e.tensor_reduce(
            LNT[:, 0:3 * CT],
            GTH[:, :].rearrange("p (r f) -> p f r", f=3 * CT), AX.X, ALU.add),
          reads=["stpk"], writes=["LNT"], tag="B")
        A("dve", lambda e: e.tensor_add(LNT[:, 0:3 * CT], LNT[:, 0:3 * CT],
                                        INITA[:, :]),
          reads=["LNT", "INITA"], writes=["LNT"], tag="B")
        sidx[0] = 0
        A("dve", lambda e: e.tensor_copy(sl(4), LNT[:, 0:CT]),
          reads=["LNT"], writes=["st0"], tag="B")
        A("dve", lambda e: e.tensor_copy(sl(5), LNT[:, CT:2 * CT]),
          reads=["LNT"], writes=["st0"], tag="B")
        A("dve", lambda e: e.tensor_copy(sl(8), LNT[:, 2 * CT:3 * CT]),
          reads=["LNT"], writes=["st0"], tag="B")

        out_toks = []
        D("sp", "rld", lambda e: e.dma_start(out=XR[:, :], in_=xr_sp[0, :, :]),
          reads=["xr_sp0"],
          writes=["XR_%d_%d" % (ct, d) for ct in range(CT) for d in range(g.NDW)],
          tag="rld0")
        for w in range(g.NSW):
            scan_window(0, w, True)
            if w % 2 == 1:
                outproj(0, w // 2)
        for w in range(g.NLW):
            ln_pass(0, w, "3")
        gu_sweep(0, 2, side=[
            (lambda w=w: scan_window(1, w, True, to_scratch=True))
            for w in range(g.NSW)])
        down_pass(0, 2)
        out_toks += x_store(0)
        D("sp", "rld", lambda e: e.dma_start(out=XR[:, :], in_=xr_sp[1, :, :]),
          reads=["xr_sp1"],
          writes=["XR_%d_%d" % (ct, d) for ct in range(CT) for d in range(g.NDW)],
          tag="rld1")
        for wp in range(g.NSW // 2):
            outproj(1, wp, from_scratch=True)
        for w in range(g.NLW):
            ln_pass(1, w, "3")
        gu_sweep(1, 2)
        down_pass(1, 2)
        out_toks += x_store(1)

        A("pool", lambda e: e.engine_nop(), reads=out_toks, tag="fence")

        for ch in sorted(sch.chan_count):
            sems[ch] = EC(nc.semaphore("sem_" + ch.replace(":", "_")))
        sch.plan()

        @block.tensor
        def _(eng):
            sch.emit_engine("pe", eng, sems)

        @block.vector
        def _(eng):
            sch.emit_engine("dve", eng, sems)

        @block.scalar
        def _(eng):
            sch.emit_engine("act", eng, sems)

        @block.gpsimd
        def _(eng):
            sch.emit_engine("pool", eng, sems)

        @block.sync
        def _(eng):
            sch.emit_engine("sp", eng, sems)

    return nc, sch


def kernel(**inputs):
    g = CFG_REAL
    in_maps = host_prep(g, inputs)
    nc, _ = build_nc(g)
    res = run_bass_kernel_spmd(nc, in_maps, core_ids=list(range(NCORES)))
    out = np.empty((4, 2 * g.TOKS, g.C), np.float32)
    for core in range(NCORES):
        b, s = core // 2, core % 2
        out[b, s * g.TOKS:(s + 1) * g.TOKS] = res.results[core]["x_out"]
    return out


